# version 04
# baseline (speedup 1.0000x reference)
import numpy as np
import ml_dtypes
import concourse.bass as bass
import concourse.mybir as mybir
from concourse.bass_utils import run_bass_kernel_spmd

F32 = mybir.dt.float32
BF16 = mybir.dt.bfloat16
AF = mybir.ActivationFunctionType
ALU = mybir.AluOpType
AX = mybir.AxisListType

D = 1024
T = 2048
NKT = 8
NH = 16
NJ = 16
DIP = 2816
NFT = 22
NEG = -3.0e38
EPS = 1e-5

ENGINES = ("PE", "ACT", "DVE", "POOL", "SP")


class Op:
    __slots__ = ("eng", "fn", "reads", "writes", "dma_key", "deps", "idx",
                 "sem_key", "inc_val", "clock", "needed", "waits")

    def __init__(self, eng, fn, reads, writes, dma_key):
        self.eng = eng
        self.fn = fn
        self.reads = reads
        self.writes = writes
        self.dma_key = dma_key
        self.deps = []
        self.sem_key = None
        self.inc_val = None
        self.clock = None
        self.needed = False
        self.waits = ()


class Prog:
    def __init__(self):
        self.ops = []
        self.last_w = {}
        self.readers = {}
        self.pending_fence = None

    def fence_pool(self):
        per_eng = {}
        for key in [k for k in self.last_w if k.startswith("pool.")]:
            cands = [self.last_w[key]] + self.readers.get(key, [])
            for o in cands:
                cur = per_eng.get(o.eng)
                if cur is None or o.idx > cur.idx:
                    per_eng[o.eng] = o
            del self.last_w[key]
            self.readers.pop(key, None)
        self.pending_fence = list(per_eng.values())

    def op(self, eng, fn, r=(), w=(), dma_key=None):
        o = Op(eng, fn, tuple(r), tuple(w), dma_key)
        o.idx = len(self.ops)
        deps = {}
        for b in o.reads:
            if b in self.last_w:
                d = self.last_w[b]
                deps[d.idx] = d
        for b in o.writes:
            if b in self.last_w:
                d = self.last_w[b]
                deps[d.idx] = d
            for rd in self.readers.get(b, ()):
                deps[rd.idx] = rd
        if self.pending_fence and any(b.startswith("pool.") for b in o.writes):
            for d in self.pending_fence:
                deps[d.idx] = d
        o.deps = list(deps.values())
        for b in o.reads:
            self.readers.setdefault(b, []).append(o)
        for b in o.writes:
            self.last_w[b] = o
            self.readers[b] = []
        self.ops.append(o)
        return o

    def resolve(self):
        for o in self.ops:
            for d in o.deps:
                if d.dma_key is not None or d.eng != o.eng:
                    d.needed = True
            if o.dma_key is not None:
                o.needed = True
        counters = {}
        for o in self.ops:
            if not o.needed:
                continue
            key = o.dma_key if o.dma_key is not None else o.eng
            step = 16 if o.dma_key is not None else 1
            counters[key] = counters.get(key, 0) + step
            o.sem_key = key
            o.inc_val = counters[key]
        clocks = {e: {} for e in ENGINES}

        def merge(dst, src):
            for k, v in src.items():
                if dst.get(k, 0) < v:
                    dst[k] = v

        for o in self.ops:
            ck = clocks[o.eng]
            waits = []
            for d in o.deps:
                if d.sem_key is None:
                    continue
                if d.dma_key is None and d.eng == o.eng:
                    continue
                if ck.get(d.sem_key, 0) < d.inc_val:
                    waits.append((d.sem_key, d.inc_val))
                    if d.clock:
                        merge(ck, d.clock)
                    if ck.get(d.sem_key, 0) < d.inc_val:
                        ck[d.sem_key] = d.inc_val
            o.waits = tuple(waits)
            if o.sem_key is not None:
                o.clock = dict(ck)
                o.clock[o.sem_key] = o.inc_val
        self.counters = counters
        return counters

    def emit(self, eng_name, eng_obj, sems):
        for o in self.ops:
            if o.eng != eng_name:
                continue
            for key, val in o.waits:
                eng_obj.wait_ge(sems[key], val)
            inst = o.fn(eng_obj)
            if o.sem_key is not None:
                inst.then_inc(sems[o.sem_key], 16 if o.dma_key else 1)


def _bf(a):
    return np.ascontiguousarray(np.asarray(a, dtype=np.float32)).astype(ml_dtypes.bfloat16)


def _f32(a):
    return np.ascontiguousarray(np.asarray(a), dtype=np.float32)


def host_tables():
    HD = 64
    half = np.arange(0, HD, 2, dtype=np.float64)
    inv_freq = 1.0 / 10000.0 ** (half / HD)
    t = np.arange(T, dtype=np.float64)
    freqs = np.concatenate([t[:, None] * inv_freq[None, :]] * 2, -1)
    base = (half + 0.4 * HD) / (1.4 * HD)
    power = (t - T // 2) / 128.0
    scale = np.concatenate([base[None, :] ** power[:, None]] * 2, -1)
    out = {}
    for nm, tab in [("ct", np.cos(freqs) * scale), ("st", np.sin(freqs) * scale),
                    ("cti", np.cos(freqs) / scale), ("sti", np.sin(freqs) / scale)]:
        tt = tab.T
        out[nm] = _f32(np.concatenate([tt, tt], axis=0))
    return out


def host_consts():
    c = {}
    r = np.arange(128)
    k = r[:, None]
    c384 = np.arange(384)[None, :]
    c["B384"] = _bf(NEG * ((c384 < k) | (c384 > k + 256)))
    c256 = np.arange(256)[None, :]
    c["B256"] = _bf(NEG * (c256 > k + 128))
    c128 = np.arange(128)[None, :]
    c["B128"] = _bf(NEG * (c128 > k))
    c["IDENT"] = _bf(np.eye(128))
    c["ONESF"] = _f32(np.ones((128, 128)))
    c["ONESB"] = _bf(np.ones((128, 128)))
    c["NEGROW"] = _bf(np.full((1, 384), NEG))
    c["ONE1"] = _bf(np.ones((1, 128)))
    bw = np.zeros((128, 9 * 128), np.float32)
    for u in range(9):
        rel = 5 - u
        blk = bw[:, 128 * u:128 * (u + 1)]
        if rel == 0:
            blk[:] = NEG * (k < r[None, :])
        elif rel == 1:
            pass
        elif rel == 2:
            blk[:] = NEG * (k > r[None, :])
        else:
            blk[:] = NEG
    c["BAPPW"] = _bf(bw)
    sp = np.zeros((2, 128), np.float32)
    sp[0, :64] = 1.0
    sp[1, 64:] = 1.0
    c["SELPAIR"] = _bf(sp)
    bd = np.zeros((128, 2), np.float32)
    bd[:64, 0] = 1.0
    bd[64:, 1] = 1.0
    c["BD2"] = _bf(bd)
    c["ONESTK"] = _bf(np.ones((128, 1)))
    c["ONES64"] = _f32(np.ones((1, 64)))
    p = np.zeros((64, 64), np.float32)
    for i in range(32):
        p[i + 32, i] = -1.0
        p[i, i + 32] = 1.0
    pp = np.zeros((128, 128), np.float32)
    pp[:64, :64] = p
    pp[64:, 64:] = p
    c["PERM"] = _bf(pp)
    return c


def stage_weights(inp):
    st = {}
    L = 6
    wqkv = _f32(inp["L0_ffn_Wqkv"])
    a = wqkv.reshape(L, NKT, 128, 24, 128)
    slab = np.empty((L, 4, NKT, 128, 768), np.float32)
    for g in range(4):
        for part in range(3):
            for pr in range(2):
                nt = part * 8 + 2 * g + pr
                slab[:, g, :, :, 256 * part + 128 * pr:256 * part + 128 * (pr + 1)] = \
                    a[:, :, :, nt, :]
    st["wqkv_l"] = np.ascontiguousarray(slab)
    wo = _f32(inp["L0_ffn_Wo"]).reshape(L, NKT, 128, NKT, 128)
    wos = np.empty((L, 4, NKT, 2, 128, 128), np.float32)
    for g in range(4):
        for kp in range(2):
            wos[:, g, :, kp] = wo[:, 2 * g + kp].transpose(0, 2, 1, 3)
    st["wo_l"] = np.ascontiguousarray(wos)
    W1 = _f32(inp["L0_ffn_W1"])
    z = np.zeros((L, D, DIP - 2730), np.float32)
    W1a = np.concatenate([W1[:, :, :2730], z], axis=2).reshape(L, D, NFT, 128)
    W1g = np.concatenate([W1[:, :, 2730:], z], axis=2).reshape(L, D, NFT, 128)
    w1s = np.empty((L, NFT, NKT, 128, 256), np.float32)
    for j in range(NFT):
        w1s[:, j, :, :, 0:128] = W1a[:, :, j].reshape(L, NKT, 128, 128)
        w1s[:, j, :, :, 128:256] = W1g[:, :, j].reshape(L, NKT, 128, 128)
    st["w1_l"] = np.ascontiguousarray(w1s)
    b1 = _f32(inp["L0_ffn_b1"])
    zb = np.zeros((L, DIP - 2730), np.float32)
    b1a = np.concatenate([b1[:, :2730], zb], 1).reshape(L, NFT, 128)
    b1g = np.concatenate([b1[:, 2730:], zb], 1).reshape(L, NFT, 128)
    st["b1a_l"] = np.ascontiguousarray(b1a.transpose(0, 2, 1))
    st["b1g_l"] = np.ascontiguousarray(b1g.transpose(0, 2, 1))
    W2 = _f32(inp["L0_ffn_W2"])
    W2p = np.concatenate([W2, np.zeros((L, DIP - 2730, D), np.float32)], 1)
    st["w2_l"] = np.ascontiguousarray(
        W2p.reshape(L, NFT, 128, NKT, 128).transpose(0, 3, 1, 2, 4))
    st["b2_l"] = np.ascontiguousarray(
        _f32(inp["L0_ffn_b2"]).reshape(L, NKT, 128).transpose(0, 2, 1))
    st["g_l"] = np.ascontiguousarray(
        _f32(inp["L0_ffn_ln_g"]).reshape(L, NKT, 128).transpose(0, 2, 1))
    st["bl_l"] = np.ascontiguousarray(
        _f32(inp["L0_ffn_ln_b"]).reshape(L, NKT, 128).transpose(0, 2, 1))
    qs = _f32(inp["L0_ffn_q_scale"])
    ks = _f32(inp["L0_ffn_k_scale"])
    st["qs_l"] = np.ascontiguousarray(
        np.tile(qs, (1, 2))[:, None, :].repeat(2, 1))
    st["ks_l"] = np.ascontiguousarray(
        np.tile(ks, (1, 2))[:, None, :].repeat(2, 1))
    wq = _f32(inp["L0_attn_w_qkv"]).reshape(NKT, 128, 24, 128)
    slabt = np.empty((4, NKT, 128, 768), np.float32)
    for g in range(4):
        for part in range(3):
            for pr in range(2):
                nt = part * 8 + 2 * g + pr
                slabt[g, :, :, 256 * part + 128 * pr:256 * part + 128 * (pr + 1)] = \
                    wq[:, :, nt, :]
    st["wqkv_t"] = np.ascontiguousarray(slabt)
    wp = _f32(inp["L0_attn_w_proj"]).reshape(NKT, 128, NKT, 128)
    wps = np.empty((4, NKT, 2, 128, 128), np.float32)
    for g in range(4):
        for kp in range(2):
            wps[g, :, kp] = wp[2 * g + kp].transpose(1, 0, 2)
    st["wproj_t"] = np.ascontiguousarray(wps)
    return st


class MK:
    def __init__(self, nc, cfg):
        self.nc = nc
        self.cfg = cfg
        self.P = Prog()
        self.declare_dram()
        self.alloc_sbuf()
        self.alloc_psum()

    def declare_dram(self):
        nc = self.nc
        dt = {}
        dt["xT"] = nc.dram_tensor("xT", [D, T], F32, kind="ExternalInput")
        dt["out"] = nc.dram_tensor("out", [D, T], F32, kind="ExternalOutput")
        dt["x2scr"] = nc.dram_tensor("x2scr", [D, T], F32, kind="Internal")
        for nm in ("ct", "st", "cti", "sti"):
            dt[nm] = nc.dram_tensor(nm, [128, T], F32, kind="ExternalInput")
        cshapes = dict(B384=[128, 384], B256=[128, 256], B128=[128, 128],
                       IDENT=[128, 128], ONESB=[128, 128], NEGROW=[1, 384],
                       ONE1=[1, 128], BAPPW=[128, 9 * 128],
                       SELPAIR=[2, 128], BD2=[128, 2],
                       ONESTK=[128, 1], PERM=[128, 128])
        for nm, shp in cshapes.items():
            dt[nm] = nc.dram_tensor(nm, shp, BF16, kind="ExternalInput")
        dt["ONESF"] = nc.dram_tensor("ONESF", [128, 128], F32, kind="ExternalInput")
        dt["ONES64"] = nc.dram_tensor("ONES64", [1, 64], F32, kind="ExternalInput")
        dt["wqkv_l"] = nc.dram_tensor("wqkv_l", [6, 4, NKT, 128, 768], F32, kind="ExternalInput")
        dt["wo_l"] = nc.dram_tensor("wo_l", [6, 4, NKT, 2, 128, 128], F32, kind="ExternalInput")
        dt["w1_l"] = nc.dram_tensor("w1_l", [6, NFT, NKT, 128, 256], F32, kind="ExternalInput")
        dt["w2_l"] = nc.dram_tensor("w2_l", [6, NKT, NFT, 128, 128], F32, kind="ExternalInput")
        for nm, shp in [("b1a_l", [6, 128, NFT]), ("b1g_l", [6, 128, NFT]),
                        ("b2_l", [6, 128, NKT]), ("g_l", [6, 128, NKT]),
                        ("bl_l", [6, 128, NKT]), ("qs_l", [6, 2, 128]),
                        ("ks_l", [6, 2, 128])]:
            dt[nm] = nc.dram_tensor(nm, shp, F32, kind="ExternalInput")
        dt["wqkv_t"] = nc.dram_tensor("wqkv_t", [4, NKT, 128, 768], F32, kind="ExternalInput")
        dt["wproj_t"] = nc.dram_tensor("wproj_t", [4, NKT, 2, 128, 128], F32, kind="ExternalInput")
        self.dram = dt

    def alloc_sbuf(self):
        nc = self.nc
        sb = {}

        def A(name, shape, dtype):
            h = nc.alloc_sbuf_tensor("sb_" + name, shape, dtype)
            sb[name] = h[:]
            return h

        A("Y", [128, NKT * T], F32)
        A("SBB", [128, NKT * T], BF16)
        poolb = nc.alloc_sbuf_tensor("POOLB", [128, 24576], BF16)
        pb = poolb[:]
        sb["QP"] = pb[:, 0:4096]
        sb["KP"] = pb[:, 4096:8192]
        sb["VP"] = pb[:, 8192:12288]
        sb["OF"] = pb[:, 12288:16384]
        sb["WA"] = pb[:, 16384:22528]
        sb["WO"] = pb[:, 22528:24576]
        sb["GB"] = pb[:, 0:11264]
        sb["W1A"] = pb[:, 11264:13312]
        sb["W1B"] = pb[:, 13312:15360]
        sb["W2A"] = pb[:, 15360:18176]
        sb["W2B"] = pb[:, 18176:20992]
        for nm in ("CT", "STT", "CTI", "STI"):
            A(nm, [128, T], F32)
        for nm, shp, dt_ in [("B384", [128, 384], BF16), ("B256", [128, 256], BF16),
                             ("B128", [128, 128], BF16), ("IDENT", [128, 128], BF16),
                             ("ONESB", [128, 128], BF16), ("NEGROW", [1, 384], BF16),
                             ("ONE1", [1, 128], BF16), ("BAPPW", [128, 9 * 128], BF16),
                             ("ONESF", [128, 128], F32),
                             ("SELPAIR", [2, 128], BF16), ("BD2", [128, 2], BF16),
                             ("ONESTK", [128, 1], BF16), ("ONES64", [1, 64], F32),
                             ("PERM", [128, 128], BF16)]:
            A(nm, shp, dt_)
        for nm, shp in [("B1A", [128, NFT]), ("B1G", [128, NFT]),
                        ("B2V", [128, NKT]), ("GV", [128, NKT]),
                        ("BLV", [128, NKT]), ("QSR", [2, 128]), ("KSR", [2, 128]),
                        ("QS8", [2, 128])]:
            A(nm, shp, F32)
        A("SELQ", [2, 128], BF16)
        A("SELK", [2, 128], BF16)
        for i in range(4):
            A(f"SC{i}", [128, 512], F32)
        A("QSQ0", [128, 512], BF16)
        A("QSQ1", [128, 512], BF16)
        A("EB0", [128, 512], BF16)
        A("EB1", [128, 512], BF16)
        A("MTZ", [1, 512], BF16)
        A("MBF", [128, 16], F32)
        A("MBB0", [128, 16], BF16)
        A("MBB1", [128, 16], BF16)
        A("LB", [64, 512], F32)
        A("RL", [1, 512], F32)
        A("MU", [128, 512], F32)
        A("RSTD", [128, 512], F32)
        self.sb = sb
        assert nc.sbuf_bytes_remaining >= 0, "SBUF over budget"

    def alloc_psum(self):
        nc = self.nc
        ps = {}
        for i in range(6):
            ps[f"ps{i}"] = nc.alloc_psum_tensor(f"ps{i}", [128, 512], F32)[:]
        ps["psSTA"] = nc.alloc_psum_tensor("psSTA", [128, 512], F32)[:]
        ps["psSTB"] = nc.alloc_psum_tensor("psSTB", [128, 512], F32)[:]
        self.psum = ps

    def mm(self, out_ap, lhsT, rhs, bank, r, start, stop):
        self.P.op("PE", lambda e, o=out_ap, l=lhsT, rr=rhs, s=start, t=stop:
                  e.matmul(o, l, rr, start=s, stop=t), r=r, w=(bank,))

    def dma(self, eng, dst, src, key, r=(), w=()):
        self.P.op(eng, lambda e, d=dst, s=src: e.dma_start(d, s),
                  r=r, w=w, dma_key=key)

    def load_consts(self):
        sb, dt = self.sb, self.dram
        for s in ("B384", "B256", "B128", "IDENT", "ONESB", "NEGROW", "ONE1",
                  "BAPPW", "ONESF", "SELPAIR", "BD2", "ONESTK", "ONES64",
                  "PERM"):
            self.dma("SP", sb[s], dt[s][:], key="c_" + s, w=("c" + s,))
        for s, d in [("CT", "ct"), ("STT", "st"), ("CTI", "cti"), ("STI", "sti")]:
            self.dma("SP", sb[s], dt[d][:], key="c_" + s, w=("c" + s,))

        for kt in range(NKT):
            self.dma("SP", sb["Y"][:, kt * T:(kt + 1) * T],
                     dt["xT"][128 * kt:128 * (kt + 1), :], key=f"yl{kt}",
                     w=(f"Y{kt}",))

    def load_layer_vecs(self, li):
        dt, sb, P = self.dram, self.sb, self.P
        for dn, sn, buf in [("b1a_l", "B1A", "b1a"), ("b1g_l", "B1G", "b1g"),
                            ("b2_l", "B2V", "b2v"), ("g_l", "GV", "gv"),
                            ("bl_l", "BLV", "blv")]:
            self.dma("SP", sb[sn], dt[dn][li], key="lv_" + sn, w=(buf,))
        self.dma("SP", sb["QSR"], dt["qs_l"][li], key="lv_QSR", w=("qsr",))
        self.dma("SP", sb["KSR"], dt["ks_l"][li], key="lv_KSR", w=("ksr",))
        P.op("DVE", lambda e: e.tensor_scalar_mul(sb["QS8"], sb["QSR"], 8.0),
             r=("qsr",), w=("qs8",))
        P.op("POOL", lambda e: e.tensor_mul(sb["SELQ"], sb["SELPAIR"], sb["QS8"]),
             r=("cSELPAIR", "qs8"), w=("selq",))
        P.op("POOL", lambda e: e.tensor_mul(sb["SELK"], sb["SELPAIR"], sb["KSR"]),
             r=("cSELPAIR", "ksr"), w=("selk",))

    def ln_stats(self, chunk, shifted):
        sb, ps, P = self.sb, self.psum, self.P
        c0 = 512 * chunk
        for kt in range(NKT):
            sh = shifted and kt >= 4
            edge = sh and chunk == 0
            if edge:
                P.op("DVE", lambda e, k=kt: e.tensor_copy(
                    sb["SC2"][:, 1:512], sb["Y"][:, k * T: k * T + 511]),
                    r=(f"Y{kt}",), w=("sc2",))
                P.op("DVE", lambda e: e.memset(sb["SC2"][:, 0:1], 0),
                     r=(), w=("sc2",))
                rhs = sb["SC2"]
                rkey = "sc2"
            elif sh:
                rhs = sb["Y"][:, kt * T + c0 - 1: kt * T + c0 + 511]
                rkey = f"Y{kt}"
            else:
                rhs = sb["Y"][:, kt * T + c0: kt * T + c0 + 512]
                rkey = f"Y{kt}"
            self.mm(ps["ps4"][:, 0:512], sb["ONESF"], rhs, "ps4",
                    (rkey, "cONESF"), start=(kt == 0), stop=(kt == NKT - 1))
            xq = sb["SC0"] if kt % 2 == 0 else sb["SC1"]
            xqk = "sc0" if kt % 2 == 0 else "sc1"
            P.op("ACT", lambda e, x=xq, rr=rhs: e.activation(
                x, rr, AF.Square), r=(rkey,), w=(xqk,))
            self.mm(ps["ps5"][:, 0:512], sb["ONESF"], xq,
                    "ps5", (xqk, "cONESF"), start=(kt == 0), stop=(kt == NKT - 1))
        P.op("DVE", lambda e: e.tensor_scalar_mul(
            sb["MU"], ps["ps4"][:, 0:512], 1.0 / D), r=("ps4",), w=("mu",))
        P.op("ACT", lambda e: e.activation(sb["RSTD"], sb["MU"], AF.Square),
             r=("mu",), w=("rstd",))
        P.op("DVE", lambda e: e.scalar_tensor_tensor(
            sb["RSTD"], ps["ps5"][:, 0:512], 1.0 / D, sb["RSTD"],
            ALU.mult, ALU.subtract), r=("ps5", "rstd"), w=("rstd",))
        P.op("DVE", lambda e: e.tensor_scalar_add(sb["RSTD"], sb["RSTD"], EPS),
             r=("rstd",), w=("rstd",))
        P.op("ACT", lambda e: e.activation(sb["RSTD"], sb["RSTD"], AF.Ln),
             r=("rstd",), w=("rstd",))
        P.op("ACT", lambda e: e.activation(sb["RSTD"], sb["RSTD"], AF.Exp,
                                           scale=-0.5),
             r=("rstd",), w=("rstd",))

    def ln_apply_inplace(self, chunk):
        sb, P = self.sb, self.P
        c0 = 512 * chunk
        for kt in range(NKT):
            yap = sb["Y"][:, kt * T + c0: kt * T + c0 + 512]
            P.op("DVE", lambda e, y=yap: e.tensor_sub(y, y, sb["MU"]),
                 r=(f"Y{kt}", "mu"), w=(f"Y{kt}",))
            P.op("DVE", lambda e, y=yap: e.tensor_mul(y, y, sb["RSTD"]),
                 r=(f"Y{kt}", "rstd"), w=(f"Y{kt}",))

    def ln_apply_to_sbb(self, chunk, affine, shifted):
        sb, P = self.sb, self.P
        c0 = 512 * chunk
        for kt in range(NKT):
            sh = shifted and kt >= 4
            edge = sh and chunk == 0
            dst = sb["SBB"][:, kt * T + c0: kt * T + c0 + 512]
            t1 = sb["SC2"]
            if edge:
                src = sb["Y"][:, kt * T: kt * T + 511]
                P.op("DVE", lambda e, s=src: e.tensor_sub(
                    t1[:, 1:512], s, sb["MU"][:, 1:512]),
                    r=(f"Y{kt}", "mu"), w=("sc2",))
                P.op("POOL", lambda e: e.tensor_scalar_mul(
                    t1[:, 0:1], sb["MU"][:, 0:1], -1.0), r=("mu",), w=("sc2",))
            else:
                src = sb["Y"][:, kt * T + c0 - (1 if sh else 0):
                              kt * T + c0 + 512 - (1 if sh else 0)]
                P.op("DVE", lambda e, s=src: e.tensor_sub(t1, s, sb["MU"]),
                     r=(f"Y{kt}", "mu"), w=("sc2",))
            if affine:
                P.op("DVE", lambda e: e.tensor_mul(sb["SC3"], t1, sb["RSTD"]),
                     r=("sc2", "rstd"), w=("sc3",))
                P.op("DVE", lambda e, d=dst, kk=kt: e.tensor_scalar(
                    d, sb["SC3"], sb["GV"][:, kk:kk + 1], sb["BLV"][:, kk:kk + 1],
                    ALU.mult, ALU.add),
                    r=("sc3", "gv", "blv"), w=(f"SBB{kt}",))
            else:
                P.op("DVE", lambda e, d=dst: e.tensor_mul(d, t1, sb["RSTD"]),
                     r=("sc2", "rstd"), w=(f"SBB{kt}",))

    def shift_to_sbb(self):
        sb, P = self.sb, self.P
        for kt in range(NKT):
            dst = sb["SBB"][:, kt * T: (kt + 1) * T]
            if kt < 4:
                P.op("DVE", lambda e, d=dst, k=kt: e.tensor_copy(
                    d, sb["Y"][:, k * T:(k + 1) * T]),
                    r=(f"Y{kt}",), w=(f"SBB{kt}",))
            else:
                P.op("DVE", lambda e, d=dst, k=kt: e.tensor_copy(
                    d[:, 1:T], sb["Y"][:, k * T: (k + 1) * T - 1]),
                    r=(f"Y{kt}",), w=(f"SBB{kt}",))
                P.op("DVE", lambda e, d=dst: e.memset(d[:, 0:1], 0),
                     r=(), w=(f"SBB{kt}",))

    def attn(self, li, top):
        sb, ps, dt, P = self.sb, self.psum, self.dram, self.P
        wsl = dt["wqkv_t"] if top else dt["wqkv_l"][li]
        wosl = dt["wproj_t"] if top else dt["wo_l"][li]
        for g in range(4):
            self.dma("POOL", sb["WA"].rearrange("p (kt n) -> p kt n", kt=NKT),
                     wsl[g].rearrange("kt p n -> p kt n"), key="wa", w=("pool.WA",))
            self.dma("POOL", sb["WO"].rearrange("p (kt n) -> p kt n", kt=NKT * 2),
                     wosl[g].rearrange("nt kp p n -> p (nt kp) n"),
                     key="wo", w=("pool.WO",))
            pending = []
            ui = 0
            for pr in range(2):
                for part, dstbuf in ((0, "QP"), (1, "KP")):
                    for ch in range(4):
                        bank = "ps0" if ui % 2 == 0 else "ps1"
                        pb = ps[bank]
                        for kt in range(NKT):
                            lhs = sb["WA"][:, kt * 768 + 256 * part + 128 * pr:
                                           kt * 768 + 256 * part + 128 * (pr + 1)]
                            rhs = sb["SBB"][:, kt * T + 512 * ch: kt * T + 512 * ch + 512]
                            self.mm(pb[:, 0:512], lhs, rhs, bank,
                                    ("pool.WA", f"SBB{kt}"),
                                    start=(kt == 0), stop=(kt == NKT - 1))
                        if top:
                            dst = sb[dstbuf][:, pr * T + 512 * ch: pr * T + 512 * ch + 512]
                            P.op("ACT", lambda e, d=dst, p=pb: e.activation(
                                d, p[:, 0:512], AF.Copy),
                                r=(bank,), w=(f"pool.{dstbuf}{pr}",))
                        else:
                            qsbk = "sc0" if ui % 2 == 0 else "sc1"
                            qsb = sb["SC0"] if ui % 2 == 0 else sb["SC1"]
                            P.op("ACT", lambda e, q=qsb, p=pb: e.activation(
                                q, p[:, 0:512], AF.Copy), r=(bank,), w=(qsbk,))
                            pending.append((pr, part, ch, dstbuf, ui))
                            if len(pending) >= 2:
                                self.prep_qk(*pending.pop(0))
                        ui += 1
            for args in pending:
                self.prep_qk(*args)
            for tt in range(16):
                pb = ps["ps2"]
                for kt in range(NKT):
                    lhs = sb["SBB"][:, kt * T + 128 * tt: kt * T + 128 * (tt + 1)]
                    rhs = sb["WA"][:, kt * 768 + 512: kt * 768 + 768]
                    self.mm(pb[:, 0:256], lhs, rhs, "ps2",
                            ("pool.WA", f"SBB{kt}"),
                            start=(kt == 0), stop=(kt == NKT - 1))
                P.op("ACT", lambda e, t=tt, p=pb: e.activation(
                    sb["VP"][:, 256 * t: 256 * (t + 1)], p[:, 0:256], AF.Copy),
                    r=("ps2",), w=(f"pool.VP{tt % 2}",))
            self.heads_weave(top)
            for nt in range(NKT):
                for ch in range(4):
                    bank = "ps0" if (nt + ch) % 2 == 0 else "ps1"
                    pb = ps[bank]
                    for kp in range(2):
                        lhs = sb["WO"][:, (nt * 2 + kp) * 128:(nt * 2 + kp + 1) * 128]
                        rhs = sb["OF"][:, kp * T + 512 * ch: kp * T + 512 * ch + 512]
                        self.mm(pb[:, 0:512], lhs, rhs, bank,
                                ("pool.WO", f"pool.OF{kp}"),
                                start=(kp == 0), stop=(kp == 1))
                    yap = sb["Y"][:, nt * T + 512 * ch: nt * T + 512 * ch + 512]
                    P.op("DVE", lambda e, y=yap, p=pb: e.tensor_add(
                        y, y, p[:, 0:512]), r=(f"Y{nt}", bank), w=(f"Y{nt}",))

    def prep_qk(self, pr, part, ch, dstbuf, ui):
        sb, ps, P = self.sb, self.psum, self.P
        pk = "ps3" if ui % 2 == 0 else "ps5"
        pchain = ps[pk]
        qsb = sb["SC0"] if ui % 2 == 0 else sb["SC1"]
        qsbk = "sc0" if ui % 2 == 0 else "sc1"
        qsq = sb["QSQ0"] if ui % 2 == 0 else sb["QSQ1"]
        qsqk = "qsq0" if ui % 2 == 0 else "qsq1"
        P.op("ACT", lambda e, q=qsq, s=qsb: e.activation(q, s, AF.Square),
             r=(qsbk,), w=(qsqk,))
        self.mm(pchain[0:2, 0:512], sb["BD2"], qsq, pk,
                (qsqk, "cBD2"), start=True, stop=True)
        P.op("DVE", lambda e, p=pchain: e.tensor_scalar_max(
            p[0:2, 0:512], p[0:2, 0:512], 1e-24), r=(pk,), w=(pk,))
        P.op("ACT", lambda e, p=pchain: e.activation(
            p[32:34, 0:512], p[0:2, 0:512], AF.Ln), r=(pk,), w=(pk,))
        rb = qsq[0:2, 0:512]
        P.op("ACT", lambda e, p=pchain, rr=rb: e.activation(
            rr, p[32:34, 0:512], AF.Exp, scale=-0.5), r=(pk,), w=(qsqk,))
        selbuf = "SELQ" if part == 0 else "SELK"
        selkey = "selq" if part == 0 else "selk"
        self.mm(pchain[:, 0:512], sb[selbuf], rb, pk,
                (qsqk, selkey), start=True, stop=True)
        slot = sb[dstbuf][:, pr * T + 512 * ch: pr * T + 512 * ch + 512]
        slotk = f"pool.{dstbuf}{pr}"
        P.op("DVE", lambda e, s=slot, q=qsb, p=pchain: e.tensor_mul(
            s, q, p[:, 0:512]), r=(qsbk, pk), w=(slotk,))
        self.mm(pchain[:, 0:512], sb["PERM"], slot, pk,
                (slotk, "cPERM"), start=True, stop=True)
        ctb = "CT" if part == 0 else "CTI"
        stb = "STT" if part == 0 else "STI"
        z0, z0k = (sb["SC2"], "sc2") if ui % 2 == 0 else (sb["SC3"], "sc3")
        tab = sb[ctb][:, 512 * ch: 512 * ch + 512]
        P.op("DVE", lambda e, s=slot, t=tab, z=z0: e.tensor_mul(z, s, t),
             r=(slotk, "c" + ctb), w=(z0k,))
        tab2 = sb[stb][:, 512 * ch: 512 * ch + 512]
        P.op("DVE", lambda e, s=slot, t=tab2, p=pchain: e.tensor_mul(
            s, p[:, 0:512], t), r=(pk, "c" + stb), w=(slotk,))
        P.op("DVE", lambda e, s=slot, z=z0: e.tensor_add(s, s, z),
             r=(slotk, z0k), w=(slotk,))

    def heads_weave(self, top):
        for hl in range(5):
            hs = hl if hl < 4 else None
            ha = hl - 1 if hl > 0 else None
            for c in range(4):
                if ha is not None:
                    self.mtz_stage(ha, c)
                glo = 0 if top else max(0, 4 * c - 2)
                ntt = 4 * c + 4 - glo
                prev = None
                napp = ntt if ha is not None else 0
                nst = 4 if hs is not None else 0
                for i in range(max(napp, nst)):
                    if i < napp:
                        self.apply_tile_pe(ha, c, glo, i, top)
                        if prev is not None:
                            self.apply_tile_av(ha, c, glo, prev, ntt, top)
                        prev = i
                    if i < nst:
                        self.stats_j(hs, 4 * c + i, top)
                if prev is not None:
                    self.apply_tile_av(ha, c, glo, prev, ntt, top)
                if ha is not None and c > 0:
                    self.apply_stage2(ha, c - 1, top)
            if ha is not None:
                self.apply_stage2(ha, 3, top)

    def mtz_stage(self, hl, c):
        sb, ps, P = self.sb, self.psum, self.P
        mbb = sb["MBB0"] if hl % 2 == 0 else sb["MBB1"]
        mbk = "mbb0" if hl % 2 == 0 else "mbb1"
        for i in range(4):
            self.mm(ps["psSTB"][0:1, 128 * i:128 * (i + 1)],
                    mbb[:, 4 * c + i:4 * c + i + 1], sb["IDENT"],
                    "psSTB", (mbk, "cIDENT"), start=True, stop=True)
        P.op("ACT", lambda e: e.activation(
            sb["MTZ"], ps["psSTB"][0:1, 0:512], AF.Copy),
            r=("psSTB",), w=("mtz",))

    def apply_tile_pe(self, hl, c, glo, tt, top):
        sb, ps, P = self.sb, self.psum, self.P
        pr = hl // 2
        rh = 64 * (hl % 2)
        qk = f"pool.QP{pr}"
        kk = f"pool.KP{pr}"
        g = glo + tt
        kqb = "ps2" if tt % 2 == 0 else "ps4"
        pkq = ps[kqb]
        lhsk = sb["KP"][rh:rh + 64, pr * T + 128 * g: pr * T + 128 * (g + 1)]
        rhsq = sb["QP"][rh:rh + 64, pr * T + 512 * c: pr * T + 512 * c + 512]
        self.mm(pkq[:, 0:512], lhsk, rhsq, kqb, (qk, kk),
                start=True, stop=False)
        if top:
            dd = g - 4 * c
            if dd > 0:
                self.mm(pkq[:, 0:128 * dd], sb["ONE1"],
                        sb["NEGROW"][0:1, 0:128 * dd], kqb,
                        ("cONE1", "cNEGROW"), start=False, stop=False)
            if 0 <= dd <= 3:
                self.mm(pkq[:, 128 * dd:128 * dd + 128], sb["IDENT"],
                        sb["BAPPW"][:, 128 * 3:128 * 4], kqb,
                        ("cIDENT", "cBAPPW"), start=False, stop=False)
        else:
            s = 3 - g + 4 * c
            self.mm(pkq[:, 0:512], sb["IDENT"],
                    sb["BAPPW"][:, 128 * s:128 * s + 512], kqb,
                    ("cIDENT", "cBAPPW"), start=False, stop=False)
        self.mm(pkq[:, 0:512], sb["ONE1"], sb["MTZ"], kqb,
                ("cONE1", "mtz"), start=False, stop=True)
        ebuf = sb["EB0"] if tt % 2 == 0 else sb["EB1"]
        ebk = "eb0" if tt % 2 == 0 else "eb1"
        P.op("ACT", lambda e, eb=ebuf, p=pkq, sc=(0.125 if top else 1.0):
             e.activation(eb, p[:, 0:512], AF.Exp, scale=sc),
             r=(kqb,), w=(ebk,))

    def apply_tile_av(self, hl, c, glo, tt, ntt, top):
        sb, ps, P = self.sb, self.psum, self.P
        pr = hl // 2
        rh = 64 * (hl % 2)
        g = glo + tt
        avb = "ps3" if c % 2 == 0 else "ps5"
        pav = ps[avb]
        ebuf = sb["EB0"] if tt % 2 == 0 else sb["EB1"]
        ebk = "eb0" if tt % 2 == 0 else "eb1"
        vslot = sb["VP"][:, 256 * g + 128 * pr + rh:
                         256 * g + 128 * pr + rh + 64]
        self.mm(pav[0:64, 0:512], vslot, ebuf, avb,
                (ebk, f"pool.VP{g % 2}"),
                start=(tt == 0), stop=(tt == ntt - 1))
        lbk = "ps0" if c % 2 == 0 else "ps1"
        self.mm(ps[lbk][64:65, 0:512], sb["ONESTK"], ebuf, lbk,
                (ebk, "cONESTK"), start=(tt == 0), stop=(tt == ntt - 1))

    def stats_j(self, hl, j, top):
        sb, ps, P = self.sb, self.psum, self.P
        mbb = sb["MBB0"] if hl % 2 == 0 else sb["MBB1"]
        mbk = "mbb0" if hl % 2 == 0 else "mbb1"
        pr = hl // 2
        rh = 64 * (hl % 2)
        qk = f"pool.QP{pr}"
        kk = f"pool.KP{pr}"
        pst = ps["psSTA"]
        tk_lo = 0 if top else max(0, 128 * (j - 2))
        tk_hi = 128 * (j + 1)
        wj = tk_hi - tk_lo
        lhs = sb["QP"][rh:rh + 64, pr * T + 128 * j: pr * T + 128 * (j + 1)]
        if top:
            nch = (wj + 511) // 512
            for cc in range(nch):
                a = tk_lo + 512 * cc
                b = min(tk_hi, a + 512)
                rhs = sb["KP"][rh:rh + 64, pr * T + a: pr * T + b]
                diag = (b == tk_hi)
                self.mm(pst[:, 0:(b - a)], lhs, rhs, "psSTA",
                        (qk, kk), start=True, stop=not diag)
                if diag:
                    off = (b - a) - 128
                    self.mm(pst[:, off:off + 128], sb["IDENT"],
                            sb["B128"], "psSTA", ("cIDENT", "cB128"),
                            start=False, stop=True)
                red_dst = sb["MBF"][:, cc:cc + 1]
                P.op("DVE", lambda e, rd=red_dst, n=(b - a):
                     e.reduce_max(rd, pst[:, 0:n],
                                  axis=AX.X), r=("psSTA",), w=("mbf",))
            P.op("POOL", lambda e, n=nch: e.tensor_copy(
                sb["MBF"][:, 8:8 + n], sb["MBF"][:, 0:n]),
                r=("mbf",), w=("mbf2",))
            P.op("DVE", lambda e, jj=j, n=nch, m=mbb: e.reduce_max(
                m[:, jj:jj + 1], sb["MBF"][:, 8:8 + n], axis=AX.X,
                negate=True), r=("mbf2",), w=(mbk,))
        else:
            rhs = sb["KP"][rh:rh + 64, pr * T + tk_lo: pr * T + tk_hi]
            self.mm(pst[:, 0:wj], lhs, rhs, "psSTA", (qk, kk),
                    start=True, stop=False)
            mask = {128: "B128", 256: "B256", 384: "B384"}[wj]
            self.mm(pst[:, 0:wj], sb["IDENT"], sb[mask],
                    "psSTA", ("cIDENT", "c" + mask), start=False, stop=True)
            P.op("DVE", lambda e, jj=j, n=wj, m=mbb: e.reduce_max(
                m[:, jj:jj + 1], pst[:, 0:n], axis=AX.X,
                negate=True), r=("psSTA",), w=(mbk,))

    def apply_stage2(self, hl, c, top):
        sb, ps, P = self.sb, self.psum, self.P
        pr = hl // 2
        rh = 64 * (hl % 2)
        avb = "ps3" if c % 2 == 0 else "ps5"
        lbk = "ps0" if c % 2 == 0 else "ps1"
        pav = ps[avb]
        P.op("ACT", lambda e, p=ps[lbk]: e.activation(
            ps["psSTB"][32:33, 0:512], p[64:65, 0:512], AF.Ln),
            r=(lbk,), w=("psSTB",))
        P.op("ACT", lambda e: e.activation(
            sb["RL"], ps["psSTB"][32:33, 0:512], AF.Exp, scale=-1.0),
            r=("psSTB",), w=("rl",))
        self.mm(ps["psSTB"][64:128, 0:512], sb["ONES64"], sb["RL"], "psSTB",
                ("rl", "cONES64"), start=True, stop=True)
        P.op("ACT", lambda e: e.activation(
            sb["LB"], ps["psSTB"][64:128, 0:512], AF.Copy),
            r=("psSTB",), w=("lb",))
        odst = sb["OF"][rh:rh + 64, pr * T + 512 * c: pr * T + 512 * c + 512]
        P.op("DVE", lambda e, o=odst, p=pav: e.tensor_mul(
            o, p[0:64, 0:512], sb["LB"]),
            r=(avb, "lb"), w=(f"pool.OF{pr}",))

    def ffn(self, li):
        sb, ps, dt, P = self.sb, self.psum, self.dram, self.P
        for ch in range(4):
            self.ln_stats(ch, shifted=True)
            self.ln_apply_to_sbb(ch, affine=True, shifted=True)
        for ch in range(4):
            for j in range(NFT):
                wbuf = sb["W1A"] if j % 2 == 0 else sb["W1B"]
                wkey = "pool.W1A" if j % 2 == 0 else "pool.W1B"
                self.dma("POOL", wbuf.rearrange("p (kt n) -> p kt n", kt=NKT),
                         dt["w1_l"][li, j].rearrange("kt p n -> p kt n"),
                         key="w1" + wkey[-1], w=(wkey,))
                ab = "ps0" if j % 2 == 0 else "ps2"
                gb = "ps1" if j % 2 == 0 else "ps3"
                for kt in range(NKT):
                    rhs = sb["SBB"][:, kt * T + 512 * ch: kt * T + 512 * ch + 512]
                    self.mm(ps[ab][:, 0:512], wbuf[:, kt * 256: kt * 256 + 128],
                            rhs, ab, (wkey, f"SBB{kt}"),
                            start=(kt == 0), stop=(kt == NKT - 1))
                    self.mm(ps[gb][:, 0:512], wbuf[:, kt * 256 + 128: kt * 256 + 256],
                            rhs, gb, (wkey, f"SBB{kt}"),
                            start=(kt == 0), stop=(kt == NKT - 1))
                gl = sb["SC0"] if j % 2 == 0 else sb["SC1"]
                glk = "sc0" if j % 2 == 0 else "sc1"
                P.op("ACT", lambda e, g=gl, jj=j, pg=ps[gb]: e.activation(
                    g, pg[:, 0:512], AF.Gelu,
                    bias=sb["B1G"][:, jj:jj + 1]), r=(gb, "b1g"), w=(glk,))
                gdst = sb["GB"][:, 512 * j: 512 * (j + 1)]
                P.op("DVE", lambda e, g=gl, d=gdst, jj=j, pa=ps[ab]:
                     e.scalar_tensor_tensor(
                    d, pa[:, 0:512], sb["B1A"][:, jj:jj + 1], g,
                    ALU.add, ALU.mult), r=(ab, glk, "b1a"),
                    w=(f"pool.GB{j % 2}",))
            for nt in range(NKT):
                wbuf = sb["W2A"] if nt % 2 == 0 else sb["W2B"]
                wkey = "pool.W2A" if nt % 2 == 0 else "pool.W2B"
                self.dma("POOL", wbuf.rearrange("p (kt n) -> p kt n", kt=NFT),
                         dt["w2_l"][li, nt].rearrange("kt p n -> p kt n"),
                         key="w2" + wkey[-1], w=(wkey,))
                ob = "ps4" if nt % 2 == 0 else "ps5"
                for kt in range(NFT):
                    self.mm(ps[ob][:, 0:512], wbuf[:, kt * 128:(kt + 1) * 128],
                            sb["GB"][:, 512 * kt: 512 * (kt + 1)], ob,
                            (wkey, f"pool.GB{kt % 2}"),
                            start=(kt == 0), stop=(kt == NFT - 1))
                yap = sb["Y"][:, nt * T + 512 * ch: nt * T + 512 * ch + 512]
                P.op("DVE", lambda e, y=yap, n=nt, po=ps[ob]:
                     e.scalar_tensor_tensor(
                    y, po[:, 0:512], sb["B2V"][:, n:n + 1], y,
                    ALU.add, ALU.add), r=(ob, "b2v", f"Y{nt}"), w=(f"Y{nt}",))

    def merge_with_input(self, dram_src, factor, to_out=False):
        sb, dt, P = self.sb, self.dram, self.P
        for kt in range(NKT):
            for ch in range(4):
                par = (kt * 4 + ch) % 2
                xb = sb["SC0"] if par == 0 else sb["SC1"]
                xbk = "sc0" if par == 0 else "sc1"
                self.dma("SP", xb,
                         dram_src[128 * kt:128 * (kt + 1), 512 * ch:512 * ch + 512],
                         key=xbk, r=(), w=(xbk,))
                yap = sb["Y"][:, kt * T + 512 * ch: kt * T + 512 * ch + 512]
                if to_out:
                    ob = sb["SC2"] if par == 0 else sb["SC3"]
                    obk = "sc2" if par == 0 else "sc3"
                    P.op("DVE", lambda e, y=yap, x=xb, o=ob, f=factor:
                         e.scalar_tensor_tensor(o, y, f, x, ALU.mult, ALU.add),
                         r=(f"Y{kt}", xbk), w=(obk,))
                    self.dma("SP", dt["out"][128 * kt:128 * (kt + 1),
                                             512 * ch:512 * ch + 512],
                             ob, key="outst", r=(obk,), w=())
                else:
                    P.op("DVE", lambda e, y=yap, x=xb, f=factor:
                         e.scalar_tensor_tensor(y, y, f, x, ALU.mult, ALU.add),
                         r=(f"Y{kt}", xbk), w=(f"Y{kt}",))

    def store_y_to_out(self):
        sb, dt = self.sb, self.dram
        for kt in range(NKT):
            self.dma("SP", dt["out"][128 * kt:128 * (kt + 1), :],
                     sb["Y"][:, kt * T:(kt + 1) * T], key="outst",
                     r=(f"Y{kt}",), w=())

    def ffn6(self, n_sub):
        for i in range((n_sub + 1) // 2):
            self.load_layer_vecs(i)
            self.P.fence_pool()
            self.shift_to_sbb()
            self.attn(i, top=False)
            if 2 * i + 2 > n_sub:
                return
            self.P.fence_pool()
            self.ffn(i)

    def build(self):
        cfg = self.cfg
        self.load_consts()
        if cfg.get("ln0", True):
            for ch in range(4):
                self.ln_stats(ch, shifted=False)
                self.ln_apply_inplace(ch)
        self.ffn6(cfg.get("n_sub1", 12))
        if cfg.get("merge1", True):
            self.merge_with_input(self.dram["xT"], 0.5)
        if cfg.get("top", True):
            for ch in range(4):
                self.ln_stats(ch, shifted=False)
                self.ln_apply_to_sbb(ch, affine=False, shifted=False)
            self.P.fence_pool()
            self.attn(0, top=True)
        if cfg.get("call2", True):
            for kt in range(NKT):
                self.dma("SP", self.dram["x2scr"][128 * kt:128 * (kt + 1), :],
                         self.sb["Y"][:, kt * T:(kt + 1) * T], key="x2st",
                         r=(f"Y{kt}",), w=())
            for ch in range(4):
                self.ln_stats(ch, shifted=False)
                self.ln_apply_inplace(ch)
            self.ffn6(cfg.get("n_sub2", 12))
            self.merge_with_input(self.dram["x2scr"], 0.5, to_out=True)
        elif cfg.get("dump") == "of":
            for kp in range(2):
                self.dma("POOL", self.dram["out"][128 * kp:128 * (kp + 1), :],
                         self.sb["OF"][:, kp * T:(kp + 1) * T], key="outst",
                         r=(f"pool.OF{kp}",), w=())
        elif cfg.get("dump") == "sbb":
            for kt in range(NKT):
                self.dma("POOL", self.dram["out"][128 * kt:128 * (kt + 1), :],
                         self.sb["SBB"][:, kt * T:(kt + 1) * T], key="outst",
                         r=(f"SBB{kt}",), w=())
        else:
            self.store_y_to_out()


def build_program(cfg=None):
    cfg = cfg or {}
    nc = bass.Bass("TRN2", target_bir_lowering=False)
    mk = MK(nc, cfg)
    mk.build()
    counters = mk.P.resolve()
    sems = {k: nc.alloc_semaphore(name=f"s{i}")
            for i, k in enumerate(counters)}
    with nc.allow_low_precision(reason="bf16 matmul operands / exp tiles"), \
            nc.Block() as block:
        @block.tensor
        def _(tensor):
            mk.P.emit("PE", tensor, sems)

        @block.scalar
        def _(scalar):
            mk.P.emit("ACT", scalar, sems)

        @block.vector
        def _(vector):
            mk.P.emit("DVE", vector, sems)

        @block.gpsimd
        def _(gpsimd):
            mk.P.emit("POOL", gpsimd, sems)

        @block.sync
        def _(sync):
            mk.P.emit("SP", sync, sems)
            if "outst" in counters:
                sync.wait_ge(sems["outst"], counters["outst"])
    return nc


def stage_all(inputs):
    tabs = host_tables()
    consts = host_consts()
    staged = stage_weights(inputs)
    shared = {}
    shared.update({k: tabs[k] for k in ("ct", "st", "cti", "sti")})
    shared.update(consts)
    shared.update(staged)
    return shared


def kernel(**inputs):
    nc = build_program({})
    shared = stage_all(inputs)
    x = np.asarray(inputs["L0_attn_x"])
    in_maps = []
    for b in range(8):
        m = dict(shared)
        m["xT"] = np.ascontiguousarray(x[b].T)
        in_maps.append(m)
    res = run_bass_kernel_spmd(nc, in_maps, core_ids=list(range(8)))
    out = np.stack([np.asarray(res.results[b]["out"]).T for b in range(8)])
    return out.astype(np.float32)


# version 09
# speedup vs baseline: 1.4837x; 1.4837x over previous
import numpy as np
import ml_dtypes
import concourse.bass as bass
import concourse.mybir as mybir
from concourse.bass_utils import run_bass_kernel_spmd

F32 = mybir.dt.float32
BF16 = mybir.dt.bfloat16
AF = mybir.ActivationFunctionType
ALU = mybir.AluOpType
AX = mybir.AxisListType

D = 1024
T = 2048
NKT = 8
NH = 16
NJ = 16
DIP = 2816
NFT = 22
NEG = -3.0e38
EPS = 1e-5

ENGINES = ("PE", "ACT", "DVE", "POOL", "SP")


class Op:
    __slots__ = ("eng", "fn", "reads", "writes", "dma_key", "deps", "idx",
                 "sem_key", "inc_val", "clock", "needed", "waits", "phase")

    def __init__(self, eng, fn, reads, writes, dma_key):
        self.eng = eng
        self.fn = fn
        self.reads = reads
        self.writes = writes
        self.dma_key = dma_key
        self.deps = []
        self.sem_key = None
        self.inc_val = None
        self.clock = None
        self.needed = False
        self.waits = ()


class Prog:
    def __init__(self):
        self.ops = []
        self.last_w = {}
        self.readers = {}
        self.pending_fence = None
        self.phase = "init"
        self.inst_phase = {}

    def fence_pool(self):
        per_eng = {}
        for key in [k for k in self.last_w if k.startswith("pool.")]:
            cands = [self.last_w[key]] + self.readers.get(key, [])
            for o in cands:
                cur = per_eng.get(o.eng)
                if cur is None or o.idx > cur.idx:
                    per_eng[o.eng] = o
            del self.last_w[key]
            self.readers.pop(key, None)
        self.pending_fence = list(per_eng.values())

    def op(self, eng, fn, r=(), w=(), dma_key=None):
        o = Op(eng, fn, tuple(r), tuple(w), dma_key)
        o.phase = self.phase
        o.idx = len(self.ops)
        deps = {}
        for b in o.reads:
            if b in self.last_w:
                d = self.last_w[b]
                deps[d.idx] = d
        for b in o.writes:
            if b in self.last_w:
                d = self.last_w[b]
                deps[d.idx] = d
            for rd in self.readers.get(b, ()):
                deps[rd.idx] = rd
        if self.pending_fence and any(b.startswith("pool.") for b in o.writes):
            for d in self.pending_fence:
                deps[d.idx] = d
        o.deps = list(deps.values())
        for b in o.reads:
            self.readers.setdefault(b, []).append(o)
        for b in o.writes:
            self.last_w[b] = o
            self.readers[b] = []
        self.ops.append(o)
        return o

    def resolve(self):
        for o in self.ops:
            for d in o.deps:
                if d.dma_key is not None or d.eng != o.eng:
                    d.needed = True
            if o.dma_key is not None:
                o.needed = True
        counters = {}
        for o in self.ops:
            if not o.needed:
                continue
            key = o.dma_key if o.dma_key is not None else o.eng
            step = 16 if o.dma_key is not None else 1
            counters[key] = counters.get(key, 0) + step
            o.sem_key = key
            o.inc_val = counters[key]
        clocks = {e: {} for e in ENGINES}

        def merge(dst, src):
            for k, v in src.items():
                if dst.get(k, 0) < v:
                    dst[k] = v

        for o in self.ops:
            ck = clocks[o.eng]
            waits = []
            for d in o.deps:
                if d.sem_key is None:
                    continue
                if d.dma_key is None and d.eng == o.eng:
                    continue
                if ck.get(d.sem_key, 0) < d.inc_val:
                    waits.append((d.sem_key, d.inc_val))
                    if d.clock:
                        merge(ck, d.clock)
                    if ck.get(d.sem_key, 0) < d.inc_val:
                        ck[d.sem_key] = d.inc_val
            o.waits = tuple(waits)
            if o.sem_key is not None:
                o.clock = dict(ck)
                o.clock[o.sem_key] = o.inc_val
        self.counters = counters
        return counters

    def emit(self, eng_name, eng_obj, sems):
        for o in self.ops:
            if o.eng != eng_name:
                continue
            for key, val in o.waits:
                eng_obj.wait_ge(sems[key], val)
            inst = o.fn(eng_obj)
            try:
                self.inst_phase[inst.ins.name] = o.phase
            except Exception:
                pass
            if o.sem_key is not None:
                inst.then_inc(sems[o.sem_key], 16 if o.dma_key else 1)


def _bf(a):
    return np.ascontiguousarray(np.asarray(a, dtype=np.float32)).astype(ml_dtypes.bfloat16)


def _f32(a):
    return np.ascontiguousarray(np.asarray(a), dtype=np.float32)


def host_tables():
    HD = 64
    half = np.arange(0, HD, 2, dtype=np.float64)
    inv_freq = 1.0 / 10000.0 ** (half / HD)
    t = np.arange(T, dtype=np.float64)
    freqs = np.concatenate([t[:, None] * inv_freq[None, :]] * 2, -1)
    base = (half + 0.4 * HD) / (1.4 * HD)
    power = (t - T // 2) / 128.0
    scale = np.concatenate([base[None, :] ** power[:, None]] * 2, -1)
    out = {}
    for nm, tab in [("ct", np.cos(freqs) * scale), ("st", np.sin(freqs) * scale),
                    ("cti", np.cos(freqs) / scale), ("sti", np.sin(freqs) / scale)]:
        tt = tab.T
        out[nm] = _f32(np.concatenate([tt, tt], axis=0))
    return out


def host_consts():
    c = {}
    r = np.arange(128)
    k = r[:, None]
    c384 = np.arange(384)[None, :]
    c["B384"] = _bf(NEG * ((c384 < k) | (c384 > k + 256)))
    c256 = np.arange(256)[None, :]
    c["B256"] = _bf(NEG * (c256 > k + 128))
    c128 = np.arange(128)[None, :]
    c["B128"] = _bf(NEG * (c128 > k))
    c["IDENT"] = _bf(np.eye(128))
    c["ONESF"] = _f32(np.ones((128, 128)))
    c["ONESB"] = _bf(np.ones((128, 128)))
    c["NEGROW"] = _bf(np.full((1, 384), NEG))
    c["ONE1"] = _bf(np.ones((1, 128)))
    bw = np.zeros((128, 9 * 128), np.float32)
    for u in range(9):
        rel = 5 - u
        blk = bw[:, 128 * u:128 * (u + 1)]
        if rel == 0:
            blk[:] = NEG * (k < r[None, :])
        elif rel == 1:
            pass
        elif rel == 2:
            blk[:] = NEG * (k > r[None, :])
        else:
            blk[:] = NEG
    c["BAPPW"] = _bf(bw)
    mb_ = np.zeros((128, 384), np.float32)
    c384b = np.arange(384)[None, :]
    mb_[(c384b < k) | (c384b > k + 256)] = NEG
    c["MASKB"] = _f32(mb_)
    sp = np.zeros((2, 128), np.float32)
    sp[0, :64] = 1.0
    sp[1, 64:] = 1.0
    c["SELPAIR"] = _bf(sp)
    bd = np.zeros((128, 2), np.float32)
    bd[:64, 0] = 1.0
    bd[64:, 1] = 1.0
    c["BD2"] = _bf(bd)
    c["ONESTK"] = _bf(np.ones((128, 1)))
    c["ONES64"] = _f32(np.ones((1, 64)))
    p = np.zeros((64, 64), np.float32)
    for i in range(32):
        p[i + 32, i] = -1.0
        p[i, i + 32] = 1.0
    pp = np.zeros((128, 128), np.float32)
    pp[:64, :64] = p
    pp[64:, 64:] = p
    c["PERM"] = _bf(pp)
    return c


def stage_weights(inp):
    st = {}
    L = 6
    wqkv = _f32(inp["L0_ffn_Wqkv"])
    a = wqkv.reshape(L, NKT, 128, 24, 128)
    slab = np.empty((L, 4, NKT, 128, 768), np.float32)
    for g in range(4):
        for part in range(3):
            for pr in range(2):
                nt = part * 8 + 2 * g + pr
                slab[:, g, :, :, 256 * part + 128 * pr:256 * part + 128 * (pr + 1)] = \
                    a[:, :, :, nt, :]
    st["wqkv_l"] = np.ascontiguousarray(slab)
    wo = _f32(inp["L0_ffn_Wo"]).reshape(L, NKT, 128, NKT, 128)
    wos = np.empty((L, 4, NKT, 2, 128, 128), np.float32)
    for g in range(4):
        for kp in range(2):
            wos[:, g, :, kp] = wo[:, 2 * g + kp].transpose(0, 2, 1, 3)
    st["wo_l"] = np.ascontiguousarray(wos)
    W1 = _f32(inp["L0_ffn_W1"])
    z = np.zeros((L, D, DIP - 2730), np.float32)
    W1a = np.concatenate([W1[:, :, :2730], z], axis=2).reshape(L, D, NFT, 128)
    W1g = np.concatenate([W1[:, :, 2730:], z], axis=2).reshape(L, D, NFT, 128)
    w1s = np.empty((L, NFT, NKT, 128, 256), np.float32)
    for j in range(NFT):
        w1s[:, j, :, :, 0:128] = W1a[:, :, j].reshape(L, NKT, 128, 128)
        w1s[:, j, :, :, 128:256] = W1g[:, :, j].reshape(L, NKT, 128, 128)
    st["w1_l"] = np.ascontiguousarray(w1s)
    b1 = _f32(inp["L0_ffn_b1"])
    zb = np.zeros((L, DIP - 2730), np.float32)
    b1a = np.concatenate([b1[:, :2730], zb], 1).reshape(L, NFT, 128)
    b1g = np.concatenate([b1[:, 2730:], zb], 1).reshape(L, NFT, 128)
    st["b1a_l"] = np.ascontiguousarray(b1a.transpose(0, 2, 1))
    st["b1g_l"] = np.ascontiguousarray(b1g.transpose(0, 2, 1))
    W2 = _f32(inp["L0_ffn_W2"])
    W2p = np.concatenate([W2, np.zeros((L, DIP - 2730, D), np.float32)], 1)
    st["w2_l"] = np.ascontiguousarray(
        W2p.reshape(L, NFT, 128, NKT, 128).transpose(0, 3, 1, 2, 4))
    st["b2_l"] = np.ascontiguousarray(
        _f32(inp["L0_ffn_b2"]).reshape(L, NKT, 128).transpose(0, 2, 1))
    st["g_l"] = np.ascontiguousarray(
        _f32(inp["L0_ffn_ln_g"]).reshape(L, NKT, 128).transpose(0, 2, 1))
    st["bl_l"] = np.ascontiguousarray(
        _f32(inp["L0_ffn_ln_b"]).reshape(L, NKT, 128).transpose(0, 2, 1))
    qs = _f32(inp["L0_ffn_q_scale"])
    ks = _f32(inp["L0_ffn_k_scale"])
    st["qs_l"] = np.ascontiguousarray(
        np.tile(qs, (1, 2))[:, None, :].repeat(2, 1))
    st["ks_l"] = np.ascontiguousarray(
        np.tile(ks, (1, 2))[:, None, :].repeat(2, 1))
    wq = _f32(inp["L0_attn_w_qkv"]).reshape(NKT, 128, 24, 128)
    slabt = np.empty((4, NKT, 128, 768), np.float32)
    for g in range(4):
        for part in range(3):
            for pr in range(2):
                nt = part * 8 + 2 * g + pr
                slabt[g, :, :, 256 * part + 128 * pr:256 * part + 128 * (pr + 1)] = \
                    wq[:, :, nt, :]
    st["wqkv_t"] = np.ascontiguousarray(slabt)
    wp = _f32(inp["L0_attn_w_proj"]).reshape(NKT, 128, NKT, 128)
    wps = np.empty((4, NKT, 2, 128, 128), np.float32)
    for g in range(4):
        for kp in range(2):
            wps[g, :, kp] = wp[2 * g + kp].transpose(1, 0, 2)
    st["wproj_t"] = np.ascontiguousarray(wps)
    return st


class MK:
    def __init__(self, nc, cfg):
        self.nc = nc
        self.cfg = cfg
        self.P = Prog()
        self.declare_dram()
        self.alloc_sbuf()
        self.alloc_psum()

    def declare_dram(self):
        nc = self.nc
        dt = {}
        dt["xT"] = nc.dram_tensor("xT", [D, T], F32, kind="ExternalInput")
        dt["out"] = nc.dram_tensor("out", [D, T], F32, kind="ExternalOutput")
        dt["x2scr"] = nc.dram_tensor("x2scr", [D, T], F32, kind="Internal")
        for nm in ("ct", "st", "cti", "sti"):
            dt[nm] = nc.dram_tensor(nm, [128, T], F32, kind="ExternalInput")
        cshapes = dict(B384=[128, 384], B256=[128, 256], B128=[128, 128],
                       IDENT=[128, 128], ONESB=[128, 128], NEGROW=[1, 384],
                       ONE1=[1, 128], BAPPW=[128, 9 * 128],
                       SELPAIR=[2, 128], BD2=[128, 2],
                       ONESTK=[128, 1], PERM=[128, 128])
        for nm, shp in cshapes.items():
            dt[nm] = nc.dram_tensor(nm, shp, BF16, kind="ExternalInput")
        dt["ONESF"] = nc.dram_tensor("ONESF", [128, 128], F32, kind="ExternalInput")
        dt["MASKB"] = nc.dram_tensor("MASKB", [128, 384], F32, kind="ExternalInput")
        dt["ONES64"] = nc.dram_tensor("ONES64", [1, 64], F32, kind="ExternalInput")
        dt["wqkv_l"] = nc.dram_tensor("wqkv_l", [6, 4, NKT, 128, 768], F32, kind="ExternalInput")
        dt["wo_l"] = nc.dram_tensor("wo_l", [6, 4, NKT, 2, 128, 128], F32, kind="ExternalInput")
        dt["w1_l"] = nc.dram_tensor("w1_l", [6, NFT, NKT, 128, 256], F32, kind="ExternalInput")
        dt["w2_l"] = nc.dram_tensor("w2_l", [6, NKT, NFT, 128, 128], F32, kind="ExternalInput")
        for nm, shp in [("b1a_l", [6, 128, NFT]), ("b1g_l", [6, 128, NFT]),
                        ("b2_l", [6, 128, NKT]), ("g_l", [6, 128, NKT]),
                        ("bl_l", [6, 128, NKT]), ("qs_l", [6, 2, 128]),
                        ("ks_l", [6, 2, 128])]:
            dt[nm] = nc.dram_tensor(nm, shp, F32, kind="ExternalInput")
        dt["wqkv_t"] = nc.dram_tensor("wqkv_t", [4, NKT, 128, 768], F32, kind="ExternalInput")
        dt["wproj_t"] = nc.dram_tensor("wproj_t", [4, NKT, 2, 128, 128], F32, kind="ExternalInput")
        self.dram = dt

    def alloc_sbuf(self):
        nc = self.nc
        sb = {}

        def A(name, shape, dtype):
            h = nc.alloc_sbuf_tensor("sb_" + name, shape, dtype)
            sb[name] = h[:]
            return h

        A("Y", [128, NKT * T], F32)
        A("SBB", [128, NKT * T], BF16)
        poolb = nc.alloc_sbuf_tensor("POOLB", [128, 20992], BF16)
        pb = poolb[:]
        sb["QP"] = pb[:, 0:4096]
        sb["KP"] = pb[:, 4096:8192]
        sb["OF"] = pb[:, 8192:12288]
        sb["WA"] = pb[:, 12288:18432]
        sb["WO"] = pb[:, 18432:20480]
        sb["VPX"] = nc.alloc_sbuf_tensor("VPX", [128, 4160], BF16)[:]
        sb["GB"] = pb[:, 0:11264]
        sb["W1A"] = pb[:, 11264:13312]
        sb["W1B"] = pb[:, 13312:15360]
        sb["W2A"] = pb[:, 15360:18176]
        sb["W2B"] = pb[:, 18176:20992]
        for nm in ("CT", "STT", "CTI", "STI"):
            A(nm, [128, T], F32)
        for nm, shp, dt_ in [("B128", [128, 128], BF16), ("IDENT", [128, 128], BF16),
                             ("ONESB", [128, 128], BF16), ("NEGROW", [1, 384], BF16),
                             ("ONE1", [1, 128], BF16), ("BAPPW", [128, 9 * 128], BF16),
                             ("ONESF", [128, 128], F32),
                             ("MASKB", [128, 384], F32),
                             ("SELPAIR", [2, 128], BF16), ("BD2", [128, 2], BF16),
                             ("ONESTK", [128, 1], BF16), ("ONES64", [1, 64], F32),
                             ("PERM", [128, 128], BF16)]:
            A(nm, shp, dt_)
        for nm, shp in [("B1A", [128, NFT]), ("B1G", [128, NFT]),
                        ("B2V", [128, NKT]), ("GV", [128, NKT]),
                        ("BLV", [128, NKT]), ("QSR", [2, 128]), ("KSR", [2, 128]),
                        ("QS8", [2, 128])]:
            A(nm, shp, F32)
        A("SELQ", [2, 128], BF16)
        A("SELK", [2, 128], BF16)
        for i in range(4):
            A(f"SC{i}", [128, 512], F32)
        A("QSQ0", [128, 512], BF16)
        A("QSQ1", [128, 512], BF16)
        A("EB0", [128, 512], BF16)
        A("EB1", [128, 512], BF16)
        A("MTZ", [1, 512], BF16)
        A("MBF", [128, 16], F32)
        A("MBB0", [128, 16], BF16)
        A("MBB1", [128, 16], BF16)
        A("LB", [64, 512], F32)
        A("RL", [1, 512], F32)
        A("MU", [128, 512], F32)
        A("RSTD", [128, 512], F32)
        self.sb = sb
        assert nc.sbuf_bytes_remaining >= 0, "SBUF over budget"

    def alloc_psum(self):
        nc = self.nc
        ps = {}
        for i in range(6):
            ps[f"ps{i}"] = nc.alloc_psum_tensor(f"ps{i}", [128, 512], F32)[:]
        ps["psSTA"] = nc.alloc_psum_tensor("psSTA", [128, 512], F32)[:]
        ps["psSTB"] = nc.alloc_psum_tensor("psSTB", [128, 512], F32)[:]
        self.psum = ps

    def mm(self, out_ap, lhsT, rhs, bank, r, start, stop):
        self.P.op("PE", lambda e, o=out_ap, l=lhsT, rr=rhs, s=start, t=stop:
                  e.matmul(o, l, rr, start=s, stop=t), r=r, w=(bank,))

    def dma(self, eng, dst, src, key, r=(), w=()):
        self.P.op(eng, lambda e, d=dst, s=src: e.dma_start(d, s),
                  r=r, w=w, dma_key=key)

    def load_consts(self):
        sb, dt = self.sb, self.dram
        for s in ("B128", "IDENT", "ONESB", "NEGROW", "ONE1",
                  "BAPPW", "ONESF", "MASKB", "SELPAIR", "BD2", "ONESTK",
                  "ONES64", "PERM"):
            self.dma("SP", sb[s], dt[s][:], key="c_" + s, w=("c" + s,))
        for s, d in [("CT", "ct"), ("STT", "st"), ("CTI", "cti"), ("STI", "sti")]:
            self.dma("SP", sb[s], dt[d][:], key="c_" + s, w=("c" + s,))
        ones_ap = sb["VPX"].rearrange(
            "p (t h c) -> p t h c", t=16, h=4)[:, :, :, 64:65]
        self.P.op("DVE", lambda e, a=ones_ap: e.memset(a, 1.0),
                  r=(), w=("vpones",))

        for kt in range(NKT):
            self.dma("SP", sb["Y"][:, kt * T:(kt + 1) * T],
                     dt["xT"][128 * kt:128 * (kt + 1), :], key=f"yl{kt}",
                     w=(f"Y{kt}",))

    def load_layer_vecs(self, li):
        dt, sb, P = self.dram, self.sb, self.P
        for dn, sn, buf in [("b1a_l", "B1A", "b1a"), ("b1g_l", "B1G", "b1g"),
                            ("b2_l", "B2V", "b2v"), ("g_l", "GV", "gv"),
                            ("bl_l", "BLV", "blv")]:
            self.dma("SP", sb[sn], dt[dn][li], key="lv_" + sn, w=(buf,))
        self.dma("SP", sb["QSR"], dt["qs_l"][li], key="lv_QSR", w=("qsr",))
        self.dma("SP", sb["KSR"], dt["ks_l"][li], key="lv_KSR", w=("ksr",))
        P.op("DVE", lambda e: e.tensor_scalar_mul(sb["QS8"], sb["QSR"], 8.0),
             r=("qsr",), w=("qs8",))
        P.op("POOL", lambda e: e.tensor_mul(sb["SELQ"], sb["SELPAIR"], sb["QS8"]),
             r=("cSELPAIR", "qs8"), w=("selq",))
        P.op("POOL", lambda e: e.tensor_mul(sb["SELK"], sb["SELPAIR"], sb["KSR"]),
             r=("cSELPAIR", "ksr"), w=("selk",))

    def ln_stats(self, chunk, shifted):
        sb, ps, P = self.sb, self.psum, self.P
        P.phase = 'ln'
        c0 = 512 * chunk
        for kt in range(NKT):
            sh = shifted and kt >= 4
            edge = sh and chunk == 0
            if edge:
                P.op("DVE", lambda e, k=kt: e.tensor_copy(
                    sb["SC2"][:, 1:512], sb["Y"][:, k * T: k * T + 511]),
                    r=(f"Y{kt}",), w=("sc2",))
                P.op("DVE", lambda e: e.memset(sb["SC2"][:, 0:1], 0),
                     r=(), w=("sc2",))
                rhs = sb["SC2"]
                rkey = "sc2"
            elif sh:
                rhs = sb["Y"][:, kt * T + c0 - 1: kt * T + c0 + 511]
                rkey = f"Y{kt}"
            else:
                rhs = sb["Y"][:, kt * T + c0: kt * T + c0 + 512]
                rkey = f"Y{kt}"
            self.mm(ps["ps4"][:, 0:512], sb["ONESF"], rhs, "ps4",
                    (rkey, "cONESF"), start=(kt == 0), stop=(kt == NKT - 1))
            xq = sb["SC0"] if kt % 2 == 0 else sb["SC1"]
            xqk = "sc0" if kt % 2 == 0 else "sc1"
            P.op("ACT", lambda e, x=xq, rr=rhs: e.activation(
                x, rr, AF.Square), r=(rkey,), w=(xqk,))
            self.mm(ps["ps5"][:, 0:512], sb["ONESF"], xq,
                    "ps5", (xqk, "cONESF"), start=(kt == 0), stop=(kt == NKT - 1))
        P.op("DVE", lambda e: e.tensor_scalar_mul(
            sb["MU"], ps["ps4"][:, 0:512], 1.0 / D), r=("ps4",), w=("mu",))
        P.op("ACT", lambda e: e.activation(sb["RSTD"], sb["MU"], AF.Square),
             r=("mu",), w=("rstd",))
        P.op("DVE", lambda e: e.scalar_tensor_tensor(
            sb["RSTD"], ps["ps5"][:, 0:512], 1.0 / D, sb["RSTD"],
            ALU.mult, ALU.subtract), r=("ps5", "rstd"), w=("rstd",))
        P.op("DVE", lambda e: e.tensor_scalar_add(sb["RSTD"], sb["RSTD"], EPS),
             r=("rstd",), w=("rstd",))
        P.op("ACT", lambda e: e.activation(sb["RSTD"], sb["RSTD"], AF.Ln),
             r=("rstd",), w=("rstd",))
        P.op("ACT", lambda e: e.activation(sb["RSTD"], sb["RSTD"], AF.Exp,
                                           scale=-0.5),
             r=("rstd",), w=("rstd",))

    def ln_apply_inplace(self, chunk):
        sb, P = self.sb, self.P
        c0 = 512 * chunk
        for kt in range(NKT):
            yap = sb["Y"][:, kt * T + c0: kt * T + c0 + 512]
            P.op("DVE", lambda e, y=yap: e.tensor_sub(y, y, sb["MU"]),
                 r=(f"Y{kt}", "mu"), w=(f"Y{kt}",))
            P.op("DVE", lambda e, y=yap: e.tensor_mul(y, y, sb["RSTD"]),
                 r=(f"Y{kt}", "rstd"), w=(f"Y{kt}",))

    def ln_apply_to_sbb(self, chunk, affine, shifted):
        sb, P = self.sb, self.P
        c0 = 512 * chunk
        for kt in range(NKT):
            sh = shifted and kt >= 4
            edge = sh and chunk == 0
            dst = sb["SBB"][:, kt * T + c0: kt * T + c0 + 512]
            t1 = sb["SC2"]
            if edge:
                src = sb["Y"][:, kt * T: kt * T + 511]
                P.op("DVE", lambda e, s=src: e.tensor_sub(
                    t1[:, 1:512], s, sb["MU"][:, 1:512]),
                    r=(f"Y{kt}", "mu"), w=("sc2",))
                P.op("POOL", lambda e: e.tensor_scalar_mul(
                    t1[:, 0:1], sb["MU"][:, 0:1], -1.0), r=("mu",), w=("sc2",))
            else:
                src = sb["Y"][:, kt * T + c0 - (1 if sh else 0):
                              kt * T + c0 + 512 - (1 if sh else 0)]
                P.op("DVE", lambda e, s=src: e.tensor_sub(t1, s, sb["MU"]),
                     r=(f"Y{kt}", "mu"), w=("sc2",))
            if affine:
                P.op("DVE", lambda e: e.tensor_mul(sb["SC3"], t1, sb["RSTD"]),
                     r=("sc2", "rstd"), w=("sc3",))
                P.op("DVE", lambda e, d=dst, kk=kt: e.tensor_scalar(
                    d, sb["SC3"], sb["GV"][:, kk:kk + 1], sb["BLV"][:, kk:kk + 1],
                    ALU.mult, ALU.add),
                    r=("sc3", "gv", "blv"), w=(f"SBB{kt}",))
            else:
                P.op("DVE", lambda e, d=dst: e.tensor_mul(d, t1, sb["RSTD"]),
                     r=("sc2", "rstd"), w=(f"SBB{kt}",))

    def shift_to_sbb(self):
        sb, P = self.sb, self.P
        for kt in range(NKT):
            dst = sb["SBB"][:, kt * T: (kt + 1) * T]
            if kt < 4:
                P.op("DVE", lambda e, d=dst, k=kt: e.tensor_copy(
                    d, sb["Y"][:, k * T:(k + 1) * T]),
                    r=(f"Y{kt}",), w=(f"SBB{kt}",))
            else:
                P.op("DVE", lambda e, d=dst, k=kt: e.tensor_copy(
                    d[:, 1:T], sb["Y"][:, k * T: (k + 1) * T - 1]),
                    r=(f"Y{kt}",), w=(f"SBB{kt}",))
                P.op("DVE", lambda e, d=dst: e.memset(d[:, 0:1], 0),
                     r=(), w=(f"SBB{kt}",))

    def attn(self, li, top):
        sb, ps, dt, P = self.sb, self.psum, self.dram, self.P
        wsl = dt["wqkv_t"] if top else dt["wqkv_l"][li]
        wosl = dt["wproj_t"] if top else dt["wo_l"][li]
        for g in range(4):
            self.dma("POOL", sb["WA"].rearrange("p (kt n) -> p kt n", kt=NKT),
                     wsl[g].rearrange("kt p n -> p kt n"), key="wa", w=("pool.WA",))
            self.dma("POOL", sb["WO"].rearrange("p (kt n) -> p kt n", kt=NKT * 2),
                     wosl[g].rearrange("nt kp p n -> p (nt kp) n"),
                     key="wo", w=("pool.WO",))
            self.P.phase = 'qkv'
            pending = []
            ui = 0
            for pr in range(2):
                for part, dstbuf in ((0, "QP"), (1, "KP")):
                    for ch in range(4):
                        bank = "ps0" if ui % 2 == 0 else "ps1"
                        pb = ps[bank]
                        for kt in range(NKT):
                            lhs = sb["WA"][:, kt * 768 + 256 * part + 128 * pr:
                                           kt * 768 + 256 * part + 128 * (pr + 1)]
                            rhs = sb["SBB"][:, kt * T + 512 * ch: kt * T + 512 * ch + 512]
                            self.mm(pb[:, 0:512], lhs, rhs, bank,
                                    ("pool.WA", f"SBB{kt}"),
                                    start=(kt == 0), stop=(kt == NKT - 1))
                        if top:
                            dst = sb[dstbuf][:, pr * T + 512 * ch: pr * T + 512 * ch + 512]
                            P.op("ACT", lambda e, d=dst, p=pb: e.activation(
                                d, p[:, 0:512], AF.Copy),
                                r=(bank,), w=(f"pool.{dstbuf}{pr}",))
                        else:
                            qsbk = "sc0" if ui % 2 == 0 else "sc1"
                            qsb = sb["SC0"] if ui % 2 == 0 else sb["SC1"]
                            P.op("ACT", lambda e, q=qsb, p=pb: e.activation(
                                q, p[:, 0:512], AF.Copy), r=(bank,), w=(qsbk,))
                            pending.append((pr, part, ch, dstbuf, ui))
                            if len(pending) >= 2:
                                self.prep_qk(*pending.pop(0))
                        self.vtok_tile(ui)
                        ui += 1
            for args in pending:
                self.prep_qk(*args)
            self.P.phase = 'headsT' if top else 'heads'
            if top:
                self.heads_weave(top)
            else:
                self.heads_qmajor()
            self.P.phase = 'wo'
            for nt in range(NKT):
                for ch in range(4):
                    bank = "ps0" if (nt + ch) % 2 == 0 else "ps1"
                    pb = ps[bank]
                    for kp in range(2):
                        lhs = sb["WO"][:, (nt * 2 + kp) * 128:(nt * 2 + kp + 1) * 128]
                        rhs = sb["OF"][:, kp * T + 512 * ch: kp * T + 512 * ch + 512]
                        self.mm(pb[:, 0:512], lhs, rhs, bank,
                                ("pool.WO", f"pool.OF{kp}"),
                                start=(kp == 0), stop=(kp == 1))
                    yap = sb["Y"][:, nt * T + 512 * ch: nt * T + 512 * ch + 512]
                    P.op("DVE", lambda e, y=yap, p=pb: e.tensor_add(
                        y, y, p[:, 0:512]), r=(f"Y{nt}", bank), w=(f"Y{nt}",))

    def vtok_tile(self, tt):
        sb, ps, P = self.sb, self.psum, self.P
        P.phase = 'vtok'
        pb = ps["ps2"]
        for kt in range(NKT):
            lhs = sb["SBB"][:, kt * T + 128 * tt: kt * T + 128 * (tt + 1)]
            rhs = sb["WA"][:, kt * 768 + 512: kt * 768 + 768]
            self.mm(pb[:, 0:256], lhs, rhs, "ps2",
                    ("pool.WA", f"SBB{kt}"),
                    start=(kt == 0), stop=(kt == NKT - 1))
        dst = sb["VPX"].rearrange("p (t h c) -> p t h c", t=16, h=4)[
            :, tt, :, 0:64]
        P.op("ACT", lambda e, d=dst, p=pb: e.activation(
            d, p[:, 0:256], AF.Copy), r=("ps2",), w=(f"vpx{tt % 2}",))

    def prep_qk(self, pr, part, ch, dstbuf, ui):
        sb, ps, P = self.sb, self.psum, self.P
        P.phase = 'prep'
        pk = "ps3" if ui % 2 == 0 else "ps5"
        pchain = ps[pk]
        qsb = sb["SC0"] if ui % 2 == 0 else sb["SC1"]
        qsbk = "sc0" if ui % 2 == 0 else "sc1"
        qsq = sb["QSQ0"] if ui % 2 == 0 else sb["QSQ1"]
        qsqk = "qsq0" if ui % 2 == 0 else "qsq1"
        P.op("ACT", lambda e, q=qsq, s=qsb: e.activation(q, s, AF.Square),
             r=(qsbk,), w=(qsqk,))
        self.mm(pchain[0:2, 0:512], sb["BD2"], qsq, pk,
                (qsqk, "cBD2"), start=True, stop=True)
        P.op("DVE", lambda e, p=pchain: e.tensor_scalar_max(
            p[0:2, 0:512], p[0:2, 0:512], 1e-24), r=(pk,), w=(pk,))
        P.op("ACT", lambda e, p=pchain: e.activation(
            p[32:34, 0:512], p[0:2, 0:512], AF.Ln), r=(pk,), w=(pk,))
        rb = qsq[0:2, 0:512]
        P.op("ACT", lambda e, p=pchain, rr=rb: e.activation(
            rr, p[32:34, 0:512], AF.Exp, scale=-0.5), r=(pk,), w=(qsqk,))
        selbuf = "SELQ" if part == 0 else "SELK"
        selkey = "selq" if part == 0 else "selk"
        self.mm(pchain[:, 0:512], sb[selbuf], rb, pk,
                (qsqk, selkey), start=True, stop=True)
        slot = sb[dstbuf][:, pr * T + 512 * ch: pr * T + 512 * ch + 512]
        slotk = f"pool.{dstbuf}{pr}"
        P.op("DVE", lambda e, s=slot, q=qsb, p=pchain: e.tensor_mul(
            s, q, p[:, 0:512]), r=(qsbk, pk), w=(slotk,))
        self.mm(pchain[:, 0:512], sb["PERM"], slot, pk,
                (slotk, "cPERM"), start=True, stop=True)
        ctb = "CT" if part == 0 else "CTI"
        stb = "STT" if part == 0 else "STI"
        z0, z0k = (sb["SC2"], "sc2") if ui % 2 == 0 else (sb["SC3"], "sc3")
        tab = sb[ctb][:, 512 * ch: 512 * ch + 512]
        P.op("DVE", lambda e, s=slot, t=tab, z=z0: e.tensor_mul(z, s, t),
             r=(slotk, "c" + ctb), w=(z0k,))
        tab2 = sb[stb][:, 512 * ch: 512 * ch + 512]
        P.op("DVE", lambda e, s=slot, t=tab2, p=pchain: e.tensor_mul(
            s, p[:, 0:512], t), r=(pk, "c" + stb), w=(slotk,))
        P.op("DVE", lambda e, s=slot, z=z0: e.tensor_add(s, s, z),
             r=(slotk, z0k), w=(slotk,))

    def heads_qmajor(self):
        sb, ps, P = self.sb, self.psum, self.P
        blocks = [(hl, qb) for hl in range(4) for qb in range(16)]
        n = len(blocks)

        def s_mm(i):
            hl, qb = blocks[i]
            pr, rh = hl // 2, 64 * (hl % 2)
            w = 128 * min(qb + 1, 3)
            k0 = 128 * max(qb - 2, 0)
            bank = "ps0" if i % 2 == 0 else "ps1"
            lhs = sb["QP"][rh:rh + 64, pr * T + 128 * qb: pr * T + 128 * (qb + 1)]
            rhs = sb["KP"][rh:rh + 64, pr * T + k0: pr * T + k0 + w]
            self.mm(ps[bank][:, 0:w], lhs, rhs, bank,
                    (f"pool.QP{pr}", f"pool.KP{pr}"), start=True, stop=True)

        def chain(i):
            hl, qb = blocks[i]
            w = 128 * min(qb + 1, 3)
            par = i % 2
            bank = "ps0" if par == 0 else "ps1"
            nmc = sb["MBF"][:, 4 + par:5 + par]
            ddc = sb["MBF"][:, 6 + par:7 + par]
            rrc = sb["MBF"][:, 12 + par:13 + par]
            nmk, ddk, rrk = f"nm{par}", f"dd{par}", f"rr{par}"
            eb = sb["EB0"] if par == 0 else sb["EB1"]
            ebk = "eb0" if par == 0 else "eb1"
            P.op("DVE", lambda e, b=ps[bank], ww=w: e.tensor_add(
                b[:, 0:ww], b[:, 0:ww], sb["MASKB"][:, 384 - ww:384]),
                r=(bank, "cMASKB"), w=(bank,))
            P.op("DVE", lambda e, b=ps[bank], ww=w, nm=nmc: e.reduce_max(
                nm, b[:, 0:ww], axis=AX.X, negate=True),
                r=(bank,), w=(nmk,))
            sm = sb["SC0"] if par == 0 else sb["SC1"]
            smk = "sc0" if par == 0 else "sc1"
            P.op("ACT", lambda e, s=sm, b=ps[bank], ww=w, nm=nmc, dd=ddc:
                 e.activation(s[:, 0:ww], b[:, 0:ww], AF.Exp, bias=nm,
                              scale=1.0, accum_out=dd),
                 r=(bank, nmk), w=(smk, ddk))
            P.op("DVE", lambda e, dd=ddc, rr=rrc:
                 e.reciprocal(rr, dd), r=(ddk,), w=(rrk,))
            P.op("ACT", lambda e, d=eb, s=sm, ww=w, rr=rrc:
                 e.activation(d[:, 0:ww], s[:, 0:ww], AF.Copy, scale=rr),
                 r=(smk, rrk), w=(ebk,))

        def tav(i):
            hl, qb = blocks[i]
            pr, rh = hl // 2, 64 * (hl % 2)
            w = 128 * min(qb + 1, 3)
            nt = w // 128
            k0t = max(qb - 2, 0)
            par = i % 2
            eb = sb["EB0"] if par == 0 else sb["EB1"]
            ebk = "eb0" if par == 0 else "eb1"
            tb = "ps2" if par == 0 else "ps3"
            etb = sb["QSQ0"] if par == 0 else sb["QSQ1"]
            etk = "qsq0" if par == 0 else "qsq1"
            for t in range(nt):
                self.mm(ps[tb][:, 128 * t:128 * (t + 1)],
                        eb[:, 128 * t:128 * t + 128], sb["IDENT"], tb,
                        (ebk, "cIDENT"), start=True, stop=True)
            P.op("ACT", lambda e, d=etb, ww=w, b=ps[tb]:
                 e.activation(d[:, 0:ww], b[:, 0:ww], AF.Copy),
                 r=(tb,), w=(etk,))
            ob = "ps4" if par == 0 else "ps5"
            hloc = 2 * pr + (hl % 2)
            for t in range(nt):
                gk = k0t + t
                vslot = sb["VPX"][:, 260 * gk + 65 * hloc:
                                  260 * gk + 65 * hloc + 64]
                self.mm(ps[ob][0:64, 0:128], vslot,
                        etb[:, 128 * t:128 * t + 128], ob,
                        (etk, f"vpx{gk % 2}"), start=(t == 0),
                        stop=(t == nt - 1))
            odst = sb["OF"][rh:rh + 64, pr * T + 128 * qb: pr * T + 128 * (qb + 1)]
            P.op("ACT", lambda e, d=odst, b=ps[ob]:
                 e.activation(d, b[0:64, 0:128], AF.Copy),
                 r=(ob,), w=(f"pool.OF{pr}",))

        for i in range(n + 2):
            if i < n:
                s_mm(i)
            if 1 <= i <= n:
                chain(i - 1)
            if i >= 2:
                tav(i - 2)

    def heads_weave(self, top):
        for hl in range(5):
            hs = hl if hl < 4 else None
            ha = hl - 1 if hl > 0 else None
            for c in range(4):
                if ha is not None:
                    self.mtz_stage(ha, c)
                glo = 0 if top else max(0, 4 * c - 2)
                ntt = 4 * c + 4 - glo
                prev = None
                napp = ntt if ha is not None else 0
                nst = 4 if hs is not None else 0
                for i in range(max(napp, nst)):
                    if i < napp:
                        self.apply_tile_pe(ha, c, glo, i, top)
                        if prev is not None:
                            self.apply_tile_av(ha, c, glo, prev, ntt, top)
                        prev = i
                    if i < nst:
                        self.stats_j(hs, 4 * c + i, top)
                if prev is not None:
                    self.apply_tile_av(ha, c, glo, prev, ntt, top)
                if ha is not None and c > 0:
                    self.apply_stage2(ha, c - 1, top)
            if ha is not None:
                self.apply_stage2(ha, 3, top)

    def mtz_stage(self, hl, c):
        sb, ps, P = self.sb, self.psum, self.P
        mbb = sb["MBB0"] if hl % 2 == 0 else sb["MBB1"]
        mbk = "mbb0" if hl % 2 == 0 else "mbb1"
        for i in range(4):
            self.mm(ps["psSTB"][0:1, 128 * i:128 * (i + 1)],
                    mbb[:, 4 * c + i:4 * c + i + 1], sb["IDENT"],
                    "psSTB", (mbk, "cIDENT"), start=True, stop=True)
        P.op("ACT", lambda e: e.activation(
            sb["MTZ"], ps["psSTB"][0:1, 0:512], AF.Copy),
            r=("psSTB",), w=("mtz",))

    def apply_tile_pe(self, hl, c, glo, tt, top):
        sb, ps, P = self.sb, self.psum, self.P
        pr = hl // 2
        rh = 64 * (hl % 2)
        qk = f"pool.QP{pr}"
        kk = f"pool.KP{pr}"
        g = glo + tt
        kqb = "ps2" if tt % 2 == 0 else "ps4"
        pkq = ps[kqb]
        lhsk = sb["KP"][rh:rh + 64, pr * T + 128 * g: pr * T + 128 * (g + 1)]
        rhsq = sb["QP"][rh:rh + 64, pr * T + 512 * c: pr * T + 512 * c + 512]
        self.mm(pkq[:, 0:512], lhsk, rhsq, kqb, (qk, kk),
                start=True, stop=False)
        if top:
            dd = g - 4 * c
            if dd > 0:
                self.mm(pkq[:, 0:128 * dd], sb["ONE1"],
                        sb["NEGROW"][0:1, 0:128 * dd], kqb,
                        ("cONE1", "cNEGROW"), start=False, stop=False)
            if 0 <= dd <= 3:
                self.mm(pkq[:, 128 * dd:128 * dd + 128], sb["IDENT"],
                        sb["BAPPW"][:, 128 * 3:128 * 4], kqb,
                        ("cIDENT", "cBAPPW"), start=False, stop=False)
        else:
            s = 3 - g + 4 * c
            self.mm(pkq[:, 0:512], sb["IDENT"],
                    sb["BAPPW"][:, 128 * s:128 * s + 512], kqb,
                    ("cIDENT", "cBAPPW"), start=False, stop=False)
        self.mm(pkq[:, 0:512], sb["ONE1"], sb["MTZ"], kqb,
                ("cONE1", "mtz"), start=False, stop=True)
        ebuf = sb["EB0"] if tt % 2 == 0 else sb["EB1"]
        ebk = "eb0" if tt % 2 == 0 else "eb1"
        P.op("ACT", lambda e, eb=ebuf, p=pkq, sc=(0.125 if top else 1.0):
             e.activation(eb, p[:, 0:512], AF.Exp, scale=sc),
             r=(kqb,), w=(ebk,))

    def apply_tile_av(self, hl, c, glo, tt, ntt, top):
        sb, ps, P = self.sb, self.psum, self.P
        pr = hl // 2
        rh = 64 * (hl % 2)
        g = glo + tt
        avb = "ps3" if c % 2 == 0 else "ps5"
        pav = ps[avb]
        ebuf = sb["EB0"] if tt % 2 == 0 else sb["EB1"]
        ebk = "eb0" if tt % 2 == 0 else "eb1"
        hloc = 2 * pr + (hl % 2)
        vslot = sb["VPX"][:, 260 * g + 65 * hloc: 260 * g + 65 * hloc + 65]
        self.mm(pav[0:65, 0:512], vslot, ebuf, avb,
                (ebk, f"vpx{g % 2}", "vpones"),
                start=(tt == 0), stop=(tt == ntt - 1))

    def stats_j(self, hl, j, top):
        sb, ps, P = self.sb, self.psum, self.P
        mbb = sb["MBB0"] if hl % 2 == 0 else sb["MBB1"]
        mbk = "mbb0" if hl % 2 == 0 else "mbb1"
        pr = hl // 2
        rh = 64 * (hl % 2)
        qk = f"pool.QP{pr}"
        kk = f"pool.KP{pr}"
        stk = "psSTA" if j % 2 == 0 else "ps0"
        pst = ps[stk]
        tk_lo = 0 if top else max(0, 128 * (j - 2))
        tk_hi = 128 * (j + 1)
        wj = tk_hi - tk_lo
        lhs = sb["QP"][rh:rh + 64, pr * T + 128 * j: pr * T + 128 * (j + 1)]
        if top:
            nch = (wj + 511) // 512
            for cc in range(nch):
                a = tk_lo + 512 * cc
                b = min(tk_hi, a + 512)
                rhs = sb["KP"][rh:rh + 64, pr * T + a: pr * T + b]
                diag = (b == tk_hi)
                self.mm(pst[:, 0:(b - a)], lhs, rhs, stk,
                        (qk, kk), start=True, stop=not diag)
                if diag:
                    off = (b - a) - 128
                    self.mm(pst[:, off:off + 128], sb["IDENT"],
                            sb["B128"], stk, ("cIDENT", "cB128"),
                            start=False, stop=True)
                red_dst = sb["MBF"][:, cc:cc + 1]
                P.op("DVE", lambda e, rd=red_dst, n=(b - a):
                     e.reduce_max(rd, pst[:, 0:n],
                                  axis=AX.X), r=(stk,), w=("mbf",))
            P.op("POOL", lambda e, n=nch: e.tensor_copy(
                sb["MBF"][:, 8:8 + n], sb["MBF"][:, 0:n]),
                r=("mbf",), w=("mbf2",))
            P.op("DVE", lambda e, jj=j, n=nch, m=mbb: e.reduce_max(
                m[:, jj:jj + 1], sb["MBF"][:, 8:8 + n], axis=AX.X,
                negate=True), r=("mbf2",), w=(mbk,))
        else:
            rhs = sb["KP"][rh:rh + 64, pr * T + tk_lo: pr * T + tk_hi]
            self.mm(pst[:, 0:wj], lhs, rhs, stk, (qk, kk),
                    start=True, stop=False)
            mask = {128: "B128", 256: "B256", 384: "B384"}[wj]
            self.mm(pst[:, 0:wj], sb["IDENT"], sb[mask],
                    stk, ("cIDENT", "c" + mask), start=False, stop=True)
            P.op("DVE", lambda e, jj=j, n=wj, m=mbb: e.reduce_max(
                m[:, jj:jj + 1], pst[:, 0:n], axis=AX.X,
                negate=True), r=(stk,), w=(mbk,))

    def apply_stage2(self, hl, c, top):
        sb, ps, P = self.sb, self.psum, self.P
        pr = hl // 2
        rh = 64 * (hl % 2)
        avb = "ps3" if c % 2 == 0 else "ps5"
        pav = ps[avb]
        P.op("ACT", lambda e, p=pav: e.activation(
            ps["psSTB"][32:33, 0:512], p[64:65, 0:512], AF.Ln),
            r=(avb,), w=("psSTB",))
        P.op("ACT", lambda e: e.activation(
            sb["RL"], ps["psSTB"][32:33, 0:512], AF.Exp, scale=-1.0),
            r=("psSTB",), w=("rl",))
        self.mm(ps["psSTB"][64:128, 0:512], sb["ONES64"], sb["RL"], "psSTB",
                ("rl", "cONES64"), start=True, stop=True)
        P.op("ACT", lambda e: e.activation(
            sb["LB"], ps["psSTB"][64:128, 0:512], AF.Copy),
            r=("psSTB",), w=("lb",))
        odst = sb["OF"][rh:rh + 64, pr * T + 512 * c: pr * T + 512 * c + 512]
        P.op("DVE", lambda e, o=odst, p=pav: e.tensor_mul(
            o, p[0:64, 0:512], sb["LB"]),
            r=(avb, "lb"), w=(f"pool.OF{pr}",))

    def ffn(self, li):
        sb, ps, dt, P = self.sb, self.psum, self.dram, self.P
        for ch in range(4):
            self.ln_stats(ch, shifted=True)
            self.ln_apply_to_sbb(ch, affine=True, shifted=True)
        for ch in range(4):
            self.P.phase = 'w1'
            for j in range(NFT):
                wbuf = sb["W1A"] if j % 2 == 0 else sb["W1B"]
                wkey = "pool.W1A" if j % 2 == 0 else "pool.W1B"
                self.dma("POOL", wbuf.rearrange("p (kt n) -> p kt n", kt=NKT),
                         dt["w1_l"][li, j].rearrange("kt p n -> p kt n"),
                         key="w1" + wkey[-1], w=(wkey,))
                ab = "ps0" if j % 2 == 0 else "ps2"
                gb = "ps1" if j % 2 == 0 else "ps3"
                for kt in range(NKT):
                    rhs = sb["SBB"][:, kt * T + 512 * ch: kt * T + 512 * ch + 512]
                    self.mm(ps[ab][:, 0:512], wbuf[:, kt * 256: kt * 256 + 128],
                            rhs, ab, (wkey, f"SBB{kt}"),
                            start=(kt == 0), stop=(kt == NKT - 1))
                    self.mm(ps[gb][:, 0:512], wbuf[:, kt * 256 + 128: kt * 256 + 256],
                            rhs, gb, (wkey, f"SBB{kt}"),
                            start=(kt == 0), stop=(kt == NKT - 1))
                gl = sb["SC0"] if j % 2 == 0 else sb["SC1"]
                glk = "sc0" if j % 2 == 0 else "sc1"
                P.op("ACT", lambda e, g=gl, jj=j, pg=ps[gb]: e.activation(
                    g, pg[:, 0:512], AF.Gelu,
                    bias=sb["B1G"][:, jj:jj + 1]), r=(gb, "b1g"), w=(glk,))
                gdst = sb["GB"][:, 512 * j: 512 * (j + 1)]
                P.op("DVE", lambda e, g=gl, d=gdst, jj=j, pa=ps[ab]:
                     e.scalar_tensor_tensor(
                    d, pa[:, 0:512], sb["B1A"][:, jj:jj + 1], g,
                    ALU.add, ALU.mult), r=(ab, glk, "b1a"),
                    w=(f"pool.GB{j % 2}",))
            self.P.phase = 'w2'
            for nt in range(NKT):
                wbuf = sb["W2A"] if nt % 2 == 0 else sb["W2B"]
                wkey = "pool.W2A" if nt % 2 == 0 else "pool.W2B"
                self.dma("POOL", wbuf.rearrange("p (kt n) -> p kt n", kt=NFT),
                         dt["w2_l"][li, nt].rearrange("kt p n -> p kt n"),
                         key="w2" + wkey[-1], w=(wkey,))
                ob = "ps4" if nt % 2 == 0 else "ps5"
                for kt in range(NFT):
                    self.mm(ps[ob][:, 0:512], wbuf[:, kt * 128:(kt + 1) * 128],
                            sb["GB"][:, 512 * kt: 512 * (kt + 1)], ob,
                            (wkey, f"pool.GB{kt % 2}"),
                            start=(kt == 0), stop=(kt == NFT - 1))
                yap = sb["Y"][:, nt * T + 512 * ch: nt * T + 512 * ch + 512]
                P.op("DVE", lambda e, y=yap, n=nt, po=ps[ob]:
                     e.scalar_tensor_tensor(
                    y, po[:, 0:512], sb["B2V"][:, n:n + 1], y,
                    ALU.add, ALU.add), r=(ob, "b2v", f"Y{nt}"), w=(f"Y{nt}",))

    def merge_with_input(self, dram_src, factor, to_out=False):
        sb, dt, P = self.sb, self.dram, self.P
        P.phase = 'merge'
        for kt in range(NKT):
            for ch in range(4):
                par = (kt * 4 + ch) % 2
                xb = sb["SC0"] if par == 0 else sb["SC1"]
                xbk = "sc0" if par == 0 else "sc1"
                self.dma("SP", xb,
                         dram_src[128 * kt:128 * (kt + 1), 512 * ch:512 * ch + 512],
                         key=xbk, r=(), w=(xbk,))
                yap = sb["Y"][:, kt * T + 512 * ch: kt * T + 512 * ch + 512]
                if to_out:
                    ob = sb["SC2"] if par == 0 else sb["SC3"]
                    obk = "sc2" if par == 0 else "sc3"
                    P.op("DVE", lambda e, y=yap, x=xb, o=ob, f=factor:
                         e.scalar_tensor_tensor(o, y, f, x, ALU.mult, ALU.add),
                         r=(f"Y{kt}", xbk), w=(obk,))
                    self.dma("SP", dt["out"][128 * kt:128 * (kt + 1),
                                             512 * ch:512 * ch + 512],
                             ob, key="outst", r=(obk,), w=())
                else:
                    P.op("DVE", lambda e, y=yap, x=xb, f=factor:
                         e.scalar_tensor_tensor(y, y, f, x, ALU.mult, ALU.add),
                         r=(f"Y{kt}", xbk), w=(f"Y{kt}",))

    def store_y_to_out(self):
        sb, dt = self.sb, self.dram
        for kt in range(NKT):
            self.dma("SP", dt["out"][128 * kt:128 * (kt + 1), :],
                     sb["Y"][:, kt * T:(kt + 1) * T], key="outst",
                     r=(f"Y{kt}",), w=())

    def ffn6(self, n_sub):
        for i in range((n_sub + 1) // 2):
            self.load_layer_vecs(i)
            self.P.fence_pool()
            self.shift_to_sbb()
            self.attn(i, top=False)
            if 2 * i + 2 > n_sub:
                return
            self.P.fence_pool()
            self.ffn(i)

    def build(self):
        cfg = self.cfg
        self.load_consts()
        if cfg.get("ln0", True):
            for ch in range(4):
                self.ln_stats(ch, shifted=False)
                self.ln_apply_inplace(ch)
        self.ffn6(cfg.get("n_sub1", 12))
        if cfg.get("merge1", True):
            self.merge_with_input(self.dram["xT"], 0.5)
        if cfg.get("top", True):
            for ch in range(4):
                self.ln_stats(ch, shifted=False)
                self.ln_apply_to_sbb(ch, affine=False, shifted=False)
            self.P.fence_pool()
            self.attn(0, top=True)
        if cfg.get("call2", True):
            for kt in range(NKT):
                self.dma("SP", self.dram["x2scr"][128 * kt:128 * (kt + 1), :],
                         self.sb["Y"][:, kt * T:(kt + 1) * T], key="x2st",
                         r=(f"Y{kt}",), w=())
            for ch in range(4):
                self.ln_stats(ch, shifted=False)
                self.ln_apply_inplace(ch)
            self.ffn6(cfg.get("n_sub2", 12))
            self.merge_with_input(self.dram["x2scr"], 0.5, to_out=True)
        elif cfg.get("dump") == "of":
            for kp in range(2):
                self.dma("POOL", self.dram["out"][128 * kp:128 * (kp + 1), :],
                         self.sb["OF"][:, kp * T:(kp + 1) * T], key="outst",
                         r=(f"pool.OF{kp}",), w=())
        elif cfg.get("dump") == "sbb":
            for kt in range(NKT):
                self.dma("POOL", self.dram["out"][128 * kt:128 * (kt + 1), :],
                         self.sb["SBB"][:, kt * T:(kt + 1) * T], key="outst",
                         r=(f"SBB{kt}",), w=())
        else:
            self.store_y_to_out()


_LAST_PROG = None


def build_program(cfg=None):
    global _LAST_PROG
    cfg = cfg or {}
    nc = bass.Bass("TRN2", target_bir_lowering=False,
                   detect_race_conditions=False)
    mk = MK(nc, cfg)
    _LAST_PROG = mk.P
    mk.build()
    counters = mk.P.resolve()
    sems = {k: nc.alloc_semaphore(name=f"s{i}")
            for i, k in enumerate(counters)}
    with nc.allow_low_precision(reason="bf16 matmul operands / exp tiles"), \
            nc.Block() as block:
        @block.tensor
        def _(tensor):
            mk.P.emit("PE", tensor, sems)

        @block.scalar
        def _(scalar):
            mk.P.emit("ACT", scalar, sems)

        @block.vector
        def _(vector):
            mk.P.emit("DVE", vector, sems)

        @block.gpsimd
        def _(gpsimd):
            mk.P.emit("POOL", gpsimd, sems)

        @block.sync
        def _(sync):
            mk.P.emit("SP", sync, sems)
            if "outst" in counters:
                sync.wait_ge(sems["outst"], counters["outst"])
    return nc


def stage_all(inputs):
    tabs = host_tables()
    consts = host_consts()
    staged = stage_weights(inputs)
    shared = {}
    shared.update({k: tabs[k] for k in ("ct", "st", "cti", "sti")})
    shared.update(consts)
    shared.update(staged)
    return shared


def kernel(**inputs):
    nc = build_program({})
    shared = stage_all(inputs)
    x = np.asarray(inputs["L0_attn_x"])
    in_maps = []
    for b in range(8):
        m = dict(shared)
        m["xT"] = np.ascontiguousarray(x[b].T)
        in_maps.append(m)
    res = run_bass_kernel_spmd(nc, in_maps, core_ids=list(range(8)))
    out = np.stack([np.asarray(res.results[b]["out"]).T for b in range(8)])
    return out.astype(np.float32)


# version 10
# speedup vs baseline: 1.5706x; 1.0586x over previous
import numpy as np
import ml_dtypes
import concourse.bass as bass
import concourse.mybir as mybir
from concourse.bass_utils import run_bass_kernel_spmd

F32 = mybir.dt.float32
BF16 = mybir.dt.bfloat16
AF = mybir.ActivationFunctionType
ALU = mybir.AluOpType
AX = mybir.AxisListType

D = 1024
T = 2048
NKT = 8
NH = 16
NJ = 16
DIP = 2816
NFT = 22
NEG = -3.0e38
EPS = 1e-5

ENGINES = ("PE", "ACT", "DVE", "POOL", "SP")


class Op:
    __slots__ = ("eng", "fn", "reads", "writes", "dma_key", "deps", "idx",
                 "sem_key", "inc_val", "clock", "needed", "waits", "phase")

    def __init__(self, eng, fn, reads, writes, dma_key):
        self.eng = eng
        self.fn = fn
        self.reads = reads
        self.writes = writes
        self.dma_key = dma_key
        self.deps = []
        self.sem_key = None
        self.inc_val = None
        self.clock = None
        self.needed = False
        self.waits = ()


class Prog:
    def __init__(self):
        self.ops = []
        self.last_w = {}
        self.readers = {}
        self.pending_fence = None
        self.phase = "init"
        self.inst_phase = {}

    def fence_pool(self):
        per_eng = {}
        for key in [k for k in self.last_w if k.startswith("pool.")]:
            cands = [self.last_w[key]] + self.readers.get(key, [])
            for o in cands:
                cur = per_eng.get(o.eng)
                if cur is None or o.idx > cur.idx:
                    per_eng[o.eng] = o
            del self.last_w[key]
            self.readers.pop(key, None)
        self.pending_fence = list(per_eng.values())

    def op(self, eng, fn, r=(), w=(), dma_key=None):
        o = Op(eng, fn, tuple(r), tuple(w), dma_key)
        o.phase = self.phase
        o.idx = len(self.ops)
        deps = {}
        for b in o.reads:
            if b in self.last_w:
                d = self.last_w[b]
                deps[d.idx] = d
        for b in o.writes:
            if b in self.last_w:
                d = self.last_w[b]
                deps[d.idx] = d
            for rd in self.readers.get(b, ()):
                deps[rd.idx] = rd
        if self.pending_fence and any(b.startswith("pool.") for b in o.writes):
            for d in self.pending_fence:
                deps[d.idx] = d
        o.deps = list(deps.values())
        for b in o.reads:
            self.readers.setdefault(b, []).append(o)
        for b in o.writes:
            self.last_w[b] = o
            self.readers[b] = []
        self.ops.append(o)
        return o

    def resolve(self):
        for o in self.ops:
            for d in o.deps:
                if d.dma_key is not None or d.eng != o.eng:
                    d.needed = True
            if o.dma_key is not None:
                o.needed = True
        counters = {}
        for o in self.ops:
            if not o.needed:
                continue
            key = o.dma_key if o.dma_key is not None else o.eng
            step = 16 if o.dma_key is not None else 1
            counters[key] = counters.get(key, 0) + step
            o.sem_key = key
            o.inc_val = counters[key]
        clocks = {e: {} for e in ENGINES}

        def merge(dst, src):
            for k, v in src.items():
                if dst.get(k, 0) < v:
                    dst[k] = v

        for o in self.ops:
            ck = clocks[o.eng]
            waits = []
            for d in o.deps:
                if d.sem_key is None:
                    continue
                if d.dma_key is None and d.eng == o.eng:
                    continue
                if ck.get(d.sem_key, 0) < d.inc_val:
                    waits.append((d.sem_key, d.inc_val))
                    if d.clock:
                        merge(ck, d.clock)
                    if ck.get(d.sem_key, 0) < d.inc_val:
                        ck[d.sem_key] = d.inc_val
            o.waits = tuple(waits)
            if o.sem_key is not None:
                o.clock = dict(ck)
                o.clock[o.sem_key] = o.inc_val
        self.counters = counters
        return counters

    def emit(self, eng_name, eng_obj, sems):
        for o in self.ops:
            if o.eng != eng_name:
                continue
            for key, val in o.waits:
                eng_obj.wait_ge(sems[key], val)
            inst = o.fn(eng_obj)
            try:
                self.inst_phase[inst.ins.name] = o.phase
            except Exception:
                pass
            if o.sem_key is not None:
                inst.then_inc(sems[o.sem_key], 16 if o.dma_key else 1)


def _bf(a):
    return np.ascontiguousarray(np.asarray(a, dtype=np.float32)).astype(ml_dtypes.bfloat16)


def _f32(a):
    return np.ascontiguousarray(np.asarray(a), dtype=np.float32)


def host_tables():
    HD = 64
    half = np.arange(0, HD, 2, dtype=np.float64)
    inv_freq = 1.0 / 10000.0 ** (half / HD)
    t = np.arange(T, dtype=np.float64)
    freqs = np.concatenate([t[:, None] * inv_freq[None, :]] * 2, -1)
    base = (half + 0.4 * HD) / (1.4 * HD)
    power = (t - T // 2) / 128.0
    scale = np.concatenate([base[None, :] ** power[:, None]] * 2, -1)
    out = {}
    for nm, tab in [("ct", np.cos(freqs) * scale), ("st", np.sin(freqs) * scale),
                    ("cti", np.cos(freqs) / scale), ("sti", np.sin(freqs) / scale)]:
        tt = tab.T
        out[nm] = _f32(np.concatenate([tt, tt], axis=0))
    return out


def host_consts():
    c = {}
    r = np.arange(128)
    k = r[:, None]
    c384 = np.arange(384)[None, :]
    c["B384"] = _bf(NEG * ((c384 < k) | (c384 > k + 256)))
    c256 = np.arange(256)[None, :]
    c["B256"] = _bf(NEG * (c256 > k + 128))
    c128 = np.arange(128)[None, :]
    c["B128"] = _bf(NEG * (c128 > k))
    c["IDENT"] = _bf(np.eye(128))
    c["ONESF"] = _f32(np.ones((128, 128)))
    c["ONESB"] = _bf(np.ones((128, 128)))
    c["NEGROW"] = _bf(np.full((1, 384), NEG))
    c["ONE1"] = _bf(np.ones((1, 128)))
    bw = np.zeros((128, 9 * 128), np.float32)
    for u in range(9):
        rel = 5 - u
        blk = bw[:, 128 * u:128 * (u + 1)]
        if rel == 0:
            blk[:] = NEG * (k < r[None, :])
        elif rel == 1:
            pass
        elif rel == 2:
            blk[:] = NEG * (k > r[None, :])
        else:
            blk[:] = NEG
    c["BAPPW"] = _bf(bw)
    mb_ = np.zeros((128, 384), np.float32)
    c384b = np.arange(384)[None, :]
    mb_[(c384b < k) | (c384b > k + 256)] = NEG
    c["MASKB"] = _f32(mb_)
    sp = np.zeros((2, 128), np.float32)
    sp[0, :64] = 1.0
    sp[1, 64:] = 1.0
    c["SELPAIR"] = _bf(sp)
    bd = np.zeros((128, 2), np.float32)
    bd[:64, 0] = 1.0
    bd[64:, 1] = 1.0
    c["BD2"] = _bf(bd)
    c["ONESTK"] = _bf(np.ones((128, 1)))
    c["ONES64"] = _f32(np.ones((1, 64)))
    p = np.zeros((64, 64), np.float32)
    for i in range(32):
        p[i + 32, i] = -1.0
        p[i, i + 32] = 1.0
    pp = np.zeros((128, 128), np.float32)
    pp[:64, :64] = p
    pp[64:, 64:] = p
    c["PERM"] = _bf(pp)
    return c


def stage_weights(inp):
    st = {}
    L = 6
    wqkv = _f32(inp["L0_ffn_Wqkv"])
    a = wqkv.reshape(L, NKT, 128, 24, 128)
    slab = np.empty((L, 4, NKT, 128, 768), np.float32)
    for g in range(4):
        for part in range(3):
            for pr in range(2):
                nt = part * 8 + 2 * g + pr
                slab[:, g, :, :, 256 * part + 128 * pr:256 * part + 128 * (pr + 1)] = \
                    a[:, :, :, nt, :]
    st["wqkv_l"] = np.ascontiguousarray(slab)
    wo = _f32(inp["L0_ffn_Wo"]).reshape(L, NKT, 128, NKT, 128)
    wos = np.empty((L, 4, NKT, 2, 128, 128), np.float32)
    for g in range(4):
        for kp in range(2):
            wos[:, g, :, kp] = wo[:, 2 * g + kp].transpose(0, 2, 1, 3)
    st["wo_l"] = np.ascontiguousarray(wos)
    W1 = _f32(inp["L0_ffn_W1"])
    z = np.zeros((L, D, DIP - 2730), np.float32)
    W1a = np.concatenate([W1[:, :, :2730], z], axis=2).reshape(L, D, NFT, 128)
    W1g = np.concatenate([W1[:, :, 2730:], z], axis=2).reshape(L, D, NFT, 128)
    w1s = np.empty((L, NFT, NKT, 128, 256), np.float32)
    for j in range(NFT):
        w1s[:, j, :, :, 0:128] = W1a[:, :, j].reshape(L, NKT, 128, 128)
        w1s[:, j, :, :, 128:256] = W1g[:, :, j].reshape(L, NKT, 128, 128)
    st["w1_l"] = np.ascontiguousarray(w1s)
    b1 = _f32(inp["L0_ffn_b1"])
    zb = np.zeros((L, DIP - 2730), np.float32)
    b1a = np.concatenate([b1[:, :2730], zb], 1).reshape(L, NFT, 128)
    b1g = np.concatenate([b1[:, 2730:], zb], 1).reshape(L, NFT, 128)
    st["b1a_l"] = np.ascontiguousarray(b1a.transpose(0, 2, 1))
    st["b1g_l"] = np.ascontiguousarray(b1g.transpose(0, 2, 1))
    W2 = _f32(inp["L0_ffn_W2"])
    W2p = np.concatenate([W2, np.zeros((L, DIP - 2730, D), np.float32)], 1)
    st["w2_l"] = np.ascontiguousarray(
        W2p.reshape(L, NFT, 128, NKT, 128).transpose(0, 3, 1, 2, 4))
    st["b2_l"] = np.ascontiguousarray(
        _f32(inp["L0_ffn_b2"]).reshape(L, NKT, 128).transpose(0, 2, 1))
    st["g_l"] = np.ascontiguousarray(
        _f32(inp["L0_ffn_ln_g"]).reshape(L, NKT, 128).transpose(0, 2, 1))
    st["bl_l"] = np.ascontiguousarray(
        _f32(inp["L0_ffn_ln_b"]).reshape(L, NKT, 128).transpose(0, 2, 1))
    qs = _f32(inp["L0_ffn_q_scale"])
    ks = _f32(inp["L0_ffn_k_scale"])
    st["qs_l"] = np.ascontiguousarray(
        np.tile(qs, (1, 2))[:, None, :].repeat(2, 1))
    st["ks_l"] = np.ascontiguousarray(
        np.tile(ks, (1, 2))[:, None, :].repeat(2, 1))
    wq = _f32(inp["L0_attn_w_qkv"]).reshape(NKT, 128, 24, 128)
    slabt = np.empty((4, NKT, 128, 768), np.float32)
    for g in range(4):
        for part in range(3):
            for pr in range(2):
                nt = part * 8 + 2 * g + pr
                slabt[g, :, :, 256 * part + 128 * pr:256 * part + 128 * (pr + 1)] = \
                    wq[:, :, nt, :]
    st["wqkv_t"] = np.ascontiguousarray(slabt)
    wp = _f32(inp["L0_attn_w_proj"]).reshape(NKT, 128, NKT, 128)
    wps = np.empty((4, NKT, 2, 128, 128), np.float32)
    for g in range(4):
        for kp in range(2):
            wps[g, :, kp] = wp[2 * g + kp].transpose(1, 0, 2)
    st["wproj_t"] = np.ascontiguousarray(wps)
    return st


class MK:
    def __init__(self, nc, cfg):
        self.nc = nc
        self.cfg = cfg
        self.P = Prog()
        self.declare_dram()
        self.alloc_sbuf()
        self.alloc_psum()

    def declare_dram(self):
        nc = self.nc
        dt = {}
        dt["xT"] = nc.dram_tensor("xT", [D, T], F32, kind="ExternalInput")
        dt["out"] = nc.dram_tensor("out", [D, T], F32, kind="ExternalOutput")
        dt["x2scr"] = nc.dram_tensor("x2scr", [D, T], F32, kind="Internal")
        for nm in ("ct", "st", "cti", "sti"):
            dt[nm] = nc.dram_tensor(nm, [128, T], F32, kind="ExternalInput")
        cshapes = dict(B384=[128, 384], B256=[128, 256], B128=[128, 128],
                       IDENT=[128, 128], ONESB=[128, 128], NEGROW=[1, 384],
                       ONE1=[1, 128], BAPPW=[128, 9 * 128],
                       SELPAIR=[2, 128], BD2=[128, 2],
                       ONESTK=[128, 1], PERM=[128, 128])
        for nm, shp in cshapes.items():
            dt[nm] = nc.dram_tensor(nm, shp, BF16, kind="ExternalInput")
        dt["ONESF"] = nc.dram_tensor("ONESF", [128, 128], F32, kind="ExternalInput")
        dt["MASKB"] = nc.dram_tensor("MASKB", [128, 384], F32, kind="ExternalInput")
        dt["ONES64"] = nc.dram_tensor("ONES64", [1, 64], F32, kind="ExternalInput")
        dt["wqkv_l"] = nc.dram_tensor("wqkv_l", [6, 4, NKT, 128, 768], F32, kind="ExternalInput")
        dt["wo_l"] = nc.dram_tensor("wo_l", [6, 4, NKT, 2, 128, 128], F32, kind="ExternalInput")
        dt["w1_l"] = nc.dram_tensor("w1_l", [6, NFT, NKT, 128, 256], F32, kind="ExternalInput")
        dt["w2_l"] = nc.dram_tensor("w2_l", [6, NKT, NFT, 128, 128], F32, kind="ExternalInput")
        for nm, shp in [("b1a_l", [6, 128, NFT]), ("b1g_l", [6, 128, NFT]),
                        ("b2_l", [6, 128, NKT]), ("g_l", [6, 128, NKT]),
                        ("bl_l", [6, 128, NKT]), ("qs_l", [6, 2, 128]),
                        ("ks_l", [6, 2, 128])]:
            dt[nm] = nc.dram_tensor(nm, shp, F32, kind="ExternalInput")
        dt["wqkv_t"] = nc.dram_tensor("wqkv_t", [4, NKT, 128, 768], F32, kind="ExternalInput")
        dt["wproj_t"] = nc.dram_tensor("wproj_t", [4, NKT, 2, 128, 128], F32, kind="ExternalInput")
        self.dram = dt

    def alloc_sbuf(self):
        nc = self.nc
        sb = {}

        def A(name, shape, dtype):
            h = nc.alloc_sbuf_tensor("sb_" + name, shape, dtype)
            sb[name] = h[:]
            return h

        A("Y", [128, NKT * T], F32)
        A("SBB", [128, NKT * T], BF16)
        poolb = nc.alloc_sbuf_tensor("POOLB", [128, 20992], BF16)
        pb = poolb[:]
        sb["QP"] = pb[:, 0:4096]
        sb["KP"] = pb[:, 4096:8192]
        sb["OF"] = pb[:, 8192:12288]
        sb["WA"] = pb[:, 12288:18432]
        sb["WO"] = pb[:, 18432:20480]
        sb["VPX"] = nc.alloc_sbuf_tensor("VPX", [128, 4160], BF16)[:]
        sb["GB"] = pb[:, 0:11264]
        sb["W1A"] = pb[:, 11264:13312]
        sb["W1B"] = pb[:, 13312:15360]
        sb["W2A"] = pb[:, 15360:18176]
        sb["W2B"] = pb[:, 18176:20992]
        for nm in ("CT", "STT", "CTI", "STI"):
            A(nm, [128, T], F32)
        for nm, shp, dt_ in [("B128", [128, 128], BF16), ("IDENT", [128, 128], BF16),
                             ("ONESB", [128, 128], BF16), ("NEGROW", [1, 384], BF16),
                             ("ONE1", [1, 128], BF16), ("BAPPW", [128, 9 * 128], BF16),
                             ("ONESF", [128, 128], F32),
                             ("MASKB", [128, 384], F32),
                             ("SELPAIR", [2, 128], BF16), ("BD2", [128, 2], BF16),
                             ("ONESTK", [128, 1], BF16), ("ONES64", [1, 64], F32),
                             ("PERM", [128, 128], BF16)]:
            A(nm, shp, dt_)
        for nm, shp in [("B1A", [128, NFT]), ("B1G", [128, NFT]),
                        ("B2V", [128, NKT]), ("GV", [128, NKT]),
                        ("BLV", [128, NKT]), ("QSR", [2, 128]), ("KSR", [2, 128]),
                        ("QS8", [2, 128])]:
            A(nm, shp, F32)
        A("SELQ", [2, 128], BF16)
        A("SELK", [2, 128], BF16)
        for i in range(4):
            A(f"SC{i}", [128, 512], F32)
        A("QSQ0", [128, 512], BF16)
        A("QSQ1", [128, 512], BF16)
        A("EB0", [128, 512], BF16)
        A("EB1", [128, 512], BF16)
        A("MTZ", [1, 512], BF16)
        A("MBF", [128, 16], F32)
        A("MBB0", [128, 16], BF16)
        A("MBB1", [128, 16], BF16)
        A("LB", [64, 512], F32)
        A("RL", [1, 512], F32)
        A("MU", [128, 512], F32)
        A("RSTD", [128, 512], F32)
        self.sb = sb
        assert nc.sbuf_bytes_remaining >= 0, "SBUF over budget"

    def alloc_psum(self):
        nc = self.nc
        ps = {}
        for i in range(6):
            ps[f"ps{i}"] = nc.alloc_psum_tensor(f"ps{i}", [128, 512], F32)[:]
        ps["psSTA"] = nc.alloc_psum_tensor("psSTA", [128, 512], F32)[:]
        ps["psSTB"] = nc.alloc_psum_tensor("psSTB", [128, 512], F32)[:]
        self.psum = ps

    def mm(self, out_ap, lhsT, rhs, bank, r, start, stop):
        self.P.op("PE", lambda e, o=out_ap, l=lhsT, rr=rhs, s=start, t=stop:
                  e.matmul(o, l, rr, start=s, stop=t), r=r, w=(bank,))

    def dma(self, eng, dst, src, key, r=(), w=()):
        self.P.op(eng, lambda e, d=dst, s=src: e.dma_start(d, s),
                  r=r, w=w, dma_key=key)

    def load_consts(self):
        sb, dt = self.sb, self.dram
        for s in ("B128", "IDENT", "ONESB", "NEGROW", "ONE1",
                  "BAPPW", "ONESF", "MASKB", "SELPAIR", "BD2", "ONESTK",
                  "ONES64", "PERM"):
            self.dma("SP", sb[s], dt[s][:], key="c_" + s, w=("c" + s,))
        for s, d in [("CT", "ct"), ("STT", "st"), ("CTI", "cti"), ("STI", "sti")]:
            self.dma("SP", sb[s], dt[d][:], key="c_" + s, w=("c" + s,))
        ones_ap = sb["VPX"].rearrange(
            "p (t h c) -> p t h c", t=16, h=4)[:, :, :, 64:65]
        self.P.op("DVE", lambda e, a=ones_ap: e.memset(a, 1.0),
                  r=(), w=("vpones",))

        for kt in range(NKT):
            self.dma("SP", sb["Y"][:, kt * T:(kt + 1) * T],
                     dt["xT"][128 * kt:128 * (kt + 1), :], key=f"yl{kt}",
                     w=(f"Y{kt}",))

    def load_layer_vecs(self, li):
        dt, sb, P = self.dram, self.sb, self.P
        for dn, sn, buf in [("b1a_l", "B1A", "b1a"), ("b1g_l", "B1G", "b1g"),
                            ("b2_l", "B2V", "b2v"), ("g_l", "GV", "gv"),
                            ("bl_l", "BLV", "blv")]:
            self.dma("SP", sb[sn], dt[dn][li], key="lv_" + sn, w=(buf,))
        self.dma("SP", sb["QSR"], dt["qs_l"][li], key="lv_QSR", w=("qsr",))
        self.dma("SP", sb["KSR"], dt["ks_l"][li], key="lv_KSR", w=("ksr",))
        P.op("DVE", lambda e: e.tensor_scalar_mul(sb["QS8"], sb["QSR"], 8.0),
             r=("qsr",), w=("qs8",))
        P.op("POOL", lambda e: e.tensor_mul(sb["SELQ"], sb["SELPAIR"], sb["QS8"]),
             r=("cSELPAIR", "qs8"), w=("selq",))
        P.op("POOL", lambda e: e.tensor_mul(sb["SELK"], sb["SELPAIR"], sb["KSR"]),
             r=("cSELPAIR", "ksr"), w=("selk",))

    def ln_stats(self, chunk, shifted):
        sb, ps, P = self.sb, self.psum, self.P
        P.phase = 'ln'
        c0 = 512 * chunk
        for kt in range(NKT):
            sh = shifted and kt >= 4
            edge = sh and chunk == 0
            if edge:
                P.op("DVE", lambda e, k=kt: e.tensor_copy(
                    sb["SC2"][:, 1:512], sb["Y"][:, k * T: k * T + 511]),
                    r=(f"Y{kt}",), w=("sc2",))
                P.op("DVE", lambda e: e.memset(sb["SC2"][:, 0:1], 0),
                     r=(), w=("sc2",))
                rhs = sb["SC2"]
                rkey = "sc2"
            elif sh:
                rhs = sb["Y"][:, kt * T + c0 - 1: kt * T + c0 + 511]
                rkey = f"Y{kt}"
            else:
                rhs = sb["Y"][:, kt * T + c0: kt * T + c0 + 512]
                rkey = f"Y{kt}"
            self.mm(ps["ps4"][:, 0:512], sb["ONESF"], rhs, "ps4",
                    (rkey, "cONESF"), start=(kt == 0), stop=(kt == NKT - 1))
            xq = sb["SC0"] if kt % 2 == 0 else sb["SC1"]
            xqk = "sc0" if kt % 2 == 0 else "sc1"
            P.op("ACT", lambda e, x=xq, rr=rhs: e.activation(
                x, rr, AF.Square), r=(rkey,), w=(xqk,))
            self.mm(ps["ps5"][:, 0:512], sb["ONESF"], xq,
                    "ps5", (xqk, "cONESF"), start=(kt == 0), stop=(kt == NKT - 1))
        P.op("DVE", lambda e: e.tensor_scalar_mul(
            sb["MU"], ps["ps4"][:, 0:512], 1.0 / D), r=("ps4",), w=("mu",))
        P.op("ACT", lambda e: e.activation(sb["RSTD"], sb["MU"], AF.Square),
             r=("mu",), w=("rstd",))
        P.op("DVE", lambda e: e.scalar_tensor_tensor(
            sb["RSTD"], ps["ps5"][:, 0:512], 1.0 / D, sb["RSTD"],
            ALU.mult, ALU.subtract), r=("ps5", "rstd"), w=("rstd",))
        P.op("DVE", lambda e: e.tensor_scalar_add(sb["RSTD"], sb["RSTD"], EPS),
             r=("rstd",), w=("rstd",))
        P.op("ACT", lambda e: e.activation(sb["RSTD"], sb["RSTD"], AF.Ln),
             r=("rstd",), w=("rstd",))
        P.op("ACT", lambda e: e.activation(sb["RSTD"], sb["RSTD"], AF.Exp,
                                           scale=-0.5),
             r=("rstd",), w=("rstd",))

    def ln_apply_inplace(self, chunk):
        sb, P = self.sb, self.P
        c0 = 512 * chunk
        for kt in range(NKT):
            yap = sb["Y"][:, kt * T + c0: kt * T + c0 + 512]
            P.op("DVE", lambda e, y=yap: e.tensor_sub(y, y, sb["MU"]),
                 r=(f"Y{kt}", "mu"), w=(f"Y{kt}",))
            P.op("DVE", lambda e, y=yap: e.tensor_mul(y, y, sb["RSTD"]),
                 r=(f"Y{kt}", "rstd"), w=(f"Y{kt}",))

    def ln_apply_to_sbb(self, chunk, affine, shifted):
        sb, P = self.sb, self.P
        c0 = 512 * chunk
        for kt in range(NKT):
            sh = shifted and kt >= 4
            edge = sh and chunk == 0
            dst = sb["SBB"][:, kt * T + c0: kt * T + c0 + 512]
            t1 = sb["SC2"]
            if edge:
                src = sb["Y"][:, kt * T: kt * T + 511]
                P.op("DVE", lambda e, s=src: e.tensor_sub(
                    t1[:, 1:512], s, sb["MU"][:, 1:512]),
                    r=(f"Y{kt}", "mu"), w=("sc2",))
                P.op("POOL", lambda e: e.tensor_scalar_mul(
                    t1[:, 0:1], sb["MU"][:, 0:1], -1.0), r=("mu",), w=("sc2",))
            else:
                src = sb["Y"][:, kt * T + c0 - (1 if sh else 0):
                              kt * T + c0 + 512 - (1 if sh else 0)]
                P.op("DVE", lambda e, s=src: e.tensor_sub(t1, s, sb["MU"]),
                     r=(f"Y{kt}", "mu"), w=("sc2",))
            if affine:
                P.op("DVE", lambda e: e.tensor_mul(sb["SC3"], t1, sb["RSTD"]),
                     r=("sc2", "rstd"), w=("sc3",))
                P.op("DVE", lambda e, d=dst, kk=kt: e.tensor_scalar(
                    d, sb["SC3"], sb["GV"][:, kk:kk + 1], sb["BLV"][:, kk:kk + 1],
                    ALU.mult, ALU.add),
                    r=("sc3", "gv", "blv"), w=(f"SBB{kt}",))
            else:
                P.op("DVE", lambda e, d=dst: e.tensor_mul(d, t1, sb["RSTD"]),
                     r=("sc2", "rstd"), w=(f"SBB{kt}",))

    def shift_to_sbb(self):
        sb, P = self.sb, self.P
        for kt in range(NKT):
            dst = sb["SBB"][:, kt * T: (kt + 1) * T]
            if kt < 4:
                P.op("DVE", lambda e, d=dst, k=kt: e.tensor_copy(
                    d, sb["Y"][:, k * T:(k + 1) * T]),
                    r=(f"Y{kt}",), w=(f"SBB{kt}",))
            else:
                P.op("DVE", lambda e, d=dst, k=kt: e.tensor_copy(
                    d[:, 1:T], sb["Y"][:, k * T: (k + 1) * T - 1]),
                    r=(f"Y{kt}",), w=(f"SBB{kt}",))
                P.op("DVE", lambda e, d=dst: e.memset(d[:, 0:1], 0),
                     r=(), w=(f"SBB{kt}",))

    def attn(self, li, top):
        sb, ps, dt, P = self.sb, self.psum, self.dram, self.P
        wsl = dt["wqkv_t"] if top else dt["wqkv_l"][li]
        wosl = dt["wproj_t"] if top else dt["wo_l"][li]
        for g in range(4):
            self.dma("POOL", sb["WA"].rearrange("p (kt n) -> p kt n", kt=NKT),
                     wsl[g].rearrange("kt p n -> p kt n"), key="wa", w=("pool.WA",))
            self.dma("POOL", sb["WO"].rearrange("p (kt n) -> p kt n", kt=NKT * 2),
                     wosl[g].rearrange("nt kp p n -> p (nt kp) n"),
                     key="wo", w=("pool.WO",))
            self.P.phase = 'qkv'
            pending = []
            ui = 0
            for pr in range(2):
                for part, dstbuf in ((0, "QP"), (1, "KP")):
                    for ch in range(4):
                        bank = "ps0" if ui % 2 == 0 else "ps1"
                        pb = ps[bank]
                        for kt in range(NKT):
                            lhs = sb["WA"][:, kt * 768 + 256 * part + 128 * pr:
                                           kt * 768 + 256 * part + 128 * (pr + 1)]
                            rhs = sb["SBB"][:, kt * T + 512 * ch: kt * T + 512 * ch + 512]
                            self.mm(pb[:, 0:512], lhs, rhs, bank,
                                    ("pool.WA", f"SBB{kt}"),
                                    start=(kt == 0), stop=(kt == NKT - 1))
                        if top:
                            dst = sb[dstbuf][:, pr * T + 512 * ch: pr * T + 512 * ch + 512]
                            P.op("ACT", lambda e, d=dst, p=pb: e.activation(
                                d, p[:, 0:512], AF.Copy),
                                r=(bank,), w=(f"pool.{dstbuf}{pr}",))
                        else:
                            qsbk = "sc0" if ui % 2 == 0 else "sc1"
                            qsb = sb["SC0"] if ui % 2 == 0 else sb["SC1"]
                            P.op("ACT", lambda e, q=qsb, p=pb: e.activation(
                                q, p[:, 0:512], AF.Copy), r=(bank,), w=(qsbk,))
                            pending.append((pr, part, ch, dstbuf, ui))
                            if len(pending) >= 2:
                                self.prep_qk(*pending.pop(0))
                        self.vtok_tile(ui)
                        ui += 1
            for args in pending:
                self.prep_qk(*args)
            self.P.phase = 'headsT' if top else 'heads'
            if top:
                self.heads_weave(top)
            else:
                self.heads_qmajor()
            self.P.phase = 'wo'
            for nt in range(NKT):
                for ch in range(4):
                    bank = "ps0" if (nt + ch) % 2 == 0 else "ps1"
                    pb = ps[bank]
                    for kp in range(2):
                        lhs = sb["WO"][:, (nt * 2 + kp) * 128:(nt * 2 + kp + 1) * 128]
                        rhs = sb["OF"][:, kp * T + 512 * ch: kp * T + 512 * ch + 512]
                        self.mm(pb[:, 0:512], lhs, rhs, bank,
                                ("pool.WO", f"pool.OF{kp}"),
                                start=(kp == 0), stop=(kp == 1))
                    yap = sb["Y"][:, nt * T + 512 * ch: nt * T + 512 * ch + 512]
                    P.op("DVE", lambda e, y=yap, p=pb: e.tensor_add(
                        y, y, p[:, 0:512]), r=(f"Y{nt}", bank), w=(f"Y{nt}",))

    def vtok_tile(self, tt):
        sb, ps, P = self.sb, self.psum, self.P
        P.phase = 'vtok'
        pb = ps["ps2"]
        for kt in range(NKT):
            lhs = sb["SBB"][:, kt * T + 128 * tt: kt * T + 128 * (tt + 1)]
            rhs = sb["WA"][:, kt * 768 + 512: kt * 768 + 768]
            self.mm(pb[:, 0:256], lhs, rhs, "ps2",
                    ("pool.WA", f"SBB{kt}"),
                    start=(kt == 0), stop=(kt == NKT - 1))
        dst = sb["VPX"].rearrange("p (t h c) -> p t h c", t=16, h=4)[
            :, tt, :, 0:64]
        P.op("ACT", lambda e, d=dst, p=pb: e.activation(
            d, p[:, 0:256], AF.Copy), r=("ps2",), w=(f"vpx{tt % 2}",))

    def prep_qk(self, pr, part, ch, dstbuf, ui):
        sb, ps, P = self.sb, self.psum, self.P
        P.phase = 'prep'
        pk = "ps3" if ui % 2 == 0 else "ps5"
        pchain = ps[pk]
        qsb = sb["SC0"] if ui % 2 == 0 else sb["SC1"]
        qsbk = "sc0" if ui % 2 == 0 else "sc1"
        qsq = sb["QSQ0"] if ui % 2 == 0 else sb["QSQ1"]
        qsqk = "qsq0" if ui % 2 == 0 else "qsq1"
        P.op("ACT", lambda e, q=qsq, s=qsb: e.activation(q, s, AF.Square),
             r=(qsbk,), w=(qsqk,))
        self.mm(pchain[0:2, 0:512], sb["BD2"], qsq, pk,
                (qsqk, "cBD2"), start=True, stop=True)
        P.op("DVE", lambda e, p=pchain: e.tensor_scalar_max(
            p[0:2, 0:512], p[0:2, 0:512], 1e-24), r=(pk,), w=(pk,))
        P.op("ACT", lambda e, p=pchain: e.activation(
            p[32:34, 0:512], p[0:2, 0:512], AF.Ln), r=(pk,), w=(pk,))
        rb = qsq[0:2, 0:512]
        P.op("ACT", lambda e, p=pchain, rr=rb: e.activation(
            rr, p[32:34, 0:512], AF.Exp, scale=-0.5), r=(pk,), w=(qsqk,))
        selbuf = "SELQ" if part == 0 else "SELK"
        selkey = "selq" if part == 0 else "selk"
        self.mm(pchain[:, 0:512], sb[selbuf], rb, pk,
                (qsqk, selkey), start=True, stop=True)
        slot = sb[dstbuf][:, pr * T + 512 * ch: pr * T + 512 * ch + 512]
        slotk = f"pool.{dstbuf}{pr}"
        P.op("DVE", lambda e, s=slot, q=qsb, p=pchain: e.tensor_mul(
            s, q, p[:, 0:512]), r=(qsbk, pk), w=(slotk,))
        self.mm(pchain[:, 0:512], sb["PERM"], slot, pk,
                (slotk, "cPERM"), start=True, stop=True)
        ctb = "CT" if part == 0 else "CTI"
        stb = "STT" if part == 0 else "STI"
        z0, z0k = (sb["SC2"], "sc2") if ui % 2 == 0 else (sb["SC3"], "sc3")
        tab = sb[ctb][:, 512 * ch: 512 * ch + 512]
        P.op("DVE", lambda e, s=slot, t=tab, z=z0: e.tensor_mul(z, s, t),
             r=(slotk, "c" + ctb), w=(z0k,))
        tab2 = sb[stb][:, 512 * ch: 512 * ch + 512]
        P.op("DVE", lambda e, s=slot, t=tab2, p=pchain: e.tensor_mul(
            s, p[:, 0:512], t), r=(pk, "c" + stb), w=(slotk,))
        P.op("DVE", lambda e, s=slot, z=z0: e.tensor_add(s, s, z),
             r=(slotk, z0k), w=(slotk,))

    def heads_qmajor(self):
        sb, ps, P = self.sb, self.psum, self.P
        blocks = [(hl, qb) for qb in range(16) for hl in range(4)]
        n = len(blocks)
        sbanks = ("ps0", "ps1", "psSTA")

        def s_mm(i):
            hl, qb = blocks[i]
            pr, rh = hl // 2, 64 * (hl % 2)
            w = 128 * min(qb + 1, 3)
            k0 = 128 * max(qb - 2, 0)
            bank = sbanks[i % 3]
            lhs = sb["QP"][rh:rh + 64, pr * T + 128 * qb: pr * T + 128 * (qb + 1)]
            rhs = sb["KP"][rh:rh + 64, pr * T + k0: pr * T + k0 + w]
            self.mm(ps[bank][:, 0:w], lhs, rhs, bank,
                    (f"pool.QP{pr}", f"pool.KP{pr}"), start=True, stop=True)

        def chain(i):
            hl, qb = blocks[i]
            w = 128 * min(qb + 1, 3)
            par = i % 2
            bank = sbanks[i % 3]
            nmc = sb["MBF"][:, 4 + par:5 + par]
            ddc = sb["MBF"][:, 6 + par:7 + par]
            rrc = sb["MBF"][:, 12 + par:13 + par]
            nmk, ddk, rrk = f"nm{par}", f"dd{par}", f"rr{par}"
            eb = sb["EB0"] if par == 0 else sb["EB1"]
            ebk = "eb0" if par == 0 else "eb1"
            P.op("DVE", lambda e, b=ps[bank], ww=w: e.tensor_add(
                b[:, 0:ww], b[:, 0:ww], sb["MASKB"][:, 384 - ww:384]),
                r=(bank, "cMASKB"), w=(bank,))
            P.op("DVE", lambda e, b=ps[bank], ww=w, nm=nmc: e.reduce_max(
                nm, b[:, 0:ww], axis=AX.X, negate=True),
                r=(bank,), w=(nmk,))
            sm = sb["SC0"] if par == 0 else sb["SC1"]
            smk = "sc0" if par == 0 else "sc1"
            P.op("ACT", lambda e, s=sm, b=ps[bank], ww=w, nm=nmc, dd=ddc:
                 e.activation(s[:, 0:ww], b[:, 0:ww], AF.Exp, bias=nm,
                              scale=1.0, accum_out=dd),
                 r=(bank, nmk), w=(smk, ddk))
            P.op("DVE", lambda e, dd=ddc, rr=rrc:
                 e.reciprocal(rr, dd), r=(ddk,), w=(rrk,))
            P.op("ACT", lambda e, d=eb, s=sm, ww=w, rr=rrc:
                 e.activation(d[:, 0:ww], s[:, 0:ww], AF.Copy, scale=rr),
                 r=(smk, rrk), w=(ebk,))

        def tav(i):
            hl, qb = blocks[i]
            pr, rh = hl // 2, 64 * (hl % 2)
            w = 128 * min(qb + 1, 3)
            nt = w // 128
            k0t = max(qb - 2, 0)
            par = i % 2
            eb = sb["EB0"] if par == 0 else sb["EB1"]
            ebk = "eb0" if par == 0 else "eb1"
            tb = "ps2" if par == 0 else "ps3"
            etb = sb["QSQ0"] if par == 0 else sb["QSQ1"]
            etk = "qsq0" if par == 0 else "qsq1"
            for t in range(nt):
                self.mm(ps[tb][:, 128 * t:128 * (t + 1)],
                        eb[:, 128 * t:128 * t + 128], sb["IDENT"], tb,
                        (ebk, "cIDENT"), start=True, stop=True)
            P.op("DVE", lambda e, d=etb, ww=w, b=ps[tb]:
                 e.tensor_copy(d[:, 0:ww], b[:, 0:ww]),
                 r=(tb,), w=(etk,))
            ob = "ps4" if par == 0 else "ps5"
            hloc = 2 * pr + (hl % 2)
            for t in range(nt):
                gk = k0t + t
                vslot = sb["VPX"][:, 260 * gk + 65 * hloc:
                                  260 * gk + 65 * hloc + 64]
                self.mm(ps[ob][0:64, 0:128], vslot,
                        etb[:, 128 * t:128 * t + 128], ob,
                        (etk, f"vpx{gk % 2}"), start=(t == 0),
                        stop=(t == nt - 1))
            odst = sb["OF"][rh:rh + 64, pr * T + 128 * qb: pr * T + 128 * (qb + 1)]
            P.op("ACT", lambda e, d=odst, b=ps[ob]:
                 e.activation(d, b[0:64, 0:128], AF.Copy),
                 r=(ob,), w=(f"pool.OF{pr}",))

        for i in range(n + 2):
            if i < n and i % 2 == 0:
                s_mm(i)
                s_mm(i + 1)
            if 1 <= i <= n:
                chain(i - 1)
            if i >= 2:
                tav(i - 2)

    def heads_weave(self, top):
        for hl in range(5):
            hs = hl if hl < 4 else None
            ha = hl - 1 if hl > 0 else None
            for c in range(4):
                if ha is not None:
                    self.mtz_stage(ha, c)
                glo = 0 if top else max(0, 4 * c - 2)
                ntt = 4 * c + 4 - glo
                prev = None
                napp = ntt if ha is not None else 0
                nst = 4 if hs is not None else 0
                for i in range(max(napp, nst)):
                    if i < napp:
                        self.apply_tile_pe(ha, c, glo, i, top)
                        if prev is not None:
                            self.apply_tile_av(ha, c, glo, prev, ntt, top)
                        prev = i
                    if i < nst:
                        self.stats_j(hs, 4 * c + i, top)
                if prev is not None:
                    self.apply_tile_av(ha, c, glo, prev, ntt, top)
                if ha is not None and c > 0:
                    self.apply_stage2(ha, c - 1, top)
            if ha is not None:
                self.apply_stage2(ha, 3, top)

    def mtz_stage(self, hl, c):
        sb, ps, P = self.sb, self.psum, self.P
        mbb = sb["MBB0"] if hl % 2 == 0 else sb["MBB1"]
        mbk = "mbb0" if hl % 2 == 0 else "mbb1"
        for i in range(4):
            self.mm(ps["psSTB"][0:1, 128 * i:128 * (i + 1)],
                    mbb[:, 4 * c + i:4 * c + i + 1], sb["IDENT"],
                    "psSTB", (mbk, "cIDENT"), start=True, stop=True)
        P.op("ACT", lambda e: e.activation(
            sb["MTZ"], ps["psSTB"][0:1, 0:512], AF.Copy),
            r=("psSTB",), w=("mtz",))

    def apply_tile_pe(self, hl, c, glo, tt, top):
        sb, ps, P = self.sb, self.psum, self.P
        pr = hl // 2
        rh = 64 * (hl % 2)
        qk = f"pool.QP{pr}"
        kk = f"pool.KP{pr}"
        g = glo + tt
        kqb = "ps2" if tt % 2 == 0 else "ps4"
        pkq = ps[kqb]
        lhsk = sb["KP"][rh:rh + 64, pr * T + 128 * g: pr * T + 128 * (g + 1)]
        rhsq = sb["QP"][rh:rh + 64, pr * T + 512 * c: pr * T + 512 * c + 512]
        self.mm(pkq[:, 0:512], lhsk, rhsq, kqb, (qk, kk),
                start=True, stop=False)
        if top:
            dd = g - 4 * c
            if dd > 0:
                self.mm(pkq[:, 0:128 * dd], sb["ONE1"],
                        sb["NEGROW"][0:1, 0:128 * dd], kqb,
                        ("cONE1", "cNEGROW"), start=False, stop=False)
            if 0 <= dd <= 3:
                self.mm(pkq[:, 128 * dd:128 * dd + 128], sb["IDENT"],
                        sb["BAPPW"][:, 128 * 3:128 * 4], kqb,
                        ("cIDENT", "cBAPPW"), start=False, stop=False)
        else:
            s = 3 - g + 4 * c
            self.mm(pkq[:, 0:512], sb["IDENT"],
                    sb["BAPPW"][:, 128 * s:128 * s + 512], kqb,
                    ("cIDENT", "cBAPPW"), start=False, stop=False)
        self.mm(pkq[:, 0:512], sb["ONE1"], sb["MTZ"], kqb,
                ("cONE1", "mtz"), start=False, stop=True)
        ebuf = sb["EB0"] if tt % 2 == 0 else sb["EB1"]
        ebk = "eb0" if tt % 2 == 0 else "eb1"
        P.op("ACT", lambda e, eb=ebuf, p=pkq, sc=(0.125 if top else 1.0):
             e.activation(eb, p[:, 0:512], AF.Exp, scale=sc),
             r=(kqb,), w=(ebk,))

    def apply_tile_av(self, hl, c, glo, tt, ntt, top):
        sb, ps, P = self.sb, self.psum, self.P
        pr = hl // 2
        rh = 64 * (hl % 2)
        g = glo + tt
        avb = "ps3" if c % 2 == 0 else "ps5"
        pav = ps[avb]
        ebuf = sb["EB0"] if tt % 2 == 0 else sb["EB1"]
        ebk = "eb0" if tt % 2 == 0 else "eb1"
        hloc = 2 * pr + (hl % 2)
        vslot = sb["VPX"][:, 260 * g + 65 * hloc: 260 * g + 65 * hloc + 65]
        self.mm(pav[0:65, 0:512], vslot, ebuf, avb,
                (ebk, f"vpx{g % 2}", "vpones"),
                start=(tt == 0), stop=(tt == ntt - 1))

    def stats_j(self, hl, j, top):
        sb, ps, P = self.sb, self.psum, self.P
        mbb = sb["MBB0"] if hl % 2 == 0 else sb["MBB1"]
        mbk = "mbb0" if hl % 2 == 0 else "mbb1"
        pr = hl // 2
        rh = 64 * (hl % 2)
        qk = f"pool.QP{pr}"
        kk = f"pool.KP{pr}"
        stk = "psSTA" if j % 2 == 0 else "ps0"
        pst = ps[stk]
        tk_lo = 0 if top else max(0, 128 * (j - 2))
        tk_hi = 128 * (j + 1)
        wj = tk_hi - tk_lo
        lhs = sb["QP"][rh:rh + 64, pr * T + 128 * j: pr * T + 128 * (j + 1)]
        if top:
            nch = (wj + 511) // 512
            for cc in range(nch):
                a = tk_lo + 512 * cc
                b = min(tk_hi, a + 512)
                rhs = sb["KP"][rh:rh + 64, pr * T + a: pr * T + b]
                diag = (b == tk_hi)
                self.mm(pst[:, 0:(b - a)], lhs, rhs, stk,
                        (qk, kk), start=True, stop=not diag)
                if diag:
                    off = (b - a) - 128
                    self.mm(pst[:, off:off + 128], sb["IDENT"],
                            sb["B128"], stk, ("cIDENT", "cB128"),
                            start=False, stop=True)
                red_dst = sb["MBF"][:, cc:cc + 1]
                P.op("DVE", lambda e, rd=red_dst, n=(b - a):
                     e.reduce_max(rd, pst[:, 0:n],
                                  axis=AX.X), r=(stk,), w=("mbf",))
            P.op("POOL", lambda e, n=nch: e.tensor_copy(
                sb["MBF"][:, 8:8 + n], sb["MBF"][:, 0:n]),
                r=("mbf",), w=("mbf2",))
            P.op("DVE", lambda e, jj=j, n=nch, m=mbb: e.reduce_max(
                m[:, jj:jj + 1], sb["MBF"][:, 8:8 + n], axis=AX.X,
                negate=True), r=("mbf2",), w=(mbk,))
        else:
            rhs = sb["KP"][rh:rh + 64, pr * T + tk_lo: pr * T + tk_hi]
            self.mm(pst[:, 0:wj], lhs, rhs, stk, (qk, kk),
                    start=True, stop=False)
            mask = {128: "B128", 256: "B256", 384: "B384"}[wj]
            self.mm(pst[:, 0:wj], sb["IDENT"], sb[mask],
                    stk, ("cIDENT", "c" + mask), start=False, stop=True)
            P.op("DVE", lambda e, jj=j, n=wj, m=mbb: e.reduce_max(
                m[:, jj:jj + 1], pst[:, 0:n], axis=AX.X,
                negate=True), r=(stk,), w=(mbk,))

    def apply_stage2(self, hl, c, top):
        sb, ps, P = self.sb, self.psum, self.P
        pr = hl // 2
        rh = 64 * (hl % 2)
        avb = "ps3" if c % 2 == 0 else "ps5"
        pav = ps[avb]
        P.op("ACT", lambda e, p=pav: e.activation(
            ps["psSTB"][32:33, 0:512], p[64:65, 0:512], AF.Ln),
            r=(avb,), w=("psSTB",))
        P.op("ACT", lambda e: e.activation(
            sb["RL"], ps["psSTB"][32:33, 0:512], AF.Exp, scale=-1.0),
            r=("psSTB",), w=("rl",))
        self.mm(ps["psSTB"][64:128, 0:512], sb["ONES64"], sb["RL"], "psSTB",
                ("rl", "cONES64"), start=True, stop=True)
        P.op("ACT", lambda e: e.activation(
            sb["LB"], ps["psSTB"][64:128, 0:512], AF.Copy),
            r=("psSTB",), w=("lb",))
        odst = sb["OF"][rh:rh + 64, pr * T + 512 * c: pr * T + 512 * c + 512]
        P.op("DVE", lambda e, o=odst, p=pav: e.tensor_mul(
            o, p[0:64, 0:512], sb["LB"]),
            r=(avb, "lb"), w=(f"pool.OF{pr}",))

    def ffn(self, li):
        sb, ps, dt, P = self.sb, self.psum, self.dram, self.P
        for ch in range(4):
            self.ln_stats(ch, shifted=True)
            self.ln_apply_to_sbb(ch, affine=True, shifted=True)
        for ch in range(4):
            self.P.phase = 'w1'
            for j in range(NFT):
                wbuf = sb["W1A"] if j % 2 == 0 else sb["W1B"]
                wkey = "pool.W1A" if j % 2 == 0 else "pool.W1B"
                self.dma("POOL", wbuf.rearrange("p (kt n) -> p kt n", kt=NKT),
                         dt["w1_l"][li, j].rearrange("kt p n -> p kt n"),
                         key="w1" + wkey[-1], w=(wkey,))
                ab = "ps0" if j % 2 == 0 else "ps2"
                gb = "ps1" if j % 2 == 0 else "ps3"
                for kt in range(NKT):
                    rhs = sb["SBB"][:, kt * T + 512 * ch: kt * T + 512 * ch + 512]
                    self.mm(ps[ab][:, 0:512], wbuf[:, kt * 256: kt * 256 + 128],
                            rhs, ab, (wkey, f"SBB{kt}"),
                            start=(kt == 0), stop=(kt == NKT - 1))
                    self.mm(ps[gb][:, 0:512], wbuf[:, kt * 256 + 128: kt * 256 + 256],
                            rhs, gb, (wkey, f"SBB{kt}"),
                            start=(kt == 0), stop=(kt == NKT - 1))
                gl = sb["SC0"] if j % 2 == 0 else sb["SC1"]
                glk = "sc0" if j % 2 == 0 else "sc1"
                P.op("ACT", lambda e, g=gl, jj=j, pg=ps[gb]: e.activation(
                    g, pg[:, 0:512], AF.Gelu,
                    bias=sb["B1G"][:, jj:jj + 1]), r=(gb, "b1g"), w=(glk,))
                gdst = sb["GB"][:, 512 * j: 512 * (j + 1)]
                P.op("DVE", lambda e, g=gl, d=gdst, jj=j, pa=ps[ab]:
                     e.scalar_tensor_tensor(
                    d, pa[:, 0:512], sb["B1A"][:, jj:jj + 1], g,
                    ALU.add, ALU.mult), r=(ab, glk, "b1a"),
                    w=(f"pool.GB{j % 2}",))
            self.P.phase = 'w2'
            for nt in range(NKT):
                wbuf = sb["W2A"] if nt % 2 == 0 else sb["W2B"]
                wkey = "pool.W2A" if nt % 2 == 0 else "pool.W2B"
                self.dma("POOL", wbuf.rearrange("p (kt n) -> p kt n", kt=NFT),
                         dt["w2_l"][li, nt].rearrange("kt p n -> p kt n"),
                         key="w2" + wkey[-1], w=(wkey,))
                ob = "ps4" if nt % 2 == 0 else "ps5"
                for kt in range(NFT):
                    self.mm(ps[ob][:, 0:512], wbuf[:, kt * 128:(kt + 1) * 128],
                            sb["GB"][:, 512 * kt: 512 * (kt + 1)], ob,
                            (wkey, f"pool.GB{kt % 2}"),
                            start=(kt == 0), stop=(kt == NFT - 1))
                yap = sb["Y"][:, nt * T + 512 * ch: nt * T + 512 * ch + 512]
                P.op("DVE", lambda e, y=yap, n=nt, po=ps[ob]:
                     e.scalar_tensor_tensor(
                    y, po[:, 0:512], sb["B2V"][:, n:n + 1], y,
                    ALU.add, ALU.add), r=(ob, "b2v", f"Y{nt}"), w=(f"Y{nt}",))

    def merge_with_input(self, dram_src, factor, to_out=False):
        sb, dt, P = self.sb, self.dram, self.P
        P.phase = 'merge'
        for kt in range(NKT):
            for ch in range(4):
                par = (kt * 4 + ch) % 2
                xb = sb["SC0"] if par == 0 else sb["SC1"]
                xbk = "sc0" if par == 0 else "sc1"
                self.dma("SP", xb,
                         dram_src[128 * kt:128 * (kt + 1), 512 * ch:512 * ch + 512],
                         key=xbk, r=(), w=(xbk,))
                yap = sb["Y"][:, kt * T + 512 * ch: kt * T + 512 * ch + 512]
                if to_out:
                    ob = sb["SC2"] if par == 0 else sb["SC3"]
                    obk = "sc2" if par == 0 else "sc3"
                    P.op("DVE", lambda e, y=yap, x=xb, o=ob, f=factor:
                         e.scalar_tensor_tensor(o, y, f, x, ALU.mult, ALU.add),
                         r=(f"Y{kt}", xbk), w=(obk,))
                    self.dma("SP", dt["out"][128 * kt:128 * (kt + 1),
                                             512 * ch:512 * ch + 512],
                             ob, key="outst", r=(obk,), w=())
                else:
                    P.op("DVE", lambda e, y=yap, x=xb, f=factor:
                         e.scalar_tensor_tensor(y, y, f, x, ALU.mult, ALU.add),
                         r=(f"Y{kt}", xbk), w=(f"Y{kt}",))

    def store_y_to_out(self):
        sb, dt = self.sb, self.dram
        for kt in range(NKT):
            self.dma("SP", dt["out"][128 * kt:128 * (kt + 1), :],
                     sb["Y"][:, kt * T:(kt + 1) * T], key="outst",
                     r=(f"Y{kt}",), w=())

    def ffn6(self, n_sub):
        for i in range((n_sub + 1) // 2):
            self.load_layer_vecs(i)
            self.P.fence_pool()
            self.shift_to_sbb()
            self.attn(i, top=False)
            if 2 * i + 2 > n_sub:
                return
            self.P.fence_pool()
            self.ffn(i)

    def build(self):
        cfg = self.cfg
        self.load_consts()
        if cfg.get("ln0", True):
            for ch in range(4):
                self.ln_stats(ch, shifted=False)
                self.ln_apply_inplace(ch)
        self.ffn6(cfg.get("n_sub1", 12))
        if cfg.get("merge1", True):
            self.merge_with_input(self.dram["xT"], 0.5)
        if cfg.get("top", True):
            for ch in range(4):
                self.ln_stats(ch, shifted=False)
                self.ln_apply_to_sbb(ch, affine=False, shifted=False)
            self.P.fence_pool()
            self.attn(0, top=True)
        if cfg.get("call2", True):
            for kt in range(NKT):
                self.dma("SP", self.dram["x2scr"][128 * kt:128 * (kt + 1), :],
                         self.sb["Y"][:, kt * T:(kt + 1) * T], key="x2st",
                         r=(f"Y{kt}",), w=())
            for ch in range(4):
                self.ln_stats(ch, shifted=False)
                self.ln_apply_inplace(ch)
            self.ffn6(cfg.get("n_sub2", 12))
            self.merge_with_input(self.dram["x2scr"], 0.5, to_out=True)
        elif cfg.get("dump") == "of":
            for kp in range(2):
                self.dma("POOL", self.dram["out"][128 * kp:128 * (kp + 1), :],
                         self.sb["OF"][:, kp * T:(kp + 1) * T], key="outst",
                         r=(f"pool.OF{kp}",), w=())
        elif cfg.get("dump") == "sbb":
            for kt in range(NKT):
                self.dma("POOL", self.dram["out"][128 * kt:128 * (kt + 1), :],
                         self.sb["SBB"][:, kt * T:(kt + 1) * T], key="outst",
                         r=(f"SBB{kt}",), w=())
        else:
            self.store_y_to_out()


_LAST_PROG = None


def build_program(cfg=None):
    global _LAST_PROG
    cfg = cfg or {}
    nc = bass.Bass("TRN2", target_bir_lowering=False,
                   detect_race_conditions=False)
    mk = MK(nc, cfg)
    _LAST_PROG = mk.P
    mk.build()
    counters = mk.P.resolve()
    sems = {k: nc.alloc_semaphore(name=f"s{i}")
            for i, k in enumerate(counters)}
    with nc.allow_low_precision(reason="bf16 matmul operands / exp tiles"), \
            nc.Block() as block:
        @block.tensor
        def _(tensor):
            mk.P.emit("PE", tensor, sems)

        @block.scalar
        def _(scalar):
            mk.P.emit("ACT", scalar, sems)

        @block.vector
        def _(vector):
            mk.P.emit("DVE", vector, sems)

        @block.gpsimd
        def _(gpsimd):
            mk.P.emit("POOL", gpsimd, sems)

        @block.sync
        def _(sync):
            mk.P.emit("SP", sync, sems)
            if "outst" in counters:
                sync.wait_ge(sems["outst"], counters["outst"])
    return nc


def stage_all(inputs):
    tabs = host_tables()
    consts = host_consts()
    staged = stage_weights(inputs)
    shared = {}
    shared.update({k: tabs[k] for k in ("ct", "st", "cti", "sti")})
    shared.update(consts)
    shared.update(staged)
    return shared


def kernel(**inputs):
    nc = build_program({})
    shared = stage_all(inputs)
    x = np.asarray(inputs["L0_attn_x"])
    in_maps = []
    for b in range(8):
        m = dict(shared)
        m["xT"] = np.ascontiguousarray(x[b].T)
        in_maps.append(m)
    res = run_bass_kernel_spmd(nc, in_maps, core_ids=list(range(8)))
    out = np.stack([np.asarray(res.results[b]["out"]).T for b in range(8)])
    return out.astype(np.float32)


# version 11 (final)
# speedup vs baseline: 1.5799x; 1.0059x over previous
import numpy as np
import ml_dtypes
import concourse.bass as bass
import concourse.mybir as mybir
from concourse.bass_utils import run_bass_kernel_spmd

F32 = mybir.dt.float32
BF16 = mybir.dt.bfloat16
AF = mybir.ActivationFunctionType
ALU = mybir.AluOpType
AX = mybir.AxisListType

D = 1024
T = 2048
NKT = 8
NH = 16
NJ = 16
DIP = 2816
NFT = 22
NEG = -3.0e38
EPS = 1e-5

ENGINES = ("PE", "ACT", "DVE", "POOL", "SP")


class Op:
    __slots__ = ("eng", "fn", "reads", "writes", "dma_key", "deps", "idx",
                 "sem_key", "inc_val", "clock", "needed", "waits", "phase")

    def __init__(self, eng, fn, reads, writes, dma_key):
        self.eng = eng
        self.fn = fn
        self.reads = reads
        self.writes = writes
        self.dma_key = dma_key
        self.deps = []
        self.sem_key = None
        self.inc_val = None
        self.clock = None
        self.needed = False
        self.waits = ()


class Prog:
    def __init__(self):
        self.ops = []
        self.last_w = {}
        self.readers = {}
        self.pending_fence = None
        self.phase = "init"
        self.inst_phase = {}

    def fence_pool(self):
        per_eng = {}
        for key in [k for k in self.last_w if k.startswith("pool.")]:
            cands = [self.last_w[key]] + self.readers.get(key, [])
            for o in cands:
                cur = per_eng.get(o.eng)
                if cur is None or o.idx > cur.idx:
                    per_eng[o.eng] = o
            del self.last_w[key]
            self.readers.pop(key, None)
        self.pending_fence = list(per_eng.values())

    def op(self, eng, fn, r=(), w=(), dma_key=None):
        o = Op(eng, fn, tuple(r), tuple(w), dma_key)
        o.phase = self.phase
        o.idx = len(self.ops)
        deps = {}
        for b in o.reads:
            if b in self.last_w:
                d = self.last_w[b]
                deps[d.idx] = d
        for b in o.writes:
            if b in self.last_w:
                d = self.last_w[b]
                deps[d.idx] = d
            for rd in self.readers.get(b, ()):
                deps[rd.idx] = rd
        if self.pending_fence and any(b.startswith("pool.") for b in o.writes):
            for d in self.pending_fence:
                deps[d.idx] = d
        o.deps = list(deps.values())
        for b in o.reads:
            self.readers.setdefault(b, []).append(o)
        for b in o.writes:
            self.last_w[b] = o
            self.readers[b] = []
        self.ops.append(o)
        return o

    def resolve(self):
        for o in self.ops:
            for d in o.deps:
                if d.dma_key is not None or d.eng != o.eng:
                    d.needed = True
            if o.dma_key is not None:
                o.needed = True
        counters = {}
        for o in self.ops:
            if not o.needed:
                continue
            key = o.dma_key if o.dma_key is not None else o.eng
            step = 16 if o.dma_key is not None else 1
            counters[key] = counters.get(key, 0) + step
            o.sem_key = key
            o.inc_val = counters[key]
        clocks = {e: {} for e in ENGINES}

        def merge(dst, src):
            for k, v in src.items():
                if dst.get(k, 0) < v:
                    dst[k] = v

        for o in self.ops:
            ck = clocks[o.eng]
            waits = []
            for d in o.deps:
                if d.sem_key is None:
                    continue
                if d.dma_key is None and d.eng == o.eng:
                    continue
                if ck.get(d.sem_key, 0) < d.inc_val:
                    waits.append((d.sem_key, d.inc_val))
                    if d.clock:
                        merge(ck, d.clock)
                    if ck.get(d.sem_key, 0) < d.inc_val:
                        ck[d.sem_key] = d.inc_val
            o.waits = tuple(waits)
            if o.sem_key is not None:
                o.clock = dict(ck)
                o.clock[o.sem_key] = o.inc_val
        self.counters = counters
        return counters

    def emit(self, eng_name, eng_obj, sems):
        for o in self.ops:
            if o.eng != eng_name:
                continue
            for key, val in o.waits:
                eng_obj.wait_ge(sems[key], val)
            inst = o.fn(eng_obj)
            try:
                self.inst_phase[inst.ins.name] = o.phase
            except Exception:
                pass
            if o.sem_key is not None:
                inst.then_inc(sems[o.sem_key], 16 if o.dma_key else 1)


def _bf(a):
    return np.ascontiguousarray(np.asarray(a, dtype=np.float32)).astype(ml_dtypes.bfloat16)


def _f32(a):
    return np.ascontiguousarray(np.asarray(a), dtype=np.float32)


def host_tables():
    HD = 64
    half = np.arange(0, HD, 2, dtype=np.float64)
    inv_freq = 1.0 / 10000.0 ** (half / HD)
    t = np.arange(T, dtype=np.float64)
    freqs = np.concatenate([t[:, None] * inv_freq[None, :]] * 2, -1)
    base = (half + 0.4 * HD) / (1.4 * HD)
    power = (t - T // 2) / 128.0
    scale = np.concatenate([base[None, :] ** power[:, None]] * 2, -1)
    out = {}
    for nm, tab in [("ct", np.cos(freqs) * scale), ("st", np.sin(freqs) * scale),
                    ("cti", np.cos(freqs) / scale), ("sti", np.sin(freqs) / scale)]:
        tt = tab.T
        out[nm] = _f32(np.concatenate([tt, tt], axis=0))
    return out


def host_consts():
    c = {}
    r = np.arange(128)
    k = r[:, None]
    c384 = np.arange(384)[None, :]
    c["B384"] = _bf(NEG * ((c384 < k) | (c384 > k + 256)))
    c256 = np.arange(256)[None, :]
    c["B256"] = _bf(NEG * (c256 > k + 128))
    c128 = np.arange(128)[None, :]
    c["B128"] = _bf(NEG * (c128 > k))
    c["IDENT"] = _bf(np.eye(128))
    c["ONESF"] = _f32(np.ones((128, 128)))
    c["ONESB"] = _bf(np.ones((128, 128)))
    c["NEGROW"] = _bf(np.full((1, 384), NEG))
    c["ONE1"] = _bf(np.ones((1, 128)))
    bw = np.zeros((128, 9 * 128), np.float32)
    for u in range(9):
        rel = 5 - u
        blk = bw[:, 128 * u:128 * (u + 1)]
        if rel == 0:
            blk[:] = NEG * (k < r[None, :])
        elif rel == 1:
            pass
        elif rel == 2:
            blk[:] = NEG * (k > r[None, :])
        else:
            blk[:] = NEG
    c["BAPPW"] = _bf(bw)
    mb_ = np.zeros((128, 384), np.float32)
    c384b = np.arange(384)[None, :]
    mb_[(c384b < k) | (c384b > k + 256)] = NEG
    c["MASKB"] = _f32(mb_)
    sp = np.zeros((2, 128), np.float32)
    sp[0, :64] = 1.0
    sp[1, 64:] = 1.0
    c["SELPAIR"] = _bf(sp)
    bd = np.zeros((128, 2), np.float32)
    bd[:64, 0] = 1.0
    bd[64:, 1] = 1.0
    c["BD2"] = _bf(bd)
    c["ONESTK"] = _bf(np.ones((128, 1)))
    c["ONES64"] = _f32(np.ones((1, 64)))
    p = np.zeros((64, 64), np.float32)
    for i in range(32):
        p[i + 32, i] = -1.0
        p[i, i + 32] = 1.0
    pp = np.zeros((128, 128), np.float32)
    pp[:64, :64] = p
    pp[64:, 64:] = p
    c["PERM"] = _bf(pp)
    return c


def stage_weights(inp):
    st = {}
    L = 6
    wqkv = _f32(inp["L0_ffn_Wqkv"])
    a = wqkv.reshape(L, NKT, 128, 24, 128)
    slab = np.empty((L, 4, NKT, 128, 768), np.float32)
    for g in range(4):
        for part in range(3):
            for pr in range(2):
                nt = part * 8 + 2 * g + pr
                slab[:, g, :, :, 256 * part + 128 * pr:256 * part + 128 * (pr + 1)] = \
                    a[:, :, :, nt, :]
    st["wqkv_l"] = np.ascontiguousarray(slab)
    wo = _f32(inp["L0_ffn_Wo"]).reshape(L, NKT, 128, NKT, 128)
    wos = np.empty((L, 4, NKT, 2, 128, 128), np.float32)
    for g in range(4):
        for kp in range(2):
            wos[:, g, :, kp] = wo[:, 2 * g + kp].transpose(0, 2, 1, 3)
    st["wo_l"] = np.ascontiguousarray(wos)
    W1 = _f32(inp["L0_ffn_W1"])
    z = np.zeros((L, D, DIP - 2730), np.float32)
    W1a = np.concatenate([W1[:, :, :2730], z], axis=2).reshape(L, D, NFT, 128)
    W1g = np.concatenate([W1[:, :, 2730:], z], axis=2).reshape(L, D, NFT, 128)
    w1s = np.empty((L, NFT, NKT, 128, 256), np.float32)
    for j in range(NFT):
        w1s[:, j, :, :, 0:128] = W1a[:, :, j].reshape(L, NKT, 128, 128)
        w1s[:, j, :, :, 128:256] = W1g[:, :, j].reshape(L, NKT, 128, 128)
    st["w1_l"] = np.ascontiguousarray(w1s)
    b1 = _f32(inp["L0_ffn_b1"])
    zb = np.zeros((L, DIP - 2730), np.float32)
    b1a = np.concatenate([b1[:, :2730], zb], 1).reshape(L, NFT, 128)
    b1g = np.concatenate([b1[:, 2730:], zb], 1).reshape(L, NFT, 128)
    st["b1a_l"] = np.ascontiguousarray(b1a.transpose(0, 2, 1))
    st["b1g_l"] = np.ascontiguousarray(b1g.transpose(0, 2, 1))
    W2 = _f32(inp["L0_ffn_W2"])
    W2p = np.concatenate([W2, np.zeros((L, DIP - 2730, D), np.float32)], 1)
    st["w2_l"] = np.ascontiguousarray(
        W2p.reshape(L, NFT, 128, NKT, 128).transpose(0, 3, 1, 2, 4))
    st["b2_l"] = np.ascontiguousarray(
        _f32(inp["L0_ffn_b2"]).reshape(L, NKT, 128).transpose(0, 2, 1))
    st["g_l"] = np.ascontiguousarray(
        _f32(inp["L0_ffn_ln_g"]).reshape(L, NKT, 128).transpose(0, 2, 1))
    st["bl_l"] = np.ascontiguousarray(
        _f32(inp["L0_ffn_ln_b"]).reshape(L, NKT, 128).transpose(0, 2, 1))
    qs = _f32(inp["L0_ffn_q_scale"])
    ks = _f32(inp["L0_ffn_k_scale"])
    st["qs_l"] = np.ascontiguousarray(
        np.tile(qs, (1, 2))[:, None, :].repeat(2, 1))
    st["ks_l"] = np.ascontiguousarray(
        np.tile(ks, (1, 2))[:, None, :].repeat(2, 1))
    wq = _f32(inp["L0_attn_w_qkv"]).reshape(NKT, 128, 24, 128)
    slabt = np.empty((4, NKT, 128, 768), np.float32)
    for g in range(4):
        for part in range(3):
            for pr in range(2):
                nt = part * 8 + 2 * g + pr
                slabt[g, :, :, 256 * part + 128 * pr:256 * part + 128 * (pr + 1)] = \
                    wq[:, :, nt, :]
    st["wqkv_t"] = np.ascontiguousarray(slabt)
    wp = _f32(inp["L0_attn_w_proj"]).reshape(NKT, 128, NKT, 128)
    wps = np.empty((4, NKT, 2, 128, 128), np.float32)
    for g in range(4):
        for kp in range(2):
            wps[g, :, kp] = wp[2 * g + kp].transpose(1, 0, 2)
    st["wproj_t"] = np.ascontiguousarray(wps)
    return st


class MK:
    def __init__(self, nc, cfg):
        self.nc = nc
        self.cfg = cfg
        self.P = Prog()
        self.declare_dram()
        self.alloc_sbuf()
        self.alloc_psum()

    def declare_dram(self):
        nc = self.nc
        dt = {}
        dt["xT"] = nc.dram_tensor("xT", [D, T], F32, kind="ExternalInput")
        dt["out"] = nc.dram_tensor("out", [D, T], F32, kind="ExternalOutput")
        dt["x2scr"] = nc.dram_tensor("x2scr", [D, T], F32, kind="Internal")
        for nm in ("ct", "st", "cti", "sti"):
            dt[nm] = nc.dram_tensor(nm, [128, T], F32, kind="ExternalInput")
        cshapes = dict(B384=[128, 384], B256=[128, 256], B128=[128, 128],
                       IDENT=[128, 128], ONESB=[128, 128], NEGROW=[1, 384],
                       ONE1=[1, 128], BAPPW=[128, 9 * 128],
                       SELPAIR=[2, 128], BD2=[128, 2],
                       ONESTK=[128, 1], PERM=[128, 128])
        for nm, shp in cshapes.items():
            dt[nm] = nc.dram_tensor(nm, shp, BF16, kind="ExternalInput")
        dt["ONESF"] = nc.dram_tensor("ONESF", [128, 128], F32, kind="ExternalInput")
        dt["MASKB"] = nc.dram_tensor("MASKB", [128, 384], F32, kind="ExternalInput")
        dt["ONES64"] = nc.dram_tensor("ONES64", [1, 64], F32, kind="ExternalInput")
        dt["wqkv_l"] = nc.dram_tensor("wqkv_l", [6, 4, NKT, 128, 768], F32, kind="ExternalInput")
        dt["wo_l"] = nc.dram_tensor("wo_l", [6, 4, NKT, 2, 128, 128], F32, kind="ExternalInput")
        dt["w1_l"] = nc.dram_tensor("w1_l", [6, NFT, NKT, 128, 256], F32, kind="ExternalInput")
        dt["w2_l"] = nc.dram_tensor("w2_l", [6, NKT, NFT, 128, 128], F32, kind="ExternalInput")
        for nm, shp in [("b1a_l", [6, 128, NFT]), ("b1g_l", [6, 128, NFT]),
                        ("b2_l", [6, 128, NKT]), ("g_l", [6, 128, NKT]),
                        ("bl_l", [6, 128, NKT]), ("qs_l", [6, 2, 128]),
                        ("ks_l", [6, 2, 128])]:
            dt[nm] = nc.dram_tensor(nm, shp, F32, kind="ExternalInput")
        dt["wqkv_t"] = nc.dram_tensor("wqkv_t", [4, NKT, 128, 768], F32, kind="ExternalInput")
        dt["wproj_t"] = nc.dram_tensor("wproj_t", [4, NKT, 2, 128, 128], F32, kind="ExternalInput")
        self.dram = dt

    def alloc_sbuf(self):
        nc = self.nc
        sb = {}

        def A(name, shape, dtype):
            h = nc.alloc_sbuf_tensor("sb_" + name, shape, dtype)
            sb[name] = h[:]
            return h

        A("Y", [128, NKT * T], F32)
        A("SBB", [128, NKT * T], BF16)
        poolb = nc.alloc_sbuf_tensor("POOLB", [128, 20992], BF16)
        pb = poolb[:]
        sb["QP"] = pb[:, 0:4096]
        sb["KP"] = pb[:, 4096:8192]
        sb["OF"] = pb[:, 8192:12288]
        sb["WA"] = pb[:, 12288:18432]
        sb["WO"] = pb[:, 18432:20480]
        sb["VPX"] = nc.alloc_sbuf_tensor("VPX", [128, 4160], BF16)[:]
        sb["GB"] = pb[:, 0:11264]
        sb["W1A"] = pb[:, 11264:13312]
        sb["W1B"] = pb[:, 13312:15360]
        sb["W2A"] = pb[:, 15360:18176]
        sb["W2B"] = pb[:, 18176:20992]
        for nm in ("CT", "STT", "CTI", "STI"):
            A(nm, [128, T], F32)
        for nm, shp, dt_ in [("B128", [128, 128], BF16), ("IDENT", [128, 128], BF16),
                             ("ONESB", [128, 128], BF16), ("NEGROW", [1, 384], BF16),
                             ("ONE1", [1, 128], BF16), ("BAPPW", [128, 9 * 128], BF16),
                             ("ONESF", [128, 128], F32),
                             ("MASKB", [128, 384], F32),
                             ("SELPAIR", [2, 128], BF16), ("BD2", [128, 2], BF16),
                             ("ONESTK", [128, 1], BF16), ("ONES64", [1, 64], F32),
                             ("PERM", [128, 128], BF16)]:
            A(nm, shp, dt_)
        for nm, shp in [("B1A", [128, NFT]), ("B1G", [128, NFT]),
                        ("B2V", [128, NKT]), ("GV", [128, NKT]),
                        ("BLV", [128, NKT]), ("QSR", [2, 128]), ("KSR", [2, 128]),
                        ("QS8", [2, 128])]:
            A(nm, shp, F32)
        A("SELQ", [2, 128], BF16)
        A("SELK", [2, 128], BF16)
        for i in range(4):
            A(f"SC{i}", [128, 512], F32)
        A("QSQ0", [128, 512], BF16)
        A("QSQ1", [128, 512], BF16)
        A("EB0", [128, 512], BF16)
        A("EB1", [128, 512], BF16)
        A("MTZ", [1, 512], BF16)
        A("MBF", [128, 16], F32)
        A("MBB0", [128, 16], BF16)
        A("MBB1", [128, 16], BF16)
        A("LB", [64, 512], F32)
        A("RL", [1, 512], F32)
        A("MU", [128, 512], F32)
        A("RSTD", [128, 512], F32)
        self.sb = sb
        assert nc.sbuf_bytes_remaining >= 0, "SBUF over budget"

    def alloc_psum(self):
        nc = self.nc
        ps = {}
        for i in range(6):
            ps[f"ps{i}"] = nc.alloc_psum_tensor(f"ps{i}", [128, 512], F32)[:]
        ps["psSTA"] = nc.alloc_psum_tensor("psSTA", [128, 512], F32)[:]
        ps["psSTB"] = nc.alloc_psum_tensor("psSTB", [128, 512], F32)[:]
        self.psum = ps

    def mm(self, out_ap, lhsT, rhs, bank, r, start, stop):
        self.P.op("PE", lambda e, o=out_ap, l=lhsT, rr=rhs, s=start, t=stop:
                  e.matmul(o, l, rr, start=s, stop=t), r=r, w=(bank,))

    def dma(self, eng, dst, src, key, r=(), w=()):
        self.P.op(eng, lambda e, d=dst, s=src: e.dma_start(d, s),
                  r=r, w=w, dma_key=key)

    def load_consts(self):
        sb, dt = self.sb, self.dram
        for s in ("B128", "IDENT", "ONESB", "NEGROW", "ONE1",
                  "BAPPW", "ONESF", "MASKB", "SELPAIR", "BD2", "ONESTK",
                  "ONES64", "PERM"):
            self.dma("SP", sb[s], dt[s][:], key="c_" + s, w=("c" + s,))
        for s, d in [("CT", "ct"), ("STT", "st"), ("CTI", "cti"), ("STI", "sti")]:
            self.dma("SP", sb[s], dt[d][:], key="c_" + s, w=("c" + s,))
        ones_ap = sb["VPX"].rearrange(
            "p (t h c) -> p t h c", t=16, h=4)[:, :, :, 64:65]
        self.P.op("DVE", lambda e, a=ones_ap: e.memset(a, 1.0),
                  r=(), w=("vpones",))

        for kt in range(NKT):
            self.dma("SP", sb["Y"][:, kt * T:(kt + 1) * T],
                     dt["xT"][128 * kt:128 * (kt + 1), :], key=f"yl{kt}",
                     w=(f"Y{kt}",))

    def load_layer_vecs(self, li):
        dt, sb, P = self.dram, self.sb, self.P
        for dn, sn, buf in [("b1a_l", "B1A", "b1a"), ("b1g_l", "B1G", "b1g"),
                            ("b2_l", "B2V", "b2v"), ("g_l", "GV", "gv"),
                            ("bl_l", "BLV", "blv")]:
            self.dma("SP", sb[sn], dt[dn][li], key="lv_" + sn, w=(buf,))
        self.dma("SP", sb["QSR"], dt["qs_l"][li], key="lv_QSR", w=("qsr",))
        self.dma("SP", sb["KSR"], dt["ks_l"][li], key="lv_KSR", w=("ksr",))
        P.op("DVE", lambda e: e.tensor_scalar_mul(sb["QS8"], sb["QSR"], 8.0),
             r=("qsr",), w=("qs8",))
        P.op("POOL", lambda e: e.tensor_mul(sb["SELQ"], sb["SELPAIR"], sb["QS8"]),
             r=("cSELPAIR", "qs8"), w=("selq",))
        P.op("POOL", lambda e: e.tensor_mul(sb["SELK"], sb["SELPAIR"], sb["KSR"]),
             r=("cSELPAIR", "ksr"), w=("selk",))

    def ln_stats(self, chunk, shifted):
        sb, ps, P = self.sb, self.psum, self.P
        P.phase = 'ln'
        c0 = 512 * chunk
        for kt in range(NKT):
            sh = shifted and kt >= 4
            edge = sh and chunk == 0
            if edge:
                P.op("DVE", lambda e, k=kt: e.tensor_copy(
                    sb["SC2"][:, 1:512], sb["Y"][:, k * T: k * T + 511]),
                    r=(f"Y{kt}",), w=("sc2",))
                P.op("DVE", lambda e: e.memset(sb["SC2"][:, 0:1], 0),
                     r=(), w=("sc2",))
                rhs = sb["SC2"]
                rkey = "sc2"
            elif sh:
                rhs = sb["Y"][:, kt * T + c0 - 1: kt * T + c0 + 511]
                rkey = f"Y{kt}"
            else:
                rhs = sb["Y"][:, kt * T + c0: kt * T + c0 + 512]
                rkey = f"Y{kt}"
            xb = sb["EB0"] if kt % 2 == 0 else sb["EB1"]
            xbk = "eb0" if kt % 2 == 0 else "eb1"
            P.op("DVE", lambda e, x=xb, rr=rhs: e.tensor_copy(
                x[:, 0:512], rr), r=(rkey,), w=(xbk,))
            self.mm(ps["ps4"][:, 0:512], sb["ONESB"], xb[:, 0:512], "ps4",
                    (xbk, "cONESB"), start=(kt == 0), stop=(kt == NKT - 1))
            xq = sb["QSQ0"] if kt % 2 == 0 else sb["QSQ1"]
            xqk = "qsq0" if kt % 2 == 0 else "qsq1"
            P.op("ACT", lambda e, x=xq, rr=rhs: e.activation(
                x, rr, AF.Square), r=(rkey,), w=(xqk,))
            self.mm(ps["ps5"][:, 0:512], sb["ONESB"], xq,
                    "ps5", (xqk, "cONESB"), start=(kt == 0), stop=(kt == NKT - 1))
        P.op("DVE", lambda e: e.tensor_scalar_mul(
            sb["MU"], ps["ps4"][:, 0:512], 1.0 / D), r=("ps4",), w=("mu",))
        P.op("ACT", lambda e: e.activation(sb["RSTD"], sb["MU"], AF.Square),
             r=("mu",), w=("rstd",))
        P.op("DVE", lambda e: e.scalar_tensor_tensor(
            sb["RSTD"], ps["ps5"][:, 0:512], 1.0 / D, sb["RSTD"],
            ALU.mult, ALU.subtract), r=("ps5", "rstd"), w=("rstd",))
        P.op("DVE", lambda e: e.tensor_scalar_add(sb["RSTD"], sb["RSTD"], EPS),
             r=("rstd",), w=("rstd",))
        P.op("ACT", lambda e: e.activation(sb["RSTD"], sb["RSTD"], AF.Ln),
             r=("rstd",), w=("rstd",))
        P.op("ACT", lambda e: e.activation(sb["RSTD"], sb["RSTD"], AF.Exp,
                                           scale=-0.5),
             r=("rstd",), w=("rstd",))

    def ln_apply_inplace(self, chunk):
        sb, P = self.sb, self.P
        c0 = 512 * chunk
        for kt in range(NKT):
            yap = sb["Y"][:, kt * T + c0: kt * T + c0 + 512]
            P.op("DVE", lambda e, y=yap: e.tensor_sub(y, y, sb["MU"]),
                 r=(f"Y{kt}", "mu"), w=(f"Y{kt}",))
            P.op("DVE", lambda e, y=yap: e.tensor_mul(y, y, sb["RSTD"]),
                 r=(f"Y{kt}", "rstd"), w=(f"Y{kt}",))

    def ln_apply_to_sbb(self, chunk, affine, shifted):
        sb, P = self.sb, self.P
        c0 = 512 * chunk
        for kt in range(NKT):
            sh = shifted and kt >= 4
            edge = sh and chunk == 0
            dst = sb["SBB"][:, kt * T + c0: kt * T + c0 + 512]
            t1 = sb["SC2"]
            if edge:
                src = sb["Y"][:, kt * T: kt * T + 511]
                P.op("DVE", lambda e, s=src: e.tensor_sub(
                    t1[:, 1:512], s, sb["MU"][:, 1:512]),
                    r=(f"Y{kt}", "mu"), w=("sc2",))
                P.op("POOL", lambda e: e.tensor_scalar_mul(
                    t1[:, 0:1], sb["MU"][:, 0:1], -1.0), r=("mu",), w=("sc2",))
            else:
                src = sb["Y"][:, kt * T + c0 - (1 if sh else 0):
                              kt * T + c0 + 512 - (1 if sh else 0)]
                P.op("DVE", lambda e, s=src: e.tensor_sub(t1, s, sb["MU"]),
                     r=(f"Y{kt}", "mu"), w=("sc2",))
            if affine:
                P.op("DVE", lambda e: e.tensor_mul(sb["SC3"], t1, sb["RSTD"]),
                     r=("sc2", "rstd"), w=("sc3",))
                P.op("DVE", lambda e, d=dst, kk=kt: e.tensor_scalar(
                    d, sb["SC3"], sb["GV"][:, kk:kk + 1], sb["BLV"][:, kk:kk + 1],
                    ALU.mult, ALU.add),
                    r=("sc3", "gv", "blv"), w=(f"SBB{kt}",))
            else:
                P.op("DVE", lambda e, d=dst: e.tensor_mul(d, t1, sb["RSTD"]),
                     r=("sc2", "rstd"), w=(f"SBB{kt}",))

    def shift_to_sbb(self):
        sb, P = self.sb, self.P
        for kt in range(NKT):
            dst = sb["SBB"][:, kt * T: (kt + 1) * T]
            if kt < 4:
                P.op("DVE", lambda e, d=dst, k=kt: e.tensor_copy(
                    d, sb["Y"][:, k * T:(k + 1) * T]),
                    r=(f"Y{kt}",), w=(f"SBB{kt}",))
            else:
                P.op("DVE", lambda e, d=dst, k=kt: e.tensor_copy(
                    d[:, 1:T], sb["Y"][:, k * T: (k + 1) * T - 1]),
                    r=(f"Y{kt}",), w=(f"SBB{kt}",))
                P.op("DVE", lambda e, d=dst: e.memset(d[:, 0:1], 0),
                     r=(), w=(f"SBB{kt}",))

    def attn(self, li, top):
        sb, ps, dt, P = self.sb, self.psum, self.dram, self.P
        wsl = dt["wqkv_t"] if top else dt["wqkv_l"][li]
        wosl = dt["wproj_t"] if top else dt["wo_l"][li]
        for g in range(4):
            self.dma("POOL", sb["WA"].rearrange("p (kt n) -> p kt n", kt=NKT),
                     wsl[g].rearrange("kt p n -> p kt n"), key="wa", w=("pool.WA",))
            self.dma("POOL", sb["WO"].rearrange("p (kt n) -> p kt n", kt=NKT * 2),
                     wosl[g].rearrange("nt kp p n -> p (nt kp) n"),
                     key="wo", w=("pool.WO",))
            self.P.phase = 'qkv'
            pending = []
            ui = 0
            for pr in range(2):
                for part, dstbuf in ((0, "QP"), (1, "KP")):
                    for ch in range(4):
                        bank = "ps0" if ui % 2 == 0 else "ps1"
                        pb = ps[bank]
                        for kt in range(NKT):
                            lhs = sb["WA"][:, kt * 768 + 256 * part + 128 * pr:
                                           kt * 768 + 256 * part + 128 * (pr + 1)]
                            rhs = sb["SBB"][:, kt * T + 512 * ch: kt * T + 512 * ch + 512]
                            self.mm(pb[:, 0:512], lhs, rhs, bank,
                                    ("pool.WA", f"SBB{kt}"),
                                    start=(kt == 0), stop=(kt == NKT - 1))
                        if top:
                            dst = sb[dstbuf][:, pr * T + 512 * ch: pr * T + 512 * ch + 512]
                            P.op("ACT", lambda e, d=dst, p=pb: e.activation(
                                d, p[:, 0:512], AF.Copy),
                                r=(bank,), w=(f"pool.{dstbuf}{pr}",))
                        else:
                            qsbk = "sc0" if ui % 2 == 0 else "sc1"
                            qsb = sb["SC0"] if ui % 2 == 0 else sb["SC1"]
                            P.op("ACT", lambda e, q=qsb, p=pb: e.activation(
                                q, p[:, 0:512], AF.Copy), r=(bank,), w=(qsbk,))
                            pending.append((pr, part, ch, dstbuf, ui))
                            if len(pending) >= 2:
                                self.prep_qk(*pending.pop(0))
                        self.vtok_tile(ui)
                        ui += 1
            for args in pending:
                self.prep_qk(*args)
            self.P.phase = 'headsT' if top else 'heads'
            if top:
                self.heads_weave(top)
            else:
                self.heads_qmajor()
            self.P.phase = 'wo'
            for nt in range(NKT):
                for ch in range(4):
                    bank = "ps0" if (nt + ch) % 2 == 0 else "ps1"
                    pb = ps[bank]
                    for kp in range(2):
                        lhs = sb["WO"][:, (nt * 2 + kp) * 128:(nt * 2 + kp + 1) * 128]
                        rhs = sb["OF"][:, kp * T + 512 * ch: kp * T + 512 * ch + 512]
                        self.mm(pb[:, 0:512], lhs, rhs, bank,
                                ("pool.WO", f"pool.OF{kp}"),
                                start=(kp == 0), stop=(kp == 1))
                    yap = sb["Y"][:, nt * T + 512 * ch: nt * T + 512 * ch + 512]
                    P.op("DVE", lambda e, y=yap, p=pb: e.tensor_add(
                        y, y, p[:, 0:512]), r=(f"Y{nt}", bank), w=(f"Y{nt}",))

    def vtok_tile(self, tt):
        sb, ps, P = self.sb, self.psum, self.P
        P.phase = 'vtok'
        pb = ps["ps2"]
        for kt in range(NKT):
            lhs = sb["SBB"][:, kt * T + 128 * tt: kt * T + 128 * (tt + 1)]
            rhs = sb["WA"][:, kt * 768 + 512: kt * 768 + 768]
            self.mm(pb[:, 0:256], lhs, rhs, "ps2",
                    ("pool.WA", f"SBB{kt}"),
                    start=(kt == 0), stop=(kt == NKT - 1))
        dst = sb["VPX"].rearrange("p (t h c) -> p t h c", t=16, h=4)[
            :, tt, :, 0:64]
        P.op("ACT", lambda e, d=dst, p=pb: e.activation(
            d, p[:, 0:256], AF.Copy), r=("ps2",), w=(f"vpx{tt % 2}",))

    def prep_qk(self, pr, part, ch, dstbuf, ui):
        sb, ps, P = self.sb, self.psum, self.P
        P.phase = 'prep'
        pk = "ps3" if ui % 2 == 0 else "ps5"
        pchain = ps[pk]
        qsb = sb["SC0"] if ui % 2 == 0 else sb["SC1"]
        qsbk = "sc0" if ui % 2 == 0 else "sc1"
        qsq = sb["QSQ0"] if ui % 2 == 0 else sb["QSQ1"]
        qsqk = "qsq0" if ui % 2 == 0 else "qsq1"
        P.op("ACT", lambda e, q=qsq, s=qsb: e.activation(q, s, AF.Square),
             r=(qsbk,), w=(qsqk,))
        self.mm(pchain[0:2, 0:512], sb["BD2"], qsq, pk,
                (qsqk, "cBD2"), start=True, stop=True)
        P.op("DVE", lambda e, p=pchain: e.tensor_scalar_max(
            p[0:2, 0:512], p[0:2, 0:512], 1e-24), r=(pk,), w=(pk,))
        P.op("ACT", lambda e, p=pchain: e.activation(
            p[32:34, 0:512], p[0:2, 0:512], AF.Ln), r=(pk,), w=(pk,))
        rb = qsq[0:2, 0:512]
        P.op("ACT", lambda e, p=pchain, rr=rb: e.activation(
            rr, p[32:34, 0:512], AF.Exp, scale=-0.5), r=(pk,), w=(qsqk,))
        selbuf = "SELQ" if part == 0 else "SELK"
        selkey = "selq" if part == 0 else "selk"
        self.mm(pchain[:, 0:512], sb[selbuf], rb, pk,
                (qsqk, selkey), start=True, stop=True)
        slot = sb[dstbuf][:, pr * T + 512 * ch: pr * T + 512 * ch + 512]
        slotk = f"pool.{dstbuf}{pr}"
        P.op("DVE", lambda e, s=slot, q=qsb, p=pchain: e.tensor_mul(
            s, q, p[:, 0:512]), r=(qsbk, pk), w=(slotk,))
        self.mm(pchain[:, 0:512], sb["PERM"], slot, pk,
                (slotk, "cPERM"), start=True, stop=True)
        ctb = "CT" if part == 0 else "CTI"
        stb = "STT" if part == 0 else "STI"
        z0, z0k = (sb["SC2"], "sc2") if ui % 2 == 0 else (sb["SC3"], "sc3")
        tab = sb[ctb][:, 512 * ch: 512 * ch + 512]
        P.op("DVE", lambda e, s=slot, t=tab, z=z0: e.tensor_mul(z, s, t),
             r=(slotk, "c" + ctb), w=(z0k,))
        tab2 = sb[stb][:, 512 * ch: 512 * ch + 512]
        P.op("DVE", lambda e, s=slot, t=tab2, p=pchain: e.tensor_mul(
            s, p[:, 0:512], t), r=(pk, "c" + stb), w=(slotk,))
        P.op("DVE", lambda e, s=slot, z=z0: e.tensor_add(s, s, z),
             r=(slotk, z0k), w=(slotk,))

    def heads_qmajor(self):
        sb, ps, P = self.sb, self.psum, self.P
        blocks = [(hl, qb) for qb in range(16) for hl in range(4)]
        n = len(blocks)
        sbanks = ("ps0", "ps1", "psSTA")

        def s_mm(i):
            hl, qb = blocks[i]
            pr, rh = hl // 2, 64 * (hl % 2)
            w = 128 * min(qb + 1, 3)
            k0 = 128 * max(qb - 2, 0)
            bank = sbanks[i % 3]
            lhs = sb["QP"][rh:rh + 64, pr * T + 128 * qb: pr * T + 128 * (qb + 1)]
            rhs = sb["KP"][rh:rh + 64, pr * T + k0: pr * T + k0 + w]
            self.mm(ps[bank][:, 0:w], lhs, rhs, bank,
                    (f"pool.QP{pr}", f"pool.KP{pr}"), start=True, stop=True)

        def chain(i):
            hl, qb = blocks[i]
            w = 128 * min(qb + 1, 3)
            par = i % 2
            bank = sbanks[i % 3]
            nmc = sb["MBF"][:, 4 + par:5 + par]
            ddc = sb["MBF"][:, 6 + par:7 + par]
            rrc = sb["MBF"][:, 12 + par:13 + par]
            nmk, ddk, rrk = f"nm{par}", f"dd{par}", f"rr{par}"
            eb = sb["EB0"] if par == 0 else sb["EB1"]
            ebk = "eb0" if par == 0 else "eb1"
            P.op("DVE", lambda e, b=ps[bank], ww=w: e.tensor_add(
                b[:, 0:ww], b[:, 0:ww], sb["MASKB"][:, 384 - ww:384]),
                r=(bank, "cMASKB"), w=(bank,))
            P.op("DVE", lambda e, b=ps[bank], ww=w, nm=nmc: e.reduce_max(
                nm, b[:, 0:ww], axis=AX.X, negate=True),
                r=(bank,), w=(nmk,))
            sm = sb["SC0"] if par == 0 else sb["SC1"]
            smk = "sc0" if par == 0 else "sc1"
            P.op("ACT", lambda e, s=sm, b=ps[bank], ww=w, nm=nmc, dd=ddc:
                 e.activation(s[:, 0:ww], b[:, 0:ww], AF.Exp, bias=nm,
                              scale=1.0, accum_out=dd),
                 r=(bank, nmk), w=(smk, ddk))
            P.op("DVE", lambda e, dd=ddc, rr=rrc:
                 e.reciprocal(rr, dd), r=(ddk,), w=(rrk,))
            P.op("ACT", lambda e, d=eb, s=sm, ww=w, rr=rrc:
                 e.activation(d[:, 0:ww], s[:, 0:ww], AF.Copy, scale=rr),
                 r=(smk, rrk), w=(ebk,))

        def tav(i):
            hl, qb = blocks[i]
            pr, rh = hl // 2, 64 * (hl % 2)
            w = 128 * min(qb + 1, 3)
            nt = w // 128
            k0t = max(qb - 2, 0)
            par = i % 2
            eb = sb["EB0"] if par == 0 else sb["EB1"]
            ebk = "eb0" if par == 0 else "eb1"
            tb = "ps2" if par == 0 else "ps3"
            etb = sb["QSQ0"] if par == 0 else sb["QSQ1"]
            etk = "qsq0" if par == 0 else "qsq1"
            for t in range(nt):
                self.mm(ps[tb][:, 128 * t:128 * (t + 1)],
                        eb[:, 128 * t:128 * t + 128], sb["IDENT"], tb,
                        (ebk, "cIDENT"), start=True, stop=True)
            P.op("DVE", lambda e, d=etb, ww=w, b=ps[tb]:
                 e.tensor_copy(d[:, 0:ww], b[:, 0:ww]),
                 r=(tb,), w=(etk,))
            ob = "ps4" if par == 0 else "ps5"
            hloc = 2 * pr + (hl % 2)
            for t in range(nt):
                gk = k0t + t
                vslot = sb["VPX"][:, 260 * gk + 65 * hloc:
                                  260 * gk + 65 * hloc + 64]
                self.mm(ps[ob][0:64, 0:128], vslot,
                        etb[:, 128 * t:128 * t + 128], ob,
                        (etk, f"vpx{gk % 2}"), start=(t == 0),
                        stop=(t == nt - 1))
            odst = sb["OF"][rh:rh + 64, pr * T + 128 * qb: pr * T + 128 * (qb + 1)]
            P.op("ACT", lambda e, d=odst, b=ps[ob]:
                 e.activation(d, b[0:64, 0:128], AF.Copy),
                 r=(ob,), w=(f"pool.OF{pr}",))

        for i in range(n + 2):
            if i < n and i % 2 == 0:
                s_mm(i)
                s_mm(i + 1)
            if 1 <= i <= n:
                chain(i - 1)
            if i >= 2:
                tav(i - 2)

    def heads_weave(self, top):
        for hl in range(5):
            hs = hl if hl < 4 else None
            ha = hl - 1 if hl > 0 else None
            for c in range(4):
                if ha is not None:
                    self.mtz_stage(ha, c)
                glo = 0 if top else max(0, 4 * c - 2)
                ntt = 4 * c + 4 - glo
                prev = None
                napp = ntt if ha is not None else 0
                nst = 4 if hs is not None else 0
                for i in range(max(napp, nst)):
                    if i < napp:
                        self.apply_tile_pe(ha, c, glo, i, top)
                        if prev is not None:
                            self.apply_tile_av(ha, c, glo, prev, ntt, top)
                        prev = i
                    if i < nst:
                        self.stats_j(hs, 4 * c + i, top)
                if prev is not None:
                    self.apply_tile_av(ha, c, glo, prev, ntt, top)
                if ha is not None and c > 0:
                    self.apply_stage2(ha, c - 1, top)
            if ha is not None:
                self.apply_stage2(ha, 3, top)

    def mtz_stage(self, hl, c):
        sb, ps, P = self.sb, self.psum, self.P
        mbb = sb["MBB0"] if hl % 2 == 0 else sb["MBB1"]
        mbk = "mbb0" if hl % 2 == 0 else "mbb1"
        for i in range(4):
            self.mm(ps["psSTB"][0:1, 128 * i:128 * (i + 1)],
                    mbb[:, 4 * c + i:4 * c + i + 1], sb["IDENT"],
                    "psSTB", (mbk, "cIDENT"), start=True, stop=True)
        P.op("ACT", lambda e: e.activation(
            sb["MTZ"], ps["psSTB"][0:1, 0:512], AF.Copy),
            r=("psSTB",), w=("mtz",))

    def apply_tile_pe(self, hl, c, glo, tt, top):
        sb, ps, P = self.sb, self.psum, self.P
        pr = hl // 2
        rh = 64 * (hl % 2)
        qk = f"pool.QP{pr}"
        kk = f"pool.KP{pr}"
        g = glo + tt
        kqb = "ps2" if tt % 2 == 0 else "ps4"
        pkq = ps[kqb]
        lhsk = sb["KP"][rh:rh + 64, pr * T + 128 * g: pr * T + 128 * (g + 1)]
        rhsq = sb["QP"][rh:rh + 64, pr * T + 512 * c: pr * T + 512 * c + 512]
        self.mm(pkq[:, 0:512], lhsk, rhsq, kqb, (qk, kk),
                start=True, stop=False)
        if top:
            dd = g - 4 * c
            if dd > 0:
                self.mm(pkq[:, 0:128 * dd], sb["ONE1"],
                        sb["NEGROW"][0:1, 0:128 * dd], kqb,
                        ("cONE1", "cNEGROW"), start=False, stop=False)
            if 0 <= dd <= 3:
                self.mm(pkq[:, 128 * dd:128 * dd + 128], sb["IDENT"],
                        sb["BAPPW"][:, 128 * 3:128 * 4], kqb,
                        ("cIDENT", "cBAPPW"), start=False, stop=False)
        else:
            s = 3 - g + 4 * c
            self.mm(pkq[:, 0:512], sb["IDENT"],
                    sb["BAPPW"][:, 128 * s:128 * s + 512], kqb,
                    ("cIDENT", "cBAPPW"), start=False, stop=False)
        self.mm(pkq[:, 0:512], sb["ONE1"], sb["MTZ"], kqb,
                ("cONE1", "mtz"), start=False, stop=True)
        ebuf = sb["EB0"] if tt % 2 == 0 else sb["EB1"]
        ebk = "eb0" if tt % 2 == 0 else "eb1"
        P.op("ACT", lambda e, eb=ebuf, p=pkq, sc=(0.125 if top else 1.0):
             e.activation(eb, p[:, 0:512], AF.Exp, scale=sc),
             r=(kqb,), w=(ebk,))

    def apply_tile_av(self, hl, c, glo, tt, ntt, top):
        sb, ps, P = self.sb, self.psum, self.P
        pr = hl // 2
        rh = 64 * (hl % 2)
        g = glo + tt
        avb = "ps3" if c % 2 == 0 else "ps5"
        pav = ps[avb]
        ebuf = sb["EB0"] if tt % 2 == 0 else sb["EB1"]
        ebk = "eb0" if tt % 2 == 0 else "eb1"
        hloc = 2 * pr + (hl % 2)
        vslot = sb["VPX"][:, 260 * g + 65 * hloc: 260 * g + 65 * hloc + 65]
        self.mm(pav[0:65, 0:512], vslot, ebuf, avb,
                (ebk, f"vpx{g % 2}", "vpones"),
                start=(tt == 0), stop=(tt == ntt - 1))

    def stats_j(self, hl, j, top):
        sb, ps, P = self.sb, self.psum, self.P
        mbb = sb["MBB0"] if hl % 2 == 0 else sb["MBB1"]
        mbk = "mbb0" if hl % 2 == 0 else "mbb1"
        pr = hl // 2
        rh = 64 * (hl % 2)
        qk = f"pool.QP{pr}"
        kk = f"pool.KP{pr}"
        stk = "psSTA" if j % 2 == 0 else "ps0"
        pst = ps[stk]
        tk_lo = 0 if top else max(0, 128 * (j - 2))
        tk_hi = 128 * (j + 1)
        wj = tk_hi - tk_lo
        lhs = sb["QP"][rh:rh + 64, pr * T + 128 * j: pr * T + 128 * (j + 1)]
        if top:
            nch = (wj + 511) // 512
            for cc in range(nch):
                a = tk_lo + 512 * cc
                b = min(tk_hi, a + 512)
                rhs = sb["KP"][rh:rh + 64, pr * T + a: pr * T + b]
                diag = (b == tk_hi)
                self.mm(pst[:, 0:(b - a)], lhs, rhs, stk,
                        (qk, kk), start=True, stop=not diag)
                if diag:
                    off = (b - a) - 128
                    self.mm(pst[:, off:off + 128], sb["IDENT"],
                            sb["B128"], stk, ("cIDENT", "cB128"),
                            start=False, stop=True)
                red_dst = sb["MBF"][:, cc:cc + 1]
                P.op("DVE", lambda e, rd=red_dst, n=(b - a):
                     e.reduce_max(rd, pst[:, 0:n],
                                  axis=AX.X), r=(stk,), w=("mbf",))
            P.op("POOL", lambda e, n=nch: e.tensor_copy(
                sb["MBF"][:, 8:8 + n], sb["MBF"][:, 0:n]),
                r=("mbf",), w=("mbf2",))
            P.op("DVE", lambda e, jj=j, n=nch, m=mbb: e.reduce_max(
                m[:, jj:jj + 1], sb["MBF"][:, 8:8 + n], axis=AX.X,
                negate=True), r=("mbf2",), w=(mbk,))
        else:
            rhs = sb["KP"][rh:rh + 64, pr * T + tk_lo: pr * T + tk_hi]
            self.mm(pst[:, 0:wj], lhs, rhs, stk, (qk, kk),
                    start=True, stop=False)
            mask = {128: "B128", 256: "B256", 384: "B384"}[wj]
            self.mm(pst[:, 0:wj], sb["IDENT"], sb[mask],
                    stk, ("cIDENT", "c" + mask), start=False, stop=True)
            P.op("DVE", lambda e, jj=j, n=wj, m=mbb: e.reduce_max(
                m[:, jj:jj + 1], pst[:, 0:n], axis=AX.X,
                negate=True), r=(stk,), w=(mbk,))

    def apply_stage2(self, hl, c, top):
        sb, ps, P = self.sb, self.psum, self.P
        pr = hl // 2
        rh = 64 * (hl % 2)
        avb = "ps3" if c % 2 == 0 else "ps5"
        pav = ps[avb]
        P.op("ACT", lambda e, p=pav: e.activation(
            ps["psSTB"][32:33, 0:512], p[64:65, 0:512], AF.Ln),
            r=(avb,), w=("psSTB",))
        P.op("ACT", lambda e: e.activation(
            sb["RL"], ps["psSTB"][32:33, 0:512], AF.Exp, scale=-1.0),
            r=("psSTB",), w=("rl",))
        self.mm(ps["psSTB"][64:128, 0:512], sb["ONES64"], sb["RL"], "psSTB",
                ("rl", "cONES64"), start=True, stop=True)
        P.op("ACT", lambda e: e.activation(
            sb["LB"], ps["psSTB"][64:128, 0:512], AF.Copy),
            r=("psSTB",), w=("lb",))
        odst = sb["OF"][rh:rh + 64, pr * T + 512 * c: pr * T + 512 * c + 512]
        P.op("DVE", lambda e, o=odst, p=pav: e.tensor_mul(
            o, p[0:64, 0:512], sb["LB"]),
            r=(avb, "lb"), w=(f"pool.OF{pr}",))

    def ffn(self, li):
        sb, ps, dt, P = self.sb, self.psum, self.dram, self.P
        for ch in range(4):
            self.ln_stats(ch, shifted=True)
            self.ln_apply_to_sbb(ch, affine=True, shifted=True)
        for ch in range(4):
            self.P.phase = 'w1'
            for j in range(NFT):
                wbuf = sb["W1A"] if j % 2 == 0 else sb["W1B"]
                wkey = "pool.W1A" if j % 2 == 0 else "pool.W1B"
                self.dma("POOL", wbuf.rearrange("p (kt n) -> p kt n", kt=NKT),
                         dt["w1_l"][li, j].rearrange("kt p n -> p kt n"),
                         key="w1" + wkey[-1], w=(wkey,))
                ab = "ps0" if j % 2 == 0 else "ps2"
                gb = "ps1" if j % 2 == 0 else "ps3"
                for kt in range(NKT):
                    rhs = sb["SBB"][:, kt * T + 512 * ch: kt * T + 512 * ch + 512]
                    self.mm(ps[ab][:, 0:512], wbuf[:, kt * 256: kt * 256 + 128],
                            rhs, ab, (wkey, f"SBB{kt}"),
                            start=(kt == 0), stop=(kt == NKT - 1))
                    self.mm(ps[gb][:, 0:512], wbuf[:, kt * 256 + 128: kt * 256 + 256],
                            rhs, gb, (wkey, f"SBB{kt}"),
                            start=(kt == 0), stop=(kt == NKT - 1))
                gl = sb["SC0"] if j % 2 == 0 else sb["SC1"]
                glk = "sc0" if j % 2 == 0 else "sc1"
                P.op("ACT", lambda e, g=gl, jj=j, pg=ps[gb]: e.activation(
                    g, pg[:, 0:512], AF.Gelu,
                    bias=sb["B1G"][:, jj:jj + 1]), r=(gb, "b1g"), w=(glk,))
                gdst = sb["GB"][:, 512 * j: 512 * (j + 1)]
                P.op("DVE", lambda e, g=gl, d=gdst, jj=j, pa=ps[ab]:
                     e.scalar_tensor_tensor(
                    d, pa[:, 0:512], sb["B1A"][:, jj:jj + 1], g,
                    ALU.add, ALU.mult), r=(ab, glk, "b1a"),
                    w=(f"pool.GB{j % 2}",))
            self.P.phase = 'w2'
            for nt in range(NKT):
                wbuf = sb["W2A"] if nt % 2 == 0 else sb["W2B"]
                wkey = "pool.W2A" if nt % 2 == 0 else "pool.W2B"
                self.dma("POOL", wbuf.rearrange("p (kt n) -> p kt n", kt=NFT),
                         dt["w2_l"][li, nt].rearrange("kt p n -> p kt n"),
                         key="w2" + wkey[-1], w=(wkey,))
                ob = "ps4" if nt % 2 == 0 else "ps5"
                for kt in range(NFT):
                    self.mm(ps[ob][:, 0:512], wbuf[:, kt * 128:(kt + 1) * 128],
                            sb["GB"][:, 512 * kt: 512 * (kt + 1)], ob,
                            (wkey, f"pool.GB{kt % 2}"),
                            start=(kt == 0), stop=(kt == NFT - 1))
                yap = sb["Y"][:, nt * T + 512 * ch: nt * T + 512 * ch + 512]
                P.op("DVE", lambda e, y=yap, n=nt, po=ps[ob]:
                     e.scalar_tensor_tensor(
                    y, po[:, 0:512], sb["B2V"][:, n:n + 1], y,
                    ALU.add, ALU.add), r=(ob, "b2v", f"Y{nt}"), w=(f"Y{nt}",))

    def merge_with_input(self, dram_src, factor, to_out=False):
        sb, dt, P = self.sb, self.dram, self.P
        P.phase = 'merge'
        for kt in range(NKT):
            for ch in range(4):
                par = (kt * 4 + ch) % 2
                xb = sb["SC0"] if par == 0 else sb["SC1"]
                xbk = "sc0" if par == 0 else "sc1"
                self.dma("SP", xb,
                         dram_src[128 * kt:128 * (kt + 1), 512 * ch:512 * ch + 512],
                         key=xbk, r=(), w=(xbk,))
                yap = sb["Y"][:, kt * T + 512 * ch: kt * T + 512 * ch + 512]
                if to_out:
                    ob = sb["SC2"] if par == 0 else sb["SC3"]
                    obk = "sc2" if par == 0 else "sc3"
                    P.op("DVE", lambda e, y=yap, x=xb, o=ob, f=factor:
                         e.scalar_tensor_tensor(o, y, f, x, ALU.mult, ALU.add),
                         r=(f"Y{kt}", xbk), w=(obk,))
                    self.dma("SP", dt["out"][128 * kt:128 * (kt + 1),
                                             512 * ch:512 * ch + 512],
                             ob, key="outst", r=(obk,), w=())
                else:
                    P.op("DVE", lambda e, y=yap, x=xb, f=factor:
                         e.scalar_tensor_tensor(y, y, f, x, ALU.mult, ALU.add),
                         r=(f"Y{kt}", xbk), w=(f"Y{kt}",))

    def store_y_to_out(self):
        sb, dt = self.sb, self.dram
        for kt in range(NKT):
            self.dma("SP", dt["out"][128 * kt:128 * (kt + 1), :],
                     sb["Y"][:, kt * T:(kt + 1) * T], key="outst",
                     r=(f"Y{kt}",), w=())

    def ffn6(self, n_sub):
        for i in range((n_sub + 1) // 2):
            self.load_layer_vecs(i)
            self.P.fence_pool()
            self.shift_to_sbb()
            self.attn(i, top=False)
            if 2 * i + 2 > n_sub:
                return
            self.P.fence_pool()
            self.ffn(i)

    def build(self):
        cfg = self.cfg
        self.load_consts()
        if cfg.get("ln0", True):
            for ch in range(4):
                self.ln_stats(ch, shifted=False)
                self.ln_apply_inplace(ch)
        self.ffn6(cfg.get("n_sub1", 12))
        if cfg.get("merge1", True):
            self.merge_with_input(self.dram["xT"], 0.5)
        if cfg.get("top", True):
            for ch in range(4):
                self.ln_stats(ch, shifted=False)
                self.ln_apply_to_sbb(ch, affine=False, shifted=False)
            self.P.fence_pool()
            self.attn(0, top=True)
        if cfg.get("call2", True):
            for kt in range(NKT):
                self.dma("SP", self.dram["x2scr"][128 * kt:128 * (kt + 1), :],
                         self.sb["Y"][:, kt * T:(kt + 1) * T], key="x2st",
                         r=(f"Y{kt}",), w=())
            for ch in range(4):
                self.ln_stats(ch, shifted=False)
                self.ln_apply_inplace(ch)
            self.ffn6(cfg.get("n_sub2", 12))
            self.merge_with_input(self.dram["x2scr"], 0.5, to_out=True)
        elif cfg.get("dump") == "of":
            for kp in range(2):
                self.dma("POOL", self.dram["out"][128 * kp:128 * (kp + 1), :],
                         self.sb["OF"][:, kp * T:(kp + 1) * T], key="outst",
                         r=(f"pool.OF{kp}",), w=())
        elif cfg.get("dump") == "sbb":
            for kt in range(NKT):
                self.dma("POOL", self.dram["out"][128 * kt:128 * (kt + 1), :],
                         self.sb["SBB"][:, kt * T:(kt + 1) * T], key="outst",
                         r=(f"SBB{kt}",), w=())
        else:
            self.store_y_to_out()


_LAST_PROG = None


def build_program(cfg=None):
    global _LAST_PROG
    cfg = cfg or {}
    nc = bass.Bass("TRN2", target_bir_lowering=False,
                   detect_race_conditions=False)
    mk = MK(nc, cfg)
    _LAST_PROG = mk.P
    mk.build()
    counters = mk.P.resolve()
    sems = {k: nc.alloc_semaphore(name=f"s{i}")
            for i, k in enumerate(counters)}
    with nc.allow_low_precision(reason="bf16 matmul operands / exp tiles"), \
            nc.Block() as block:
        @block.tensor
        def _(tensor):
            mk.P.emit("PE", tensor, sems)

        @block.scalar
        def _(scalar):
            mk.P.emit("ACT", scalar, sems)

        @block.vector
        def _(vector):
            mk.P.emit("DVE", vector, sems)

        @block.gpsimd
        def _(gpsimd):
            mk.P.emit("POOL", gpsimd, sems)

        @block.sync
        def _(sync):
            mk.P.emit("SP", sync, sems)
            if "outst" in counters:
                sync.wait_ge(sems["outst"], counters["outst"])
    return nc


def stage_all(inputs):
    tabs = host_tables()
    consts = host_consts()
    staged = stage_weights(inputs)
    shared = {}
    shared.update({k: tabs[k] for k in ("ct", "st", "cti", "sti")})
    shared.update(consts)
    shared.update(staged)
    return shared


def kernel(**inputs):
    nc = build_program({})
    shared = stage_all(inputs)
    x = np.asarray(inputs["L0_attn_x"])
    in_maps = []
    for b in range(8):
        m = dict(shared)
        m["xT"] = np.ascontiguousarray(x[b].T)
        in_maps.append(m)
    res = run_bass_kernel_spmd(nc, in_maps, core_ids=list(range(8)))
    out = np.stack([np.asarray(res.results[b]["out"]).T for b in range(8)])
    return out.astype(np.float32)
